# version 06
# baseline (speedup 1.0000x reference)
import numpy as np
import concourse.bass as bass
import concourse.mybir as mybir
from concourse.bass_utils import run_bass_kernel_spmd

F32 = mybir.dt.float32
BF16 = mybir.dt.bfloat16
U16 = mybir.dt.uint16
U8 = mybir.dt.uint8
AF = mybir.ActivationFunctionType
ALU = mybir.AluOpType
AX = mybir.AxisListType

C_STEPS = 99
D = 2048
NSLAB = 8
NOWN = 4
NEXP = 32
ROT = 1024


class Sem:
    def __init__(self, nc, name):
        self._cm = nc.semaphore(name)
        self.h = self._cm.__enter__()
        self.n = 0


class KB:
    ENGS = ("pe", "dve", "act", "po", "sp")

    def __init__(self, nc):
        self.nc = nc
        self.q = {e: [] for e in self.ENGS}

    def w(self, eng, sem, val=None):
        v = sem.n if val is None else val
        self.q[eng].append(lambda E: E.wait_ge(sem.h, v))

    def o(self, eng, fn, inc=()):
        inc = tuple(inc)
        for s, k in inc:
            s.n += k

        def run(E):
            ins = fn(E)
            for s, k in inc:
                ins.then_inc(s.h, k)
        self.q[eng].append(run)

    def gate(self, sems):
        vals = [(s, s.n) for s in sems]
        for e in self.ENGS:
            for s, v in vals:
                self.w(e, s, v)


def build_nc(phlim=9):
    nc = bass.Bass("TRN2", target_bir_lowering=False)

    xb = nc.dram_tensor("xb", [D, 4096], F32, kind="ExternalInput").ap()
    win = nc.dram_tensor("win", [48, 128, 16, 128], F32, kind="ExternalInput").ap()
    binp = nc.dram_tensor("binp", [128, 48], F32, kind="ExternalInput").ap()
    wout = nc.dram_tensor("wout", [16, 128, 16, 128], F32, kind="ExternalInput").ap()
    boutp = nc.dram_tensor("boutp", [128, 16], F32, kind="ExternalInput").ap()
    gw = nc.dram_tensor("gw", [128, 16, 32], F32, kind="ExternalInput").ap()
    gup = nc.dram_tensor("gup", [32, 12, 128, 16, 128], F32, kind="ExternalInput").ap()
    gdn = nc.dram_tensor("gdn", [32, 128, 16, 6, 128], F32, kind="ExternalInput").ap()
    simm = nc.dram_tensor("simm", [32, 32], F32, kind="ExternalInput").ap()
    cost = nc.dram_tensor("cost", [512, 4096], F32, kind="ExternalInput").ap()
    sint = nc.dram_tensor("sint", [512, 4096], F32, kind="ExternalInput").ap()
    cst = nc.dram_tensor("cst", [128, 200], F32, kind="ExternalInput").ap()
    crow = nc.dram_tensor("crow", [1, 128], F32, kind="ExternalInput").ap()
    ehot = nc.dram_tensor("ehot", [32, 4096], F32, kind="ExternalInput").ap()
    colsel = nc.dram_tensor("colsel", [32, 256], F32, kind="ExternalInput").ap()
    out = nc.dram_tensor("out", [D, 2048], F32, kind="ExternalOutput").ap()

    qd = nc.dram_tensor("qd", [16, NOWN, 128, 512], BF16).ap()
    kd = nc.dram_tensor("kd", [16, NSLAB, 128, 512], BF16).ap()
    vd = nc.dram_tensor("vd", [16, NSLAB, 128, 512], BF16).ap()
    x1f = nc.dram_tensor("x1f", [16, NOWN, 128, 512], F32).ap()
    x1bd = nc.dram_tensor("x1bd", [16, NOWN, 128, 512], BF16).ap()

    pmm = [nc.alloc_psum_tensor(f"pmm{i}", [128, 512], F32) for i in range(4)]
    pA = nc.alloc_psum_tensor("pA", [128, 1024], F32)
    pB = nc.alloc_psum_tensor("pB", [128, 1024], F32)

    alloc = nc.alloc_sbuf_tensor
    ident = alloc("ident", [128, 128], F32)
    onesrow = alloc("onesrow", [1, 128], F32)
    onescolb = alloc("onescolb", [128, 1], BF16)
    cst_sb = alloc("cst_sb", [128, 200], F32)
    bin_sb = alloc("bin_sb", [128, 48], F32)
    bout_sb = alloc("bout_sb", [128, 16], F32)
    gw_sb = alloc("gw_sb", [128, 16, 32], BF16)
    sim_f = alloc("sim_f", [32, 32], F32)
    sim_fx = alloc("sim_fx", [32, 32], F32)
    sim_b = alloc("sim_b", [32, 32], BF16)
    ones32 = alloc("ones32", [32, 32], F32)
    mask32 = alloc("mask32", [32, 32], U8)
    ehot_sb = alloc("ehot_sb", [32, 4096], BF16)
    colsel_sb = alloc("colsel_sb", [32, 256], BF16)
    A_acc = alloc("A_acc", [128, 16], F32)
    A_tmp = alloc("A_tmp", [128, 16], F32)
    comb_sb = alloc("comb_sb", [32, 2048], BF16)
    rowsc = alloc("rowsc", [1, 2048], F32)
    idxrow = alloc("idxrow", [1, 1024], F32)
    i8u = alloc("i8u", [128, 8], U16)
    thrU = alloc("thrU", [128, 4], U8)

    iotaP = cst_sb[:, 128:129]
    iota32 = cst_sb[:, 130:162]
    iota4 = cst_sb[:, 162:166]

    AREN_W = (nc.sbuf_bytes_remaining - 2048) // 4
    arena = alloc("arena", [128, AREN_W], F32)
    aren_f = arena[:, :]
    aren_b = arena[:, :].bitcast(BF16)

    def vf(off_b, shape):
        n = int(np.prod(shape[1:]))
        v = aren_f[:, off_b // 4: off_b // 4 + n]
        if len(shape) == 3:
            v = v.rearrange("p (a c) -> p a c", a=shape[1])
        return v

    def vb(off_b, shape):
        n = int(np.prod(shape[1:]))
        v = aren_b[:, off_b // 2: off_b // 2 + n]
        if len(shape) == 3:
            v = v.rearrange("p (a c) -> p a c", a=shape[1])
        return v

    AREN_B = AREN_W * 4

    K = KB(nc)
    dsp = Sem(nc, "dsp")
    dst0 = Sem(nc, "dst0")
    dst1 = Sem(nc, "dst1")
    dpre = Sem(nc, "dpre")
    dx = Sem(nc, "dx")
    dwa = Sem(nc, "dwa")
    dwb = Sem(nc, "dwb")
    dgu0 = Sem(nc, "dgu0")
    dgu1 = Sem(nc, "dgu1")
    ddn0 = Sem(nc, "ddn0")
    ddn1 = Sem(nc, "ddn1")
    dxb = Sem(nc, "dxb")
    pe = Sem(nc, "pe")
    dve = Sem(nc, "dve")
    ac = Sem(nc, "ac")
    po = Sem(nc, "po")
    ALLS = [dsp, dst0, dst1, dpre, dx, dwa, dwb, dgu0, dgu1, ddn0, ddn1,
            dxb, pe, dve, ac, po]

    def dma(eng, sem, outp, inp, slow=False):
        K.o(eng, lambda E, o=outp, i=inp, s=slow: E.dma_start(
            out=o, in_=i, allow_slow_non_contiguous=s), [(sem, 16)])

    dma("sp", dsp, ident[:, :], cst[:, 0:128])
    dma("sp", dsp, cst_sb[:, :], cst)
    dma("sp", dsp, onesrow[:, :], crow)
    dma("sp", dsp, bin_sb[:, :], binp)
    dma("sp", dsp, bout_sb[:, :], boutp)
    dma("sp", dsp, sim_f[:, :], simm)
    dma("po", dpre, onescolb[:, :], cst[:, 129:130], slow=True)
    dma("po", dpre, gw_sb[:, :, :], gw)
    dma("po", dpre, ehot_sb[:, :], ehot)
    dma("po", dpre, colsel_sb[:, :], colsel)
    pre_dsp, pre_dg = dsp.n, dpre.n

    K.w("dve", dsp, pre_dsp)
    K.o("dve", lambda E: E.memset(ones32[:, :], 1.0))
    K.o("dve", lambda E: E.tensor_scalar(mask32[:, :], ident[0:32, 0:32], 0.5,
                                         None, ALU.is_gt), [(dve, 1)])
    K.w("dve", dve)
    K.o("dve", lambda E: E.select(sim_fx[:, :], mask32[:, :], ones32[:, :],
                                  sim_f[:, :], add_drain=True), [(dve, 1)])
    K.w("dve", dve)
    K.o("dve", lambda E: E.tensor_copy(sim_b[:, :], sim_fx[:, :]), [(dve, 1)])
    K.o("dve", lambda E: E.memset(A_acc[:, :], 0.0), [(dve, 1)])

    aoff = 0
    xall = vb(aoff, [128, 16, 4096]); aoff += 16 * 4096 * 2
    winb = [vb(aoff + i * 16 * 128 * 2, [128, 16, 128]) for i in range(2)]
    aoff += 2 * 16 * 128 * 2
    evb = [vb(aoff + i * 512 * 2, [128, 512]) for i in range(2)]
    aoff += 2 * 512 * 2
    assert aoff <= AREN_B, (aoff, AREN_B)

    xbr = xb.rearrange("(dc p) t -> p dc t", p=128)
    for s in range(NSLAB):
        dma("po", dx, xall[:, :, 512 * s:512 * (s + 1)],
            xbr[:, :, 512 * s:512 * (s + 1)])

    plan = [(ec, list(range(NSLAB))) for ec in range(16, 48)] + \
           [(ec, list(range(NOWN))) for ec in range(16)]

    g = 0
    evac_dve_val = {}
    evstore_val = {}
    pe_after_iter = []
    dw = [dwa, dwb]
    dstp = [dst0, dst1]
    for i, (ec, slabs) in enumerate(plan):
        buf = i % 2
        if i >= 2:
            K.w("po", pe, pe_after_iter[i - 2])
        K.w("po", dw[buf], dw[buf].n)
        dma("po", dw[buf], winb[buf][:, :, :], win[ec])
        K.w("pe", dx, dx.n)
        K.w("pe", dw[buf], dw[buf].n)
        for s in slabs:
            bank = g % 4
            if g >= 4:
                K.w("pe", dve, evac_dve_val[g - 4])
            for kc in range(16):
                K.o("pe", lambda E, b=bank, bu=buf, k=kc, ss=s: E.matmul(
                    pmm[b][:, :], winb[bu][:, k, :],
                    xall[:, k, 512 * ss:512 * ss + 512],
                    start=(k == 0), stop=(k == 15)),
                    [(pe, 1)] if kc == 15 else ())
            K.w("dve", pe, pe.n)
            sb_ = dstp[g % 2]
            if g >= 2:
                K.w("dve", sb_, evstore_val[g - 2])
            eb = evb[g % 2]
            K.o("dve", lambda E, b=bank, e=ec, ebuf=eb: E.tensor_scalar(
                ebuf[:, :], pmm[b][:, :], bin_sb[:, e:e + 1], None, ALU.add),
                [(dve, 1)])
            evac_dve_val[g] = dve.n
            K.w("act", dve, dve.n)
            if ec < 16:
                tgt = qd[ec, s]
            elif ec < 32:
                tgt = kd[ec - 16, s]
            else:
                tgt = vd[ec - 32, s]
            dma("act", sb_, tgt, eb[:, :])
            evstore_val[g] = sb_.n
            g += 1
        pe_after_iter.append(pe.n)
    a1_dst0, a1_dst1 = dst0.n, dst1.n

    K.gate(ALLS)

    if phlim < 2:
        return _finish(nc, K)
    aoff = 0
    Kb = vb(aoff, [128, 16, 512]); aoff += 16 * 512 * 2
    Vb = vb(aoff, [128, 16, 512]); aoff += 16 * 512 * 2
    cosb = vf(aoff, [128, 4, 512]); aoff += 4 * 512 * 4
    sinb = vf(aoff, [128, 4, 512]); aoff += 4 * 512 * 4
    Kr = vf(aoff, [128, 8, 512]); aoff += 8 * 512 * 4
    rt1 = vf(aoff, [128, 4, 512]); aoff += 4 * 512 * 4
    rt2 = vf(aoff, [128, 4, 512]); aoff += 4 * 512 * 4
    sqb = vb(aoff, [128, 16, 512]); aoff += 16 * 512 * 2
    kvb = vf(aoff, [128, 16, 512]); aoff += 16 * 512 * 4
    kv2 = vf(aoff, [128, 16, 512]); aoff += 16 * 512 * 4
    assert aoff <= AREN_B, (aoff, AREN_B)

    kdr = kd.rearrange("ec s p t -> p ec s t")
    vdr = vd.rearrange("ec s p t -> p ec s t")
    costr = cost.rearrange("(cc p) t -> p cc t", p=128)
    sintr = sint.rearrange("(cc p) t -> p cc t", p=128)

    for s in range(NSLAB):
        K.w("sp", dst0, a1_dst0)
        K.w("sp", dst1, a1_dst1)
        K.w("sp", dsp, dsp.n)
        if s >= 1:
            K.w("sp", dve, dve.n)
            K.w("sp", ac, ac.n)
            K.w("sp", po, po.n)
        dma("sp", dsp, Kb[:, :, :], kdr[:, :, s, :])
        dma("sp", dsp, Vb[:, :, :], vdr[:, :, s, :])
        dma("sp", dsp, cosb[:, :, :], costr[:, :, 512 * s:512 * s + 512])
        dma("sp", dsp, sinb[:, :, :], sintr[:, :, 512 * s:512 * s + 512])
        K.w("dve", dsp, dsp.n)
        K.w("po", dsp, dsp.n)
        for c in range(4):
            K.o("dve", lambda E, c=c: E.tensor_mul(rt1[:, c, :], Kb[:, c, :],
                                                   cosb[:, c, :]))
            K.o("dve", lambda E, c=c: E.tensor_mul(rt2[:, c, :], Kb[:, c + 4, :],
                                                   sinb[:, c, :]),
                [(dve, 1)] if c == 3 else ())
        for c in range(4):
            K.o("po", lambda E, c=c: E.tensor_mul(kvb[:, c, :], Kb[:, c, :],
                                                  sinb[:, c, :]))
            K.o("po", lambda E, c=c: E.tensor_mul(kvb[:, c + 4, :], Kb[:, c + 4, :],
                                                  cosb[:, c, :]),
                [(po, 1)] if c == 3 else ())
        K.w("dve", dve)
        K.w("dve", po)
        for c in range(4):
            K.o("dve", lambda E, c=c: E.tensor_sub(Kr[:, c, :], rt1[:, c, :],
                                                   rt2[:, c, :]))
            K.o("dve", lambda E, c=c: E.tensor_add(Kr[:, c + 4, :], kvb[:, c + 4, :],
                                                   kvb[:, c, :]),
                [(dve, 1)] if c == 3 else ())
        K.w("act", dve, dve.n)
        for c in range(16):
            src = Kr[:, c, :] if c < 8 else Kb[:, c, :]
            K.o("act", lambda E, s_=src, c=c: E.activation(sqb[:, c, :], s_,
                                                           AF.Square),
                [(ac, 1)] if c == 15 else ())
        K.w("pe", ac, ac.n)
        K.w("pe", dve, dve.n)
        for c in range(16):
            K.o("pe", lambda E, c=c: E.matmul(pA[0:1, 0:512], onescolb[:, :],
                                              sqb[:, c, :], start=(c == 0),
                                              stop=(c == 15)),
                [(pe, 1)] if c == 15 else ())
        K.w("act", pe, pe.n)
        K.o("act", lambda E: E.activation(rowsc[0:1, 0:512], pA[0:1, 0:512],
                                          AF.Sqrt), [(ac, 1)])
        K.w("dve", ac, ac.n)
        K.o("dve", lambda E: E.tensor_scalar(rowsc[0:1, 512:1024],
                                             rowsc[0:1, 0:512], 1e-12, None,
                                             ALU.max), [(dve, 1)])
        K.w("dve", dve)
        K.o("dve", lambda E: E.reciprocal(rowsc[0:1, 0:512],
                                          rowsc[0:1, 512:1024]), [(dve, 1)])
        K.w("pe", dve, dve.n)
        K.o("pe", lambda E: E.matmul(pB[:, 0:512], onesrow[:, :],
                                     rowsc[0:1, 0:512], start=True, stop=True),
            [(pe, 1)])
        K.w("dve", pe, pe.n)
        K.w("dve", dve)
        for c in range(16):
            ksrc = Kr[:, c, :] if c < 8 else Kb[:, c, :]
            K.o("dve", lambda E, k=ksrc, c=c: E.tensor_mul(kvb[:, c, :], k,
                                                           Vb[:, c, :]),
                [(dve, 1)] if c == 15 else ())
        K.w("dve", dve)
        for c in range(16):
            K.o("dve", lambda E, c=c: E.tensor_mul(kv2[:, c, :], kvb[:, c, :],
                                                   pB[:, 0:512]),
                [(dve, 1)] if c == 15 else ())
        K.w("dve", dve)
        K.w("dve", po, po.n)
        for c in range(16):
            K.o("dve", lambda E, c=c: E.reduce_sum(A_tmp[:, c:c + 1],
                                                   kv2[:, c, :], axis=AX.X),
                [(dve, 1)] if c == 15 else ())
        K.w("po", dve, dve.n)
        K.o("po", lambda E: E.tensor_add(A_acc[:, :], A_acc[:, :], A_tmp[:, :]),
            [(po, 1)])

    K.gate(ALLS)

    if phlim < 3:
        return _finish(nc, K)
    boff = 0
    wob = [vb(boff + i * 16 * 128 * 2, [128, 16, 128]) for i in range(2)]
    boff += 2 * 16 * 128 * 2
    Qb2 = [vb(boff + i * 16 * 512 * 2, [128, 16, 512]) for i in range(2)]
    boff += 2 * 16 * 512 * 2
    qa2 = [vb(boff + i * 16 * 512 * 2, [128, 16, 512]) for i in range(2)]
    boff += 2 * 16 * 512 * 2
    Qr = vf(boff, [128, 8, 512]); boff += 8 * 512 * 4
    cosb2 = vb(boff, [128, 4, 512]); boff += 4 * 512 * 2
    sinb2 = vb(boff, [128, 4, 512]); boff += 4 * 512 * 2
    rtb1 = vf(boff, [128, 4, 512])
    sqb2 = vb(boff, [128, 16, 512])
    boff += 4 * 512 * 4
    rtb2 = vf(boff, [128, 4, 512]); boff += 4 * 512 * 4
    Xf = vf(boff, [128, 16, 512]); boff += 16 * 512 * 4
    out1 = vf(boff, [128, 16, 512]); boff += 16 * 512 * 4
    assert boff <= AREN_B, (boff, AREN_B)

    qdr = qd.rearrange("ec s p t -> p ec s t")
    x1fr = x1f.rearrange("ec s p t -> p ec s t")
    x1br = x1bd.rearrange("ec s p t -> p ec s t")

    def emit_ln(src3, bconv, sqv, eps=1e-5):
        K.w("act", dve, dve.n)
        for c in range(16):
            K.o("act", lambda E, c=c: E.activation(sqv[:, c, :], src3[:, c, :],
                                                   AF.Square),
                [(ac, 1)] if c == 15 else ())
        K.w("dve", dve)
        for c in range(16):
            K.o("dve", lambda E, c=c: E.tensor_copy(bconv[:, c, :],
                                                    src3[:, c, :]),
                [(dve, 1)] if c == 15 else ())
        K.w("pe", ac, ac.n)
        K.w("pe", dve, dve.n)
        for c in range(16):
            K.o("pe", lambda E, c=c: E.matmul(pA[0:1, 0:512], onescolb[:, :],
                                              bconv[:, c, :], start=(c == 0),
                                              stop=(c == 15)),
                [(pe, 1)] if c == 15 else ())
        for c in range(16):
            K.o("pe", lambda E, c=c: E.matmul(pA[0:1, 512:1024], onescolb[:, :],
                                              sqv[:, c, :], start=(c == 0),
                                              stop=(c == 15)),
                [(pe, 1)] if c == 15 else ())
        emit_ln_rows(src3, eps)

    def emit_ln_rows(src3, eps=1e-5):
        K.w("dve", pe, pe.n)
        K.o("dve", lambda E: E.tensor_scalar(rowsc[0:1, 0:512], pA[0:1, 0:512],
                                             1.0 / D, None, ALU.mult), [(dve, 1)])
        K.w("act", dve, dve.n)
        K.o("act", lambda E: E.activation(rowsc[0:1, 512:1024], rowsc[0:1, 0:512],
                                          AF.Square), [(ac, 1)])
        K.w("dve", ac, ac.n)
        K.o("dve", lambda E: E.scalar_tensor_tensor(
            rowsc[0:1, 1024:1536], pA[0:1, 512:1024], 1.0 / D,
            rowsc[0:1, 512:1024], ALU.mult, ALU.subtract), [(dve, 1)])
        K.w("dve", dve)
        K.o("dve", lambda E: E.tensor_scalar(rowsc[0:1, 512:1024],
                                             rowsc[0:1, 1024:1536], eps, None,
                                             ALU.add), [(dve, 1)])
        K.w("act", dve, dve.n)
        K.o("act", lambda E: E.activation(rowsc[0:1, 1024:1536],
                                          rowsc[0:1, 512:1024], AF.Sqrt),
            [(ac, 1)])
        K.w("dve", ac, ac.n)
        K.o("dve", lambda E: E.reciprocal(rowsc[0:1, 1536:2048],
                                          rowsc[0:1, 1024:1536]), [(dve, 1)])
        K.w("pe", dve, dve.n)
        K.o("pe", lambda E: E.matmul(pB[:, 0:512], onesrow[:, :],
                                     rowsc[0:1, 0:512], start=True, stop=True))
        K.o("pe", lambda E: E.matmul(pB[:, 512:1024], onesrow[:, :],
                                     rowsc[0:1, 1536:2048], start=True,
                                     stop=True), [(pe, 1)])

    def emit_ln_transform(src3):
        K.w("dve", pe, pe.n)
        for c in range(16):
            K.o("dve", lambda E, c=c: E.tensor_sub(src3[:, c, :], src3[:, c, :],
                                                   pB[:, 0:512]),
                [(dve, 1)] if c == 15 else ())
        K.w("dve", dve)
        for c in range(16):
            K.o("dve", lambda E, c=c: E.tensor_mul(src3[:, c, :], src3[:, c, :],
                                                   pB[:, 512:1024]),
                [(dve, 1)] if c == 15 else ())

    qa_done = {}
    rope_done = {}
    xf_load_val = {}
    xf_read_done = {}

    def emit_xf_load(tt):
        K.w("sp", dsp, dsp.n)
        if (tt - 1) in xf_read_done:
            K.w("sp", dve, xf_read_done[tt - 1])
        dma("sp", dsp, Xf[:, :, :], xbr[:, :, 512 * tt:512 * tt + 512])
        xf_load_val[tt] = dsp.n

    def emit_b_loads(tt):
        b = tt % 2
        K.w("sp", dst0, a1_dst0)
        K.w("sp", dst1, a1_dst1)
        K.w("sp", dsp, dsp.n)
        if tt >= 2 and (tt - 2) in qa_done:
            K.w("sp", dve, qa_done[tt - 2])
        dma("sp", dsp, Qb2[b][:, :, :], qdr[:, :, tt, :])
        K.w("po", dx, dx.n)
        if tt >= 1 and (tt - 1) in rope_done:
            K.w("po", dve, rope_done[tt - 1])
        dma("po", dx, cosb2[:, :, :], costr[:, :, 512 * tt:512 * tt + 512])
        dma("po", dx, sinb2[:, :, :], sintr[:, :, 512 * tt:512 * tt + 512])
        return (dsp.n, dx.n)

    def emit_rope_qa(tt, load_val):
        b = tt % 2
        K.w("dve", dsp, load_val[0])
        K.w("dve", dx, load_val[1])
        K.w("dve", ac, ac.n)
        K.w("dve", pe, pe.n)
        K.w("dve", dve)
        for c in range(4):
            K.o("dve", lambda E, c=c: E.tensor_mul(rtb1[:, c, :],
                                                   Qb2[b][:, c, :],
                                                   cosb2[:, c, :]))
            K.o("dve", lambda E, c=c: E.tensor_mul(rtb2[:, c, :],
                                                   Qb2[b][:, c + 4, :],
                                                   sinb2[:, c, :]),
                [(dve, 1)] if c == 3 else ())
        K.w("dve", dve)
        for c in range(4):
            K.o("dve", lambda E, c=c: E.tensor_sub(Qr[:, c, :], rtb1[:, c, :],
                                                   rtb2[:, c, :]),
                [(dve, 1)] if c == 3 else ())
        K.w("dve", dve)
        for c in range(4):
            K.o("dve", lambda E, c=c: E.tensor_mul(rtb1[:, c, :],
                                                   Qb2[b][:, c, :],
                                                   sinb2[:, c, :]))
            K.o("dve", lambda E, c=c: E.tensor_mul(rtb2[:, c, :],
                                                   Qb2[b][:, c + 4, :],
                                                   cosb2[:, c, :]),
                [(dve, 1)] if c == 3 else ())
        K.w("dve", dve)
        for c in range(4):
            K.o("dve", lambda E, c=c: E.tensor_add(Qr[:, c + 4, :],
                                                   rtb2[:, c, :],
                                                   rtb1[:, c, :]),
                [(dve, 1)] if c == 3 else ())
        rope_done[tt] = dve.n
        K.w("dve", dve)
        for c in range(16):
            s_ = Qr[:, c, :] if c < 8 else Qb2[b][:, c, :]
            K.o("dve", lambda E, s2=s_, c=c: E.tensor_scalar(
                qa2[b][:, c, :], s2, A_acc[:, c:c + 1], None, ALU.mult),
                [(dve, 1)] if c == 15 else ())
        qa_done[tt] = dve.n

    wout_g = 0
    woutev = {}
    wo_plan_pe = []
    s1_v = {}
    sq_v = {}
    ld0 = emit_b_loads(0)
    emit_xf_load(0)
    emit_rope_qa(0, ld0)
    for tt in range(NOWN):
        b = tt % 2
        if tt + 1 < NOWN:
            next_load = emit_b_loads(tt + 1)
        if tt >= 1:
            K.w("dve", dst0, dst0.n)
            K.w("dve", dxb, dxb.n)
        for ec in range(16):
            wbuf = ec % 2
            widx = tt * 16 + ec
            if widx >= 2:
                K.w("po", pe, wo_plan_pe[widx - 2])
            K.w("po", dw[wbuf], dw[wbuf].n)
            dma("po", dw[wbuf], wob[wbuf][:, :, :], wout[ec])
            bank = wout_g % 4
            K.w("pe", dw[wbuf], dw[wbuf].n)
            K.w("pe", dve, qa_done[tt])
            if wout_g >= 4:
                K.w("pe", dve, woutev[wout_g - 4])
            for kc in range(16):
                K.o("pe", lambda E, bk=bank, wb=wbuf, k=kc, bb=b: E.matmul(
                    pmm[bk][:, :], wob[wb][:, k, :], qa2[bb][:, k, :],
                    start=(k == 0), stop=(k == 15)),
                    [(pe, 1)] if kc == 15 else ())
            wo_plan_pe.append(pe.n)
            K.w("dve", pe, pe.n)
            K.w("dve", dsp, xf_load_val[tt])
            K.o("dve", lambda E, bk=bank, e=ec: E.scalar_tensor_tensor(
                out1[:, e, :], pmm[bk][:, :], bout_sb[:, e:e + 1],
                Xf[:, e, :], ALU.add, ALU.add), [(dve, 1)])
            K.w("dve", dve)
            K.o("dve", lambda E, e=ec: E.tensor_add(out1[:, e, :],
                                                    out1[:, e, :],
                                                    Xf[:, e, :]),
                [(dve, 1)])
            woutev[wout_g] = dve.n
            wout_g += 1
            K.w("dve", dve)
            K.o("dve", lambda E, e=ec: E.tensor_copy(sqb2[:, e, :],
                                                     out1[:, e, :]), [(dve, 1)])
            conv_v = dve.n
            K.w("pe", dve, conv_v)
            K.o("pe", lambda E, e=ec: E.matmul(pA[0:1, 0:512], onescolb[:, :],
                                               sqb2[:, e, :], start=(e == 0),
                                               stop=(e == 15),
                                               skip_group_check=True),
                [(pe, 1)])
            s1_v[ec] = pe.n
            K.w("act", dve, conv_v)
            K.w("act", pe, s1_v[ec])
            K.o("act", lambda E, e=ec: E.activation(sqb2[:, e, :],
                                                    out1[:, e, :], AF.Square),
                [(ac, 1)])
            sq_v[ec] = ac.n
            if ec >= 1:
                K.w("pe", ac, sq_v[ec - 1])
                K.o("pe", lambda E, e=ec - 1: E.matmul(
                    pA[0:1, 512:1024], onescolb[:, :], sqb2[:, e, :],
                    start=(e == 0), stop=False, skip_group_check=True),
                    [(pe, 1)])
        K.w("pe", ac, sq_v[15])
        K.o("pe", lambda E: E.matmul(pA[0:1, 512:1024], onescolb[:, :],
                                     sqb2[:, 15, :], start=False, stop=True,
                                     skip_group_check=True), [(pe, 1)])
        xf_read_done[tt] = dve.n
        if tt + 1 < NOWN:
            emit_xf_load(tt + 1)
        emit_ln_rows(out1)
        if tt + 1 < NOWN:
            emit_rope_qa(tt + 1, next_load)
        emit_ln_transform(out1)
        K.w("act", dve, dve.n)
        K.w("act", dst0, dst0.n)
        dma("act", dst0, x1fr[:, :, tt, :], out1[:, :, :])
        K.w("po", dve, dve.n)
        K.w("po", dxb, dxb.n)
        dma("po", dxb, x1br[:, :, tt, :], out1[:, :, :])

    b_dst0 = dst0.n
    b_dxb = dxb.n
    K.gate(ALLS)

    _co = [0]

    def cvf(shape):
        v = vf(_co[0], shape)
        n = int(np.prod(shape[1:])) * 4
        _co[0] += (n + 63) & ~63
        return v

    def cvb(shape):
        v = vb(_co[0], shape)
        n = int(np.prod(shape[1:])) * 2
        _co[0] += (n + 63) & ~63
        return v

    x1t = cvb([128, 16, 512])
    lgf = cvf([128, 512])
    ltok = cvf([128, 32])
    v8 = cvf([128, 8])
    idxf = cvf([128, 8])
    nv0 = cvf([128, 1])
    ev8 = cvf([128, 8])
    s8 = cvf([128, 1])
    r8 = cvf([128, 1])
    rw8 = cvf([128, 8])
    oh = cvb([128, 8 * 128])
    Mt = cvb([128, 16 * 128])
    scf = cvf([128, 128])
    sct = cvf([128, 16])
    bs4 = cvf([128, 4])
    cand = cvf([128, 16])
    am4 = cvf([128, 4])
    ohm = cvf([128, 16])
    pm16 = cvf([128, 16])
    bp4 = cvf([128, 4])
    rer8 = cvf([128, 8])
    ohc = cvf([128, 32 * 8])
    ohc2 = cvf([128, 32 * 8])
    combt = cvf([128, 32])
    assert _co[0] <= AREN_B, (_co[0], AREN_B)

    ohv = oh.rearrange("p (k t) -> p k t", k=8)[0:32, :, :]
    mtv = Mt.rearrange("p (r t) -> p r t", r=16)[0:32, :, :]
    sct3 = sct.rearrange("p (j i) -> p j i", i=4)
    cand3 = cand.rearrange("p (j i) -> p j i", i=4)
    ohm3 = ohm.rearrange("p (j i) -> p j i", i=4)
    pm3 = pm16.rearrange("p (j i) -> p j i", i=4)
    ohc3 = ohc.rearrange("p (e k) -> p e k", k=8)
    ohc23 = ohc2.rearrange("p (e k) -> p e k", k=8)

    for tt in range(NOWN):
        K.w("sp", dxb, b_dxb)
        K.w("sp", dsp, dsp.n)
        if tt >= 1:
            K.w("sp", pe, pe.n)
        dma("sp", dsp, x1t[:, :, :], x1br[:, :, tt, :])
        K.w("pe", dsp, dsp.n)
        K.w("pe", dpre, pre_dg)
        if tt >= 1:
            K.w("pe", dve, dve.n)
        for kc in range(16):
            K.o("pe", lambda E, k=kc: E.matmul(pmm[0][0:32, :], gw_sb[:, k, :],
                                               x1t[:, k, :], start=(k == 0),
                                               stop=(k == 15)),
                [(pe, 1)] if kc == 15 else ())
        K.w("dve", pe, pe.n)
        K.o("dve", lambda E: E.tensor_copy(lgf[0:32, :], pmm[0][0:32, :]),
            [(dve, 1)])
        for j in range(4):
            if C_STEPS < 2:
                continue
            K.w("pe", dve, dve.n)
            K.o("pe", lambda E, j=j: E.transpose(
                pmm[1][:, 0:32], lgf[0:32, 128 * j:128 * j + 128],
                ident[0:32, 0:32]), [(pe, 1)])
            K.w("dve", pe, pe.n)
            K.o("dve", lambda E: E.tensor_copy(ltok[:, :], pmm[1][:, 0:32]),
                [(dve, 1)])
            if C_STEPS < 3:
                continue
            K.w("dve", dve)
            K.o("dve", lambda E: E.max(out=v8[:, :], in_=ltok[:, :]), [(dve, 1)])
            K.w("dve", dve)
            K.o("dve", lambda E: E.max_index(i8u[:, :], v8[:, :], ltok[:, :]),
                [(dve, 1)])
            K.w("dve", dve)
            K.o("dve", lambda E: E.tensor_copy(idxf[:, :], i8u[:, :]))
            K.o("dve", lambda E: E.tensor_scalar(nv0[:, :], v8[:, 0:1], -1.0,
                                                 None, ALU.mult), [(dve, 1)])
            if C_STEPS < 4:
                continue
            K.w("act", dve, dve.n)
            import os as _os
            if _os.environ.get("CVAR") == "sq":
                K.o("act", lambda E: E.activation(ev8[:, :], v8[:, :],
                                                  AF.Square), [(ac, 1)])
            elif _os.environ.get("CVAR") == "nobias":
                K.o("act", lambda E: E.activation(ev8[:, :], v8[:, :],
                                                  AF.Exp), [(ac, 1)])
            else:
                K.o("act", lambda E: E.activation(ev8[:, :], v8[:, :], AF.Exp,
                                                  bias=nv0[:, :]), [(ac, 1)])
            K.w("dve", ac, ac.n)
            K.o("dve", lambda E: E.reduce_sum(s8[:, :], ev8[:, :], axis=AX.X),
                [(dve, 1)])
            K.w("dve", dve)
            K.o("dve", lambda E: E.reciprocal(r8[:, :], s8[:, :]), [(dve, 1)])
            K.w("dve", dve)
            K.o("dve", lambda E: E.tensor_scalar(rw8[:, :], ev8[:, :], r8[:, :],
                                                 None, ALU.mult), [(dve, 1)])
            if C_STEPS < 5:
                continue
            K.w("pe", dve, dve.n)
            for k in range(8):
                bank = k % 4
                if k >= 4:
                    K.w("pe", dve, dve.n)
                K.o("pe", lambda E, k=k, b=bank: E.transpose(
                    pmm[b][0:1, 0:128], idxf[:, k:k + 1],
                    ident[:, :]), [(pe, 1)])
                K.w("dve", pe, pe.n)
                K.o("dve", lambda E, k=k, b=bank: E.tensor_copy(
                    idxrow[0:1, 128 * k:128 * k + 128],
                    pmm[b][0:1, 0:128]), [(dve, 1)])
            if C_STEPS < 6:
                continue
            K.w("pe", dve, dve.n)
            K.o("pe", lambda E: E.matmul(pA[0:32, 0:512], onesrow[0:1, 0:32],
                                         idxrow[0:1, 0:512], start=True,
                                         stop=True))
            K.o("pe", lambda E: E.matmul(pA[0:32, 512:1024], onesrow[0:1, 0:32],
                                         idxrow[0:1, 512:1024], start=True,
                                         stop=True), [(pe, 1)])
            K.w("dve", pe, pe.n)
            K.o("dve", lambda E: E.tensor_scalar(
                ohv, pA[0:32, 0:1024].rearrange("p (k t) -> p k t", k=8),
                iotaP[0:32, :], None, ALU.is_equal), [(dve, 1)])
            if C_STEPS < 7:
                continue
            K.w("pe", dve, dve.n)
            for j2 in range(4):
                K.o("pe", lambda E, j2=j2: E.matmul(
                    pB[0:32, 128 * j2:128 * j2 + 128], sim_b[:, :],
                    ohv[:, 4 + j2, :], start=True, stop=True),
                    [(pe, 1)] if j2 == 3 else ())
            K.w("dve", pe, pe.n)
            for j2 in range(4):
                for i2 in range(4):
                    K.o("dve", lambda E, j2=j2, i2=i2: E.tensor_mul(
                        mtv[:, 4 * j2 + i2, :], ohv[:, i2, :],
                        pB[0:32, 128 * j2:128 * j2 + 128]),
                        [(dve, 1)] if (j2 == 3 and i2 == 3) else ())
            K.w("pe", dve, dve.n)
            for r in range(16):
                K.o("pe", lambda E, r=r: E.matmul(
                    pmm[3][0:16, 0:128], colsel_sb[:, 16 * r:16 * r + 16],
                    mtv[:, r, :], start=(r == 0), stop=(r == 15)),
                    [(pe, 1)] if r == 15 else ())
            K.w("dve", pe, pe.n)
            K.o("dve", lambda E: E.tensor_copy(scf[0:16, :], pmm[3][0:16, 0:128]),
                [(dve, 1)])
            K.w("pe", dve, dve.n)
            K.o("pe", lambda E: E.transpose(pmm[2][:, 0:16], scf[0:16, :],
                                            ident[0:16, 0:16]), [(pe, 1)])
            K.w("dve", pe, pe.n)
            K.o("dve", lambda E: E.tensor_copy(sct[:, :], pmm[2][:, 0:16]),
                [(dve, 1)])
            if C_STEPS < 8:
                continue
            K.w("dve", dve)
            K.o("dve", lambda E: E.reduce_max(bs4[:, :], sct3, axis=AX.X),
                [(dve, 1)])
            K.w("dve", dve)
            bs_b = bs4.unsqueeze(2).to_broadcast([128, 4, 4])
            K.o("dve", lambda E: E.tensor_tensor(cand3, sct3, bs_b, ALU.is_lt),
                [(dve, 1)])
            K.w("dve", dve)
            iota4_b = iota4.unsqueeze(1).to_broadcast([128, 4, 4])
            K.o("dve", lambda E: E.scalar_tensor_tensor(
                cand3, cand3, 8.0, iota4_b, ALU.mult, ALU.add), [(dve, 1)])
            K.w("dve", dve)
            K.o("dve", lambda E: E.tensor_reduce(am4[:, :], cand3, AX.X,
                                                 ALU.min), [(dve, 1)])
            K.w("dve", dve)
            am_b = am4.unsqueeze(2).to_broadcast([128, 4, 4])
            K.o("dve", lambda E: E.tensor_tensor(ohm3, am_b, iota4_b,
                                                 ALU.is_equal), [(dve, 1)])
            K.w("dve", dve)
            prim_b = idxf[:, 0:4].unsqueeze(1).to_broadcast([128, 4, 4])
            K.o("dve", lambda E: E.tensor_mul(pm3, ohm3, prim_b), [(dve, 1)])
            K.w("dve", dve)
            K.o("dve", lambda E: E.reduce_sum(bp4[:, :], pm3, axis=AX.X),
                [(dve, 1)])
            K.w("dve", dve)
            K.o("dve", lambda E: E.tensor_scalar(thrU[:, :], bs4[:, :], 0.5,
                                                 None, ALU.is_lt), [(dve, 1)])
            K.w("dve", dve)
            K.o("dve", lambda E: E.tensor_copy(rer8[:, 0:4], idxf[:, 0:4]),
                [(dve, 1)])
            K.w("dve", dve)
            K.o("dve", lambda E: E.select(rer8[:, 4:8], thrU[:, :],
                                          idxf[:, 4:8], bp4[:, :],
                                          add_drain=True), [(dve, 1)])
            if C_STEPS < 9:
                continue
            K.w("dve", dve)
            iotaE_b = iota32.unsqueeze(2).to_broadcast([128, 32, 8])
            rer_b = rer8.unsqueeze(1).to_broadcast([128, 32, 8])
            K.o("dve", lambda E: E.tensor_tensor(ohc3, iotaE_b, rer_b,
                                                 ALU.is_equal), [(dve, 1)])
            K.w("dve", dve)
            rw_b = rw8.unsqueeze(1).to_broadcast([128, 32, 8])
            K.o("dve", lambda E: E.tensor_mul(ohc23, ohc3, rw_b), [(dve, 1)])
            K.w("dve", dve)
            K.o("dve", lambda E: E.reduce_sum(combt[:, :], ohc23, axis=AX.X),
                [(dve, 1)])
            K.w("pe", dve, dve.n)
            K.o("pe", lambda E: E.transpose(pmm[1][0:32, 0:128], combt[:, :],
                                            ident[:, :]), [(pe, 1)])
            K.w("dve", pe, pe.n)
            col = 512 * tt + 128 * j
            K.o("dve", lambda E, c=col: E.tensor_copy(
                comb_sb[:, c:c + 128], pmm[1][0:32, 0:128]), [(dve, 1)])

    K.gate(ALLS)

    if phlim < 5:
        return _finish(nc, K)
    doff = 0
    ysb = vf(doff, [128, 16, 1024]); doff += 16 * 1024 * 4
    x1h = vb(doff, [128, 16, 1024]); doff += 16 * 1024 * 2
    ebase = doff
    hsb = vb(doff, [128, 6, 1024]); doff += 6 * 1024 * 2
    gub = [vb(doff + i * 2 * 16 * 128 * 2, [128, 2, 16 * 128]) for i in range(2)]
    doff += 2 * 2 * 16 * 128 * 2
    dnb = [vb(doff + i * 8 * 6 * 128 * 2, [128, 8 * 6 * 128]) for i in range(2)]
    doff += 2 * 8 * 6 * 128 * 2
    dexp_end = doff
    cwsb = vf(doff, [128, 1024]); doff += 1024 * 4
    hg = vf(doff, [128, 2, 512]); doff += 2 * 512 * 4
    hx = vf(doff, [128, 2, 512]); doff += 2 * 512 * 4
    assert doff <= AREN_B, (doff, AREN_B)
    ybd = vb(ebase, [128, 16, 512])
    sqd = vb(ebase + 16 * 512 * 2, [128, 16, 512])
    assert ebase + 2 * 16 * 512 * 2 <= dexp_end

    outr = out.rearrange("(dc p) t -> p dc t", p=128)

    gu_dg = {}
    gu_pe = {}
    dn_dg = {}
    dn_pe = {}
    dn_g = 0
    ydve = {}
    dgu = [dgu0, dgu1]
    ddn = [ddn0, ddn1]
    for hh in range(2):
        tt0, tt1 = 2 * hh, 2 * hh + 1
        K.w("sp", dst0, b_dst0)
        K.w("sp", dxb, b_dxb)
        K.w("sp", dsp, dsp.n)
        dma("sp", dsp, ysb[:, :, 0:512], x1fr[:, :, tt0, :])
        dma("sp", dsp, ysb[:, :, 512:1024], x1fr[:, :, tt1, :])
        dma("sp", dsp, x1h[:, :, 0:512], x1br[:, :, tt0, :])
        dma("sp", dsp, x1h[:, :, 512:1024], x1br[:, :, tt1, :])
        sp_ready = dsp.n

        prev_y_add = dve.n

        def emit_gu_dma(si, guard_val):
            e_, c_ = divmod(si, 6)
            if guard_val is not None:
                K.w("po", pe, guard_val)
            sgu = dgu[si % 2]
            K.w("po", sgu, sgu.n)
            for gi in range(2):
                fc = c_ + 6 * gi
                K.o("po", lambda E, e=e_, f=fc, b=si % 2, gi=gi: E.dma_start(
                    out=gub[b][:, gi, :].rearrange("p (dc j) -> p dc j", dc=16),
                    in_=gup[e, f]), [(sgu, 16)])
            gu_dg[(hh, si)] = (sgu, sgu.n)

        def emit_dn_dma(e_, dh, guard_val):
            if guard_val is not None:
                K.w("po", pe, guard_val)
            sdn = ddn[dh]
            K.w("po", sdn, sdn.n)
            K.o("po", lambda E, e=e_, d=dh: E.dma_start(
                out=dnb[d][:, :].rearrange("p (dc fc j) -> p dc fc j",
                                           dc=8, fc=6),
                in_=gdn[e][:, 8 * d:8 * d + 8]), [(sdn, 16)])
            dn_dg[(hh, e_, dh)] = (sdn, sdn.n)

        emit_gu_dma(0, None)
        emit_gu_dma(1, None)
        emit_dn_dma(0, 0, None)
        emit_dn_dma(0, 1, None)
        for e in range(NEXP):
            K.w("pe", dsp, sp_ready)
            if e >= 1:
                K.w("pe", po, po.n)
            cw0 = 1024 * hh
            K.o("pe", lambda E, c0=cw0, e=e: E.matmul(
                pA[:, 0:512], ehot_sb[:, 128 * e:128 * e + 128],
                comb_sb[:, c0:c0 + 512], start=True, stop=True))
            K.o("pe", lambda E, c0=cw0, e=e: E.matmul(
                pA[:, 512:1024], ehot_sb[:, 128 * e:128 * e + 128],
                comb_sb[:, c0 + 512:c0 + 1024], start=True, stop=True),
                [(pe, 1)])
            K.w("dve", pe, pe.n)
            K.o("dve", lambda E: E.tensor_copy(cwsb[:, 0:512], pA[:, 0:512]))
            K.o("dve", lambda E: E.tensor_copy(cwsb[:, 512:1024],
                                               pA[:, 512:1024]), [(dve, 1)])
            cw_done = dve.n
            for c in range(6):
                si = e * 6 + c
                K.w("pe", gu_dg[(hh, si)][0], gu_dg[(hh, si)][1])
                if (hh, e, c) != (0, 0, 0):
                    K.w("pe", ac, ac.n)
                for tt2 in range(2):
                    for kc in range(16):
                        K.o("pe", lambda E, b=tt2, k=kc, t2=tt2,
                            bu=c % 2: E.matmul(
                                pmm[b][:, :],
                                gub[bu][:, 0, :].rearrange(
                                    "p (dc j) -> p dc j", dc=16)[:, k, :],
                                x1h[:, k, 512 * t2:512 * t2 + 512],
                                start=(k == 0), stop=(k == 15)),
                            [(pe, 1)] if kc == 15 else ())
                g_pe = pe.n
                if (hh, e, c) != (0, 0, 0):
                    K.w("pe", dve, dve.n)
                for tt2 in range(2):
                    for kc in range(16):
                        K.o("pe", lambda E, b=2 + tt2, k=kc, t2=tt2,
                            bu=c % 2: E.matmul(
                                pmm[b][:, :],
                                gub[bu][:, 1, :].rearrange(
                                    "p (dc j) -> p dc j", dc=16)[:, k, :],
                                x1h[:, k, 512 * t2:512 * t2 + 512],
                                start=(k == 0), stop=(k == 15)),
                            [(pe, 1)] if kc == 15 else ())
                gu_pe[(hh, si)] = pe.n
                if si + 2 < NEXP * 6:
                    emit_gu_dma(si + 2, gu_pe[(hh, si)])
                K.w("act", pe, g_pe)
                K.w("act", dve, dve.n)
                for tt2 in range(2):
                    K.o("act", lambda E, t2=tt2: E.activation(
                        hg[:, t2, :], pmm[t2][:, :], AF.Silu),
                        [(ac, 1)] if tt2 == 1 else ())
                K.w("dve", ac, ac.n)
                K.w("dve", pe, pe.n)
                K.w("dve", po, po.n)
                for tt2 in range(2):
                    K.o("dve", lambda E, t2=tt2: E.tensor_mul(
                        hx[:, t2, :], hg[:, t2, :], pmm[2 + t2][:, :]),
                        [(dve, 1)] if tt2 == 1 else ())
                K.w("po", dve, dve.n)
                K.w("po", dve, cw_done)
                for tt2 in range(2):
                    K.o("po", lambda E, t2=tt2, c=c: E.tensor_mul(
                        hsb[:, c, 512 * t2:512 * t2 + 512], hx[:, t2, :],
                        cwsb[:, 512 * t2:512 * t2 + 512]),
                        [(po, 1)] if tt2 == 1 else ())
            h_done = po.n
            K.w("pe", po, h_done)
            K.w("dve", dve, prev_y_add)
            for dh in range(2):
                K.w("pe", dn_dg[(hh, e, dh)][0], dn_dg[(hh, e, dh)][1])
                for dcl in range(8):
                    dc = 8 * dh + dcl
                    for tt2 in range(2):
                        q = dn_g % 2
                        dtgt = pB[:, 512 * q:512 * q + 512]
                        if dn_g >= 2:
                            K.w("pe", dve, ydve[dn_g - 2])
                        for fc in range(6):
                            K.o("pe", lambda E, dt=dtgt, b=dh, dl=dcl, f=fc,
                                t2=tt2: E.matmul(
                                    dt,
                                    dnb[b][:, :].rearrange(
                                        "p (dc fc j) -> p dc fc j", dc=8,
                                        fc=6)[:, dl, f, :],
                                    hsb[:, f, 512 * t2:512 * t2 + 512],
                                    start=(f == 0), stop=(f == 5)),
                                [(pe, 1)] if fc == 5 else ())
                        K.w("dve", pe, pe.n)
                        K.o("dve", lambda E, dt=dtgt, d=dc, t2=tt2: E.tensor_add(
                            ysb[:, d, 512 * t2:512 * t2 + 512],
                            ysb[:, d, 512 * t2:512 * t2 + 512],
                            dt), [(dve, 1)])
                        ydve[dn_g] = dve.n
                        dn_g += 1
                dn_pe[(hh, e, dh)] = pe.n
                if e + 1 < NEXP:
                    emit_dn_dma(e + 1, dh, dn_pe[(hh, e, dh)])
            prev_y_add = dve.n

        for tt2 in range(2):
            yv = ysb[:, :, 512 * tt2:512 * tt2 + 512]
            K.w("dve", dve)
            K.w("act", dve, dve.n)
            K.w("pe", dve, dve.n)
            for s_ in (dgu0, dgu1, ddn0, ddn1):
                K.w("pe", s_, s_.n)
            emit_ln(yv, ybd, sqd)
            emit_ln_transform(yv)
            K.w("act", dve, dve.n)
            K.w("act", dst0, dst0.n)
            tcol = 1024 * hh + 512 * tt2
            dma("act", dst0, outr[:, :, tcol:tcol + 512], yv)
        K.gate(ALLS)

    return _finish(nc, K)


def _finish(nc, K):
    with nc.Block() as block:
        @block.tensor
        def _(tensor):
            for f in K.q["pe"]:
                f(tensor)

        @block.vector
        def _(vector):
            for f in K.q["dve"]:
                f(vector)

        @block.scalar
        def _(scalar):
            for f in K.q["act"]:
                f(scalar)

        @block.gpsimd
        def _(gpsimd):
            for f in K.q["po"]:
                f(gpsimd)

        @block.sync
        def _(sync):
            for f in K.q["sp"]:
                f(sync)
    return nc



def _host_prep(W_in, b_in, W_out, b_out, gate_w, gate_up, down, similarity):
    perm = np.concatenate([np.arange(0, ROT, 2), np.arange(1, ROT, 2),
                           np.arange(ROT, D)]).astype(np.int64)
    rows = np.concatenate([perm, D + perm, 2 * D + perm])
    W_in_p = np.asarray(W_in)[rows]
    b_in_p = np.asarray(b_in)[rows]
    win_t = np.ascontiguousarray(
        W_in_p.reshape(48, 128, 16, 128).transpose(0, 3, 2, 1))
    bin_a = np.ascontiguousarray(b_in_p.reshape(48, 128).T)
    WoutT_p = np.ascontiguousarray(np.asarray(W_out).T[perm])
    wout_t = np.ascontiguousarray(
        WoutT_p.reshape(16, 128, 16, 128).transpose(2, 1, 0, 3))
    bout_a = np.ascontiguousarray(np.asarray(b_out).reshape(16, 128).T)
    gw_t = np.ascontiguousarray(
        np.asarray(gate_w).T.reshape(16, 128, 32).transpose(1, 0, 2))
    gup_t = np.ascontiguousarray(
        np.asarray(gate_up).reshape(32, 12, 128, 16, 128)
        .transpose(0, 1, 4, 3, 2))
    gdn_t = np.ascontiguousarray(
        np.asarray(down).reshape(32, 16, 128, 6, 128)
        .transpose(0, 4, 1, 3, 2))
    sim_a = np.ascontiguousarray(np.asarray(similarity), dtype=np.float32)

    inv = (10000.0 ** (-np.arange(0, ROT, 2, dtype=np.float64) / ROT))
    ang = np.arange(4096, dtype=np.float64)[None, :] * inv[:, None]
    cos_t = np.cos(ang).astype(np.float32)
    sin_t = np.sin(ang).astype(np.float32)

    cstv = np.zeros((128, 200), dtype=np.float32)
    cstv[:, 0:128] = np.eye(128, dtype=np.float32)
    cstv[:, 128] = np.arange(128, dtype=np.float32)
    cstv[:, 129] = 1.0
    cstv[:, 130:162] = np.arange(32, dtype=np.float32)[None, :]
    cstv[:, 162:166] = np.arange(4, dtype=np.float32)[None, :]
    crowv = np.ones((1, 128), dtype=np.float32)
    eh = np.zeros((32, 4096), dtype=np.float32)
    for e in range(32):
        eh[e, 128 * e:128 * (e + 1)] = 1.0
    csel = np.zeros((32, 256), dtype=np.float32)
    for r in range(16):
        csel[:, 16 * r + r] = 1.0
    return dict(win=win_t, binp=bin_a, wout=wout_t, boutp=bout_a, gw=gw_t,
                gup=gup_t, gdn=gdn_t, simm=sim_a, cost=cos_t, sint=sin_t,
                cst=cstv, crow=crowv, ehot=eh, colsel=csel)


def make_in_maps(inputs):
    common = _host_prep(
        inputs["L0_attn_W_in"], inputs["L0_attn_b_in"], inputs["L0_attn_W_out"],
        inputs["L0_attn_b_out"], inputs["L0_ffn_gate_weight"],
        inputs["L0_ffn_gate_up_proj"], inputs["L0_ffn_down_proj"],
        inputs["L0_ffn_similarity"])
    X = np.asarray(inputs["L0_attn_X"])
    in_maps = []
    for core in range(8):
        b, h = core // 2, core % 2
        xt = np.ascontiguousarray(X[b].T)
        own = xt[:, 2048 * h:2048 * (h + 1)]
        par = xt[:, 2048 * (1 - h):2048 * (2 - h)]
        m = dict(common)
        m["xb"] = np.ascontiguousarray(np.concatenate([own, par], axis=1))
        for key in ("cost", "sint"):
            t = common[key]
            t_own = t[:, 2048 * h:2048 * (h + 1)]
            t_par = t[:, 2048 * (1 - h):2048 * (2 - h)]
            m[key] = np.ascontiguousarray(np.concatenate([t_own, t_par], axis=1))
        in_maps.append(m)
    return in_maps


def kernel(**inputs):
    in_maps = make_in_maps(inputs)
    nc = build_nc()
    res = run_bass_kernel_spmd(nc, in_maps, core_ids=list(range(8)))
    X = np.asarray(inputs["L0_attn_X"])
    out = np.empty_like(X)
    for core in range(8):
        b, h = core // 2, core % 2
        oc = np.asarray(res.results[core]["out"])
        out[b, 2048 * h:2048 * (h + 1), :] = oc.T
    return out


# version 07
# speedup vs baseline: 1.0085x; 1.0085x over previous
import numpy as np
import concourse.bass as bass
import concourse.mybir as mybir
from concourse.bass_utils import run_bass_kernel_spmd

F32 = mybir.dt.float32
BF16 = mybir.dt.bfloat16
U16 = mybir.dt.uint16
U8 = mybir.dt.uint8
AF = mybir.ActivationFunctionType
ALU = mybir.AluOpType
AX = mybir.AxisListType

C_STEPS = 99
D = 2048
NSLAB = 8
NOWN = 4
NEXP = 32
ROT = 1024


class Sem:
    def __init__(self, nc, name):
        self._cm = nc.semaphore(name)
        self.h = self._cm.__enter__()
        self.n = 0


class KB:
    ENGS = ("pe", "dve", "act", "po", "sp")

    def __init__(self, nc):
        self.nc = nc
        self.q = {e: [] for e in self.ENGS}

    def w(self, eng, sem, val=None):
        v = sem.n if val is None else val
        self.q[eng].append(lambda E: E.wait_ge(sem.h, v))

    def o(self, eng, fn, inc=()):
        inc = tuple(inc)
        for s, k in inc:
            s.n += k

        def run(E):
            ins = fn(E)
            for s, k in inc:
                ins.then_inc(s.h, k)
        self.q[eng].append(run)

    def gate(self, sems):
        vals = [(s, s.n) for s in sems]
        for e in self.ENGS:
            for s, v in vals:
                self.w(e, s, v)


def build_nc(phlim=9):
    nc = bass.Bass("TRN2", target_bir_lowering=False)

    xb = nc.dram_tensor("xb", [D, 4096], F32, kind="ExternalInput").ap()
    win = nc.dram_tensor("win", [48, 128, 16, 128], F32, kind="ExternalInput").ap()
    binp = nc.dram_tensor("binp", [128, 48], F32, kind="ExternalInput").ap()
    wout = nc.dram_tensor("wout", [16, 128, 16, 128], F32, kind="ExternalInput").ap()
    boutp = nc.dram_tensor("boutp", [128, 16], F32, kind="ExternalInput").ap()
    gw = nc.dram_tensor("gw", [128, 16, 32], F32, kind="ExternalInput").ap()
    gup = nc.dram_tensor("gup", [32, 12, 128, 16, 128], F32, kind="ExternalInput").ap()
    gdn = nc.dram_tensor("gdn", [32, 128, 16, 6, 128], F32, kind="ExternalInput").ap()
    simm = nc.dram_tensor("simm", [32, 32], F32, kind="ExternalInput").ap()
    cost = nc.dram_tensor("cost", [512, 4096], F32, kind="ExternalInput").ap()
    sint = nc.dram_tensor("sint", [512, 4096], F32, kind="ExternalInput").ap()
    cst = nc.dram_tensor("cst", [128, 200], F32, kind="ExternalInput").ap()
    crow = nc.dram_tensor("crow", [1, 128], F32, kind="ExternalInput").ap()
    ehot = nc.dram_tensor("ehot", [32, 4096], F32, kind="ExternalInput").ap()
    colsel = nc.dram_tensor("colsel", [32, 256], F32, kind="ExternalInput").ap()
    out = nc.dram_tensor("out", [D, 2048], F32, kind="ExternalOutput").ap()

    qd = nc.dram_tensor("qd", [16, NOWN, 128, 512], BF16).ap()
    kd = nc.dram_tensor("kd", [16, NSLAB, 128, 512], BF16).ap()
    vd = nc.dram_tensor("vd", [16, NSLAB, 128, 512], BF16).ap()
    x1f = nc.dram_tensor("x1f", [16, NOWN, 128, 512], F32).ap()
    x1bd = nc.dram_tensor("x1bd", [16, NOWN, 128, 512], BF16).ap()

    pmm = [nc.alloc_psum_tensor(f"pmm{i}", [128, 512], F32) for i in range(4)]
    pA = nc.alloc_psum_tensor("pA", [128, 1024], F32)
    pB = nc.alloc_psum_tensor("pB", [128, 1024], F32)

    alloc = nc.alloc_sbuf_tensor
    ident = alloc("ident", [128, 128], F32)
    onesrow = alloc("onesrow", [1, 128], F32)
    onescolb = alloc("onescolb", [128, 1], BF16)
    cst_sb = alloc("cst_sb", [128, 200], F32)
    bin_sb = alloc("bin_sb", [128, 48], F32)
    bout_sb = alloc("bout_sb", [128, 16], F32)
    gw_sb = alloc("gw_sb", [128, 16, 32], BF16)
    sim_f = alloc("sim_f", [32, 32], F32)
    sim_fx = alloc("sim_fx", [32, 32], F32)
    sim_b = alloc("sim_b", [32, 32], BF16)
    ones32 = alloc("ones32", [32, 32], F32)
    mask32 = alloc("mask32", [32, 32], U8)
    ehot_sb = alloc("ehot_sb", [32, 4096], BF16)
    colsel_sb = alloc("colsel_sb", [32, 256], BF16)
    A_acc = alloc("A_acc", [128, 16], F32)
    A_tmp = alloc("A_tmp", [128, 16], F32)
    comb_sb = alloc("comb_sb", [32, 2048], BF16)
    rowsc = alloc("rowsc", [1, 2048], F32)
    idxrow = alloc("idxrow", [1, 1024], F32)
    i8u = alloc("i8u", [128, 8], U16)
    thrU = alloc("thrU", [128, 4], U8)

    iotaP = cst_sb[:, 128:129]
    iota32 = cst_sb[:, 130:162]
    iota4 = cst_sb[:, 162:166]

    AREN_W = (nc.sbuf_bytes_remaining - 2048) // 4
    arena = alloc("arena", [128, AREN_W], F32)
    aren_f = arena[:, :]
    aren_b = arena[:, :].bitcast(BF16)

    def vf(off_b, shape):
        n = int(np.prod(shape[1:]))
        v = aren_f[:, off_b // 4: off_b // 4 + n]
        if len(shape) == 3:
            v = v.rearrange("p (a c) -> p a c", a=shape[1])
        return v

    def vb(off_b, shape):
        n = int(np.prod(shape[1:]))
        v = aren_b[:, off_b // 2: off_b // 2 + n]
        if len(shape) == 3:
            v = v.rearrange("p (a c) -> p a c", a=shape[1])
        return v

    AREN_B = AREN_W * 4

    K = KB(nc)
    dsp = Sem(nc, "dsp")
    dst0 = Sem(nc, "dst0")
    dst1 = Sem(nc, "dst1")
    dpre = Sem(nc, "dpre")
    dx = Sem(nc, "dx")
    dwa = Sem(nc, "dwa")
    dwb = Sem(nc, "dwb")
    dgu0 = Sem(nc, "dgu0")
    dgu1 = Sem(nc, "dgu1")
    ddn0 = Sem(nc, "ddn0")
    ddn1 = Sem(nc, "ddn1")
    dxb = Sem(nc, "dxb")
    pe = Sem(nc, "pe")
    dve = Sem(nc, "dve")
    ac = Sem(nc, "ac")
    po = Sem(nc, "po")
    ALLS = [dsp, dst0, dst1, dpre, dx, dwa, dwb, dgu0, dgu1, ddn0, ddn1,
            dxb, pe, dve, ac, po]

    def dma(eng, sem, outp, inp, slow=False):
        K.o(eng, lambda E, o=outp, i=inp, s=slow: E.dma_start(
            out=o, in_=i, allow_slow_non_contiguous=s), [(sem, 16)])

    dma("sp", dsp, ident[:, :], cst[:, 0:128])
    dma("sp", dsp, cst_sb[:, :], cst)
    dma("sp", dsp, onesrow[:, :], crow)
    dma("sp", dsp, bin_sb[:, :], binp)
    dma("sp", dsp, bout_sb[:, :], boutp)
    dma("sp", dsp, sim_f[:, :], simm)
    dma("po", dpre, onescolb[:, :], cst[:, 129:130], slow=True)
    dma("po", dpre, gw_sb[:, :, :], gw)
    dma("po", dpre, ehot_sb[:, :], ehot)
    dma("po", dpre, colsel_sb[:, :], colsel)
    pre_dsp, pre_dg = dsp.n, dpre.n

    K.w("dve", dsp, pre_dsp)
    K.o("dve", lambda E: E.memset(ones32[:, :], 1.0))
    K.o("dve", lambda E: E.tensor_scalar(mask32[:, :], ident[0:32, 0:32], 0.5,
                                         None, ALU.is_gt), [(dve, 1)])
    K.w("dve", dve)
    K.o("dve", lambda E: E.select(sim_fx[:, :], mask32[:, :], ones32[:, :],
                                  sim_f[:, :], add_drain=True), [(dve, 1)])
    K.w("dve", dve)
    K.o("dve", lambda E: E.tensor_copy(sim_b[:, :], sim_fx[:, :]), [(dve, 1)])
    K.o("dve", lambda E: E.memset(A_acc[:, :], 0.0), [(dve, 1)])

    aoff = 0
    xall = vb(aoff, [128, 16, 4096]); aoff += 16 * 4096 * 2
    winb = [vb(aoff + i * 16 * 128 * 2, [128, 16, 128]) for i in range(2)]
    aoff += 2 * 16 * 128 * 2
    evb = [vb(aoff + i * 512 * 2, [128, 512]) for i in range(2)]
    aoff += 2 * 512 * 2
    assert aoff <= AREN_B, (aoff, AREN_B)

    xbr = xb.rearrange("(dc p) t -> p dc t", p=128)
    for s in range(NSLAB):
        dma("po", dx, xall[:, :, 512 * s:512 * (s + 1)],
            xbr[:, :, 512 * s:512 * (s + 1)])

    plan = [(ec, list(range(NSLAB))) for ec in range(16, 48)] + \
           [(ec, list(range(NOWN))) for ec in range(16)]

    g = 0
    evac_dve_val = {}
    evstore_val = {}
    pe_after_iter = []
    dw = [dwa, dwb]
    dstp = [dst0, dst1]
    for i, (ec, slabs) in enumerate(plan):
        buf = i % 2
        if i >= 2:
            K.w("po", pe, pe_after_iter[i - 2])
        K.w("po", dw[buf], dw[buf].n)
        dma("po", dw[buf], winb[buf][:, :, :], win[ec])
        K.w("pe", dx, dx.n)
        K.w("pe", dw[buf], dw[buf].n)
        for s in slabs:
            bank = g % 4
            if g >= 4:
                K.w("pe", dve, evac_dve_val[g - 4])
            for kc in range(16):
                K.o("pe", lambda E, b=bank, bu=buf, k=kc, ss=s: E.matmul(
                    pmm[b][:, :], winb[bu][:, k, :],
                    xall[:, k, 512 * ss:512 * ss + 512],
                    start=(k == 0), stop=(k == 15)),
                    [(pe, 1)] if kc == 15 else ())
            K.w("dve", pe, pe.n)
            sb_ = dstp[g % 2]
            if g >= 2:
                K.w("dve", sb_, evstore_val[g - 2])
            eb = evb[g % 2]
            K.o("dve", lambda E, b=bank, e=ec, ebuf=eb: E.tensor_scalar(
                ebuf[:, :], pmm[b][:, :], bin_sb[:, e:e + 1], None, ALU.add),
                [(dve, 1)])
            evac_dve_val[g] = dve.n
            K.w("act", dve, dve.n)
            if ec < 16:
                tgt = qd[ec, s]
            elif ec < 32:
                tgt = kd[ec - 16, s]
            else:
                tgt = vd[ec - 32, s]
            dma("act", sb_, tgt, eb[:, :])
            evstore_val[g] = sb_.n
            g += 1
        pe_after_iter.append(pe.n)
    a1_dst0, a1_dst1 = dst0.n, dst1.n

    K.gate(ALLS)

    if phlim < 2:
        return _finish(nc, K)
    aoff = 0
    Kb = vb(aoff, [128, 16, 512]); aoff += 16 * 512 * 2
    Vb = vb(aoff, [128, 16, 512]); aoff += 16 * 512 * 2
    cosb = vf(aoff, [128, 4, 512]); aoff += 4 * 512 * 4
    sinb = vf(aoff, [128, 4, 512]); aoff += 4 * 512 * 4
    Kr = vf(aoff, [128, 8, 512]); aoff += 8 * 512 * 4
    rt1 = vf(aoff, [128, 4, 512]); aoff += 4 * 512 * 4
    rt2 = vf(aoff, [128, 4, 512]); aoff += 4 * 512 * 4
    sqb = vb(aoff, [128, 16, 512]); aoff += 16 * 512 * 2
    kvb = vf(aoff, [128, 16, 512]); aoff += 16 * 512 * 4
    kv2 = vf(aoff, [128, 16, 512]); aoff += 16 * 512 * 4
    assert aoff <= AREN_B, (aoff, AREN_B)

    kdr = kd.rearrange("ec s p t -> p ec s t")
    vdr = vd.rearrange("ec s p t -> p ec s t")
    costr = cost.rearrange("(cc p) t -> p cc t", p=128)
    sintr = sint.rearrange("(cc p) t -> p cc t", p=128)

    for s in range(NSLAB):
        K.w("sp", dst0, a1_dst0)
        K.w("sp", dst1, a1_dst1)
        K.w("sp", dsp, dsp.n)
        if s >= 1:
            K.w("sp", dve, dve.n)
            K.w("sp", ac, ac.n)
            K.w("sp", po, po.n)
        dma("sp", dsp, Kb[:, :, :], kdr[:, :, s, :])
        dma("sp", dsp, Vb[:, :, :], vdr[:, :, s, :])
        dma("sp", dsp, cosb[:, :, :], costr[:, :, 512 * s:512 * s + 512])
        dma("sp", dsp, sinb[:, :, :], sintr[:, :, 512 * s:512 * s + 512])
        K.w("dve", dsp, dsp.n)
        K.w("po", dsp, dsp.n)
        for c in range(4):
            K.o("dve", lambda E, c=c: E.tensor_mul(rt1[:, c, :], Kb[:, c, :],
                                                   cosb[:, c, :]))
            K.o("dve", lambda E, c=c: E.tensor_mul(rt2[:, c, :], Kb[:, c + 4, :],
                                                   sinb[:, c, :]),
                [(dve, 1)] if c == 3 else ())
        for c in range(4):
            K.o("po", lambda E, c=c: E.tensor_mul(kvb[:, c, :], Kb[:, c, :],
                                                  sinb[:, c, :]))
            K.o("po", lambda E, c=c: E.tensor_mul(kvb[:, c + 4, :], Kb[:, c + 4, :],
                                                  cosb[:, c, :]),
                [(po, 1)] if c == 3 else ())
        K.w("dve", dve)
        K.w("dve", po)
        for c in range(4):
            K.o("dve", lambda E, c=c: E.tensor_sub(Kr[:, c, :], rt1[:, c, :],
                                                   rt2[:, c, :]))
            K.o("dve", lambda E, c=c: E.tensor_add(Kr[:, c + 4, :], kvb[:, c + 4, :],
                                                   kvb[:, c, :]),
                [(dve, 1)] if c == 3 else ())
        K.w("act", dve, dve.n)
        for c in range(16):
            src = Kr[:, c, :] if c < 8 else Kb[:, c, :]
            K.o("act", lambda E, s_=src, c=c: E.activation(sqb[:, c, :], s_,
                                                           AF.Square),
                [(ac, 1)] if c == 15 else ())
        K.w("pe", ac, ac.n)
        K.w("pe", dve, dve.n)
        for c in range(16):
            K.o("pe", lambda E, c=c: E.matmul(pA[0:1, 0:512], onescolb[:, :],
                                              sqb[:, c, :], start=(c == 0),
                                              stop=(c == 15)),
                [(pe, 1)] if c == 15 else ())
        K.w("act", pe, pe.n)
        K.o("act", lambda E: E.activation(rowsc[0:1, 0:512], pA[0:1, 0:512],
                                          AF.Sqrt), [(ac, 1)])
        K.w("dve", ac, ac.n)
        K.o("dve", lambda E: E.tensor_scalar(rowsc[0:1, 512:1024],
                                             rowsc[0:1, 0:512], 1e-12, None,
                                             ALU.max), [(dve, 1)])
        K.w("dve", dve)
        K.o("dve", lambda E: E.reciprocal(rowsc[0:1, 0:512],
                                          rowsc[0:1, 512:1024]), [(dve, 1)])
        K.w("pe", dve, dve.n)
        K.o("pe", lambda E: E.matmul(pB[:, 0:512], onesrow[:, :],
                                     rowsc[0:1, 0:512], start=True, stop=True),
            [(pe, 1)])
        K.w("dve", pe, pe.n)
        K.w("dve", dve)
        for c in range(16):
            ksrc = Kr[:, c, :] if c < 8 else Kb[:, c, :]
            K.o("dve", lambda E, k=ksrc, c=c: E.tensor_mul(kvb[:, c, :], k,
                                                           Vb[:, c, :]),
                [(dve, 1)] if c == 15 else ())
        K.w("dve", dve)
        for c in range(16):
            K.o("dve", lambda E, c=c: E.tensor_mul(kv2[:, c, :], kvb[:, c, :],
                                                   pB[:, 0:512]),
                [(dve, 1)] if c == 15 else ())
        K.w("dve", dve)
        K.w("dve", po, po.n)
        for c in range(16):
            K.o("dve", lambda E, c=c: E.reduce_sum(A_tmp[:, c:c + 1],
                                                   kv2[:, c, :], axis=AX.X),
                [(dve, 1)] if c == 15 else ())
        K.w("po", dve, dve.n)
        K.o("po", lambda E: E.tensor_add(A_acc[:, :], A_acc[:, :], A_tmp[:, :]),
            [(po, 1)])

    K.gate(ALLS)

    if phlim < 3:
        return _finish(nc, K)
    boff = 0
    wob = [vb(boff + i * 16 * 128 * 2, [128, 16, 128]) for i in range(2)]
    boff += 2 * 16 * 128 * 2
    Qb2 = [vb(boff + i * 16 * 512 * 2, [128, 16, 512]) for i in range(2)]
    boff += 2 * 16 * 512 * 2
    qa2 = [vb(boff + i * 16 * 512 * 2, [128, 16, 512]) for i in range(2)]
    boff += 2 * 16 * 512 * 2
    Qr = vf(boff, [128, 8, 512]); boff += 8 * 512 * 4
    cosb2 = vb(boff, [128, 4, 512]); boff += 4 * 512 * 2
    sinb2 = vb(boff, [128, 4, 512]); boff += 4 * 512 * 2
    rtb1 = vf(boff, [128, 4, 512])
    sqb2 = vb(boff, [128, 16, 512])
    boff += 4 * 512 * 4
    rtb2 = vf(boff, [128, 4, 512]); boff += 4 * 512 * 4
    Xf = vf(boff, [128, 16, 512]); boff += 16 * 512 * 4
    out1 = vf(boff, [128, 16, 512]); boff += 16 * 512 * 4
    assert boff <= AREN_B, (boff, AREN_B)

    qdr = qd.rearrange("ec s p t -> p ec s t")
    x1fr = x1f.rearrange("ec s p t -> p ec s t")
    x1br = x1bd.rearrange("ec s p t -> p ec s t")

    def emit_ln(src3, bconv, sqv, eps=1e-5):
        K.w("act", dve, dve.n)
        for c in range(16):
            K.o("act", lambda E, c=c: E.activation(sqv[:, c, :], src3[:, c, :],
                                                   AF.Square),
                [(ac, 1)] if c == 15 else ())
        K.w("dve", dve)
        for c in range(16):
            K.o("dve", lambda E, c=c: E.tensor_copy(bconv[:, c, :],
                                                    src3[:, c, :]),
                [(dve, 1)] if c == 15 else ())
        K.w("pe", ac, ac.n)
        K.w("pe", dve, dve.n)
        for c in range(16):
            K.o("pe", lambda E, c=c: E.matmul(pA[0:1, 0:512], onescolb[:, :],
                                              bconv[:, c, :], start=(c == 0),
                                              stop=(c == 15)),
                [(pe, 1)] if c == 15 else ())
        for c in range(16):
            K.o("pe", lambda E, c=c: E.matmul(pA[0:1, 512:1024], onescolb[:, :],
                                              sqv[:, c, :], start=(c == 0),
                                              stop=(c == 15)),
                [(pe, 1)] if c == 15 else ())
        emit_ln_rows(src3, eps)

    def emit_ln_rows(src3, eps=1e-5):
        K.w("dve", pe, pe.n)
        K.o("dve", lambda E: E.tensor_scalar(rowsc[0:1, 0:512], pA[0:1, 0:512],
                                             1.0 / D, None, ALU.mult), [(dve, 1)])
        K.w("act", dve, dve.n)
        K.o("act", lambda E: E.activation(rowsc[0:1, 512:1024], rowsc[0:1, 0:512],
                                          AF.Square), [(ac, 1)])
        K.w("dve", ac, ac.n)
        K.o("dve", lambda E: E.scalar_tensor_tensor(
            rowsc[0:1, 1024:1536], pA[0:1, 512:1024], 1.0 / D,
            rowsc[0:1, 512:1024], ALU.mult, ALU.subtract), [(dve, 1)])
        K.w("dve", dve)
        K.o("dve", lambda E: E.tensor_scalar(rowsc[0:1, 512:1024],
                                             rowsc[0:1, 1024:1536], eps, None,
                                             ALU.add), [(dve, 1)])
        K.w("act", dve, dve.n)
        K.o("act", lambda E: E.activation(rowsc[0:1, 1024:1536],
                                          rowsc[0:1, 512:1024], AF.Sqrt),
            [(ac, 1)])
        K.w("dve", ac, ac.n)
        K.o("dve", lambda E: E.reciprocal(rowsc[0:1, 1536:2048],
                                          rowsc[0:1, 1024:1536]), [(dve, 1)])
        K.w("pe", dve, dve.n)
        K.o("pe", lambda E: E.matmul(pB[:, 0:512], onesrow[:, :],
                                     rowsc[0:1, 0:512], start=True, stop=True))
        K.o("pe", lambda E: E.matmul(pB[:, 512:1024], onesrow[:, :],
                                     rowsc[0:1, 1536:2048], start=True,
                                     stop=True), [(pe, 1)])

    def emit_ln_transform(src3):
        K.w("dve", pe, pe.n)
        for c in range(16):
            K.o("dve", lambda E, c=c: E.tensor_sub(src3[:, c, :], src3[:, c, :],
                                                   pB[:, 0:512]),
                [(dve, 1)] if c == 15 else ())
        K.w("dve", dve)
        for c in range(16):
            K.o("dve", lambda E, c=c: E.tensor_mul(src3[:, c, :], src3[:, c, :],
                                                   pB[:, 512:1024]),
                [(dve, 1)] if c == 15 else ())

    qa_done = {}
    rope_done = {}
    xf_load_val = {}
    xf_read_done = {}

    def emit_xf_load(tt):
        K.w("sp", dsp, dsp.n)
        if (tt - 1) in xf_read_done:
            K.w("sp", dve, xf_read_done[tt - 1])
        dma("sp", dsp, Xf[:, :, :], xbr[:, :, 512 * tt:512 * tt + 512])
        xf_load_val[tt] = dsp.n

    def emit_b_loads(tt):
        b = tt % 2
        K.w("sp", dst0, a1_dst0)
        K.w("sp", dst1, a1_dst1)
        K.w("sp", dsp, dsp.n)
        if tt >= 2 and (tt - 2) in qa_done:
            K.w("sp", dve, qa_done[tt - 2])
        dma("sp", dsp, Qb2[b][:, :, :], qdr[:, :, tt, :])
        K.w("po", dx, dx.n)
        if tt >= 1 and (tt - 1) in rope_done:
            K.w("po", dve, rope_done[tt - 1])
        dma("po", dx, cosb2[:, :, :], costr[:, :, 512 * tt:512 * tt + 512])
        dma("po", dx, sinb2[:, :, :], sintr[:, :, 512 * tt:512 * tt + 512])
        return (dsp.n, dx.n)

    def emit_rope_qa(tt, load_val):
        b = tt % 2
        K.w("dve", dsp, load_val[0])
        K.w("dve", dx, load_val[1])
        K.w("dve", ac, ac.n)
        K.w("dve", pe, pe.n)
        K.w("dve", dve)
        for c in range(4):
            K.o("dve", lambda E, c=c: E.tensor_mul(rtb1[:, c, :],
                                                   Qb2[b][:, c, :],
                                                   cosb2[:, c, :]))
            K.o("dve", lambda E, c=c: E.tensor_mul(rtb2[:, c, :],
                                                   Qb2[b][:, c + 4, :],
                                                   sinb2[:, c, :]),
                [(dve, 1)] if c == 3 else ())
        K.w("dve", dve)
        for c in range(4):
            K.o("dve", lambda E, c=c: E.tensor_sub(Qr[:, c, :], rtb1[:, c, :],
                                                   rtb2[:, c, :]),
                [(dve, 1)] if c == 3 else ())
        K.w("dve", dve)
        for c in range(4):
            K.o("dve", lambda E, c=c: E.tensor_mul(rtb1[:, c, :],
                                                   Qb2[b][:, c, :],
                                                   sinb2[:, c, :]))
            K.o("dve", lambda E, c=c: E.tensor_mul(rtb2[:, c, :],
                                                   Qb2[b][:, c + 4, :],
                                                   cosb2[:, c, :]),
                [(dve, 1)] if c == 3 else ())
        K.w("dve", dve)
        for c in range(4):
            K.o("dve", lambda E, c=c: E.tensor_add(Qr[:, c + 4, :],
                                                   rtb2[:, c, :],
                                                   rtb1[:, c, :]),
                [(dve, 1)] if c == 3 else ())
        rope_done[tt] = dve.n
        K.w("dve", dve)
        for c in range(16):
            s_ = Qr[:, c, :] if c < 8 else Qb2[b][:, c, :]
            K.o("dve", lambda E, s2=s_, c=c: E.tensor_scalar(
                qa2[b][:, c, :], s2, A_acc[:, c:c + 1], None, ALU.mult),
                [(dve, 1)] if c == 15 else ())
        qa_done[tt] = dve.n

    wout_g = 0
    woutev = {}
    wo_plan_pe = []
    s1_v = {}
    sq_v = {}
    conv_v = {}

    def emit_s1(e_):
        if e_ < 0:
            return
        K.w("pe", dve, conv_v[e_])
        K.o("pe", lambda E, e=e_: E.matmul(pA[0:1, 0:512], onescolb[:, :],
                                           sqb2[:, e, :], start=(e == 0),
                                           stop=(e == 15),
                                           skip_group_check=True), [(pe, 1)])
        s1_v[e_] = pe.n

    def emit_sq(e_):
        if e_ < 0:
            return
        K.w("act", dve, conv_v[e_])
        K.w("act", pe, s1_v[e_])
        K.o("act", lambda E, e=e_: E.activation(sqb2[:, e, :], out1[:, e, :],
                                                AF.Square), [(ac, 1)])
        sq_v[e_] = ac.n

    def emit_s2(e_):
        if e_ < 0:
            return
        K.w("pe", ac, sq_v[e_])
        K.o("pe", lambda E, e=e_: E.matmul(pA[0:1, 512:1024], onescolb[:, :],
                                           sqb2[:, e, :], start=(e == 0),
                                           stop=(e == 15),
                                           skip_group_check=True), [(pe, 1)])

    ld0 = emit_b_loads(0)
    emit_xf_load(0)
    emit_rope_qa(0, ld0)
    for tt in range(NOWN):
        b = tt % 2
        if tt + 1 < NOWN:
            next_load = emit_b_loads(tt + 1)
        if tt >= 1:
            K.w("dve", dst0, dst0.n)
            K.w("dve", dxb, dxb.n)
        for ec in range(16):
            wbuf = ec % 2
            widx = tt * 16 + ec
            if widx >= 2:
                K.w("po", pe, wo_plan_pe[widx - 2])
            K.w("po", dw[wbuf], dw[wbuf].n)
            dma("po", dw[wbuf], wob[wbuf][:, :, :], wout[ec])
            bank = wout_g % 4
            K.w("pe", dw[wbuf], dw[wbuf].n)
            K.w("pe", dve, qa_done[tt])
            if wout_g >= 4:
                K.w("pe", dve, woutev[wout_g - 4])
            for kc in range(16):
                K.o("pe", lambda E, bk=bank, wb=wbuf, k=kc, bb=b: E.matmul(
                    pmm[bk][:, :], wob[wb][:, k, :], qa2[bb][:, k, :],
                    start=(k == 0), stop=(k == 15)),
                    [(pe, 1)] if kc == 15 else ())
            wo_plan_pe.append(pe.n)
            K.w("dve", pe, pe.n)
            K.w("dve", dsp, xf_load_val[tt])
            K.o("dve", lambda E, bk=bank, e=ec: E.scalar_tensor_tensor(
                out1[:, e, :], pmm[bk][:, :], bout_sb[:, e:e + 1],
                Xf[:, e, :], ALU.add, ALU.add), [(dve, 1)])
            K.w("dve", dve)
            K.o("dve", lambda E, e=ec: E.tensor_add(out1[:, e, :],
                                                    out1[:, e, :],
                                                    Xf[:, e, :]),
                [(dve, 1)])
            woutev[wout_g] = dve.n
            wout_g += 1
            K.w("dve", dve)
            K.o("dve", lambda E, e=ec: E.tensor_copy(sqb2[:, e, :],
                                                     out1[:, e, :]), [(dve, 1)])
            conv_v[ec] = dve.n
            emit_s1(ec - 2)
            emit_sq(ec - 2)
            emit_s2(ec - 3)
        for e_ in (14, 15):
            emit_s1(e_)
            emit_sq(e_)
        for e_ in (13, 14, 15):
            emit_s2(e_)
        xf_read_done[tt] = dve.n
        if tt + 1 < NOWN:
            emit_xf_load(tt + 1)
        emit_ln_rows(out1)
        if tt + 1 < NOWN:
            emit_rope_qa(tt + 1, next_load)
        emit_ln_transform(out1)
        K.w("act", dve, dve.n)
        K.w("act", dst0, dst0.n)
        dma("act", dst0, x1fr[:, :, tt, :], out1[:, :, :])
        K.w("po", dve, dve.n)
        K.w("po", dxb, dxb.n)
        dma("po", dxb, x1br[:, :, tt, :], out1[:, :, :])

    b_dst0 = dst0.n
    b_dxb = dxb.n
    K.gate(ALLS)

    _co = [0]

    def cvf(shape):
        v = vf(_co[0], shape)
        n = int(np.prod(shape[1:])) * 4
        _co[0] += (n + 63) & ~63
        return v

    def cvb(shape):
        v = vb(_co[0], shape)
        n = int(np.prod(shape[1:])) * 2
        _co[0] += (n + 63) & ~63
        return v

    x1t = cvb([128, 16, 512])
    lgf = cvf([128, 512])
    ltok = cvf([128, 32])
    v8 = cvf([128, 8])
    idxf = cvf([128, 8])
    nv0 = cvf([128, 1])
    ev8 = cvf([128, 8])
    s8 = cvf([128, 1])
    r8 = cvf([128, 1])
    rw8 = cvf([128, 8])
    oh = cvb([128, 8 * 128])
    Mt = cvb([128, 16 * 128])
    scf = cvf([128, 128])
    sct = cvf([128, 16])
    bs4 = cvf([128, 4])
    cand = cvf([128, 16])
    am4 = cvf([128, 4])
    ohm = cvf([128, 16])
    pm16 = cvf([128, 16])
    bp4 = cvf([128, 4])
    rer8 = cvf([128, 8])
    ohc = cvf([128, 32 * 8])
    ohc2 = cvf([128, 32 * 8])
    combt = cvf([128, 32])
    assert _co[0] <= AREN_B, (_co[0], AREN_B)

    ohv = oh.rearrange("p (k t) -> p k t", k=8)[0:32, :, :]
    mtv = Mt.rearrange("p (r t) -> p r t", r=16)[0:32, :, :]
    sct3 = sct.rearrange("p (j i) -> p j i", i=4)
    cand3 = cand.rearrange("p (j i) -> p j i", i=4)
    ohm3 = ohm.rearrange("p (j i) -> p j i", i=4)
    pm3 = pm16.rearrange("p (j i) -> p j i", i=4)
    ohc3 = ohc.rearrange("p (e k) -> p e k", k=8)
    ohc23 = ohc2.rearrange("p (e k) -> p e k", k=8)

    for tt in range(NOWN):
        K.w("sp", dxb, b_dxb)
        K.w("sp", dsp, dsp.n)
        if tt >= 1:
            K.w("sp", pe, pe.n)
        dma("sp", dsp, x1t[:, :, :], x1br[:, :, tt, :])
        K.w("pe", dsp, dsp.n)
        K.w("pe", dpre, pre_dg)
        if tt >= 1:
            K.w("pe", dve, dve.n)
        for kc in range(16):
            K.o("pe", lambda E, k=kc: E.matmul(pmm[0][0:32, :], gw_sb[:, k, :],
                                               x1t[:, k, :], start=(k == 0),
                                               stop=(k == 15)),
                [(pe, 1)] if kc == 15 else ())
        K.w("dve", pe, pe.n)
        K.o("dve", lambda E: E.tensor_copy(lgf[0:32, :], pmm[0][0:32, :]),
            [(dve, 1)])
        for j in range(4):
            if C_STEPS < 2:
                continue
            K.w("pe", dve, dve.n)
            K.o("pe", lambda E, j=j: E.transpose(
                pmm[1][:, 0:32], lgf[0:32, 128 * j:128 * j + 128],
                ident[0:32, 0:32]), [(pe, 1)])
            K.w("dve", pe, pe.n)
            K.o("dve", lambda E: E.tensor_copy(ltok[:, :], pmm[1][:, 0:32]),
                [(dve, 1)])
            if C_STEPS < 3:
                continue
            K.w("dve", dve)
            K.o("dve", lambda E: E.max(out=v8[:, :], in_=ltok[:, :]), [(dve, 1)])
            K.w("dve", dve)
            K.o("dve", lambda E: E.max_index(i8u[:, :], v8[:, :], ltok[:, :]),
                [(dve, 1)])
            K.w("dve", dve)
            K.o("dve", lambda E: E.tensor_copy(idxf[:, :], i8u[:, :]))
            K.o("dve", lambda E: E.tensor_scalar(nv0[:, :], v8[:, 0:1], -1.0,
                                                 None, ALU.mult), [(dve, 1)])
            if C_STEPS < 4:
                continue
            K.w("act", dve, dve.n)
            import os as _os
            if _os.environ.get("CVAR") == "sq":
                K.o("act", lambda E: E.activation(ev8[:, :], v8[:, :],
                                                  AF.Square), [(ac, 1)])
            elif _os.environ.get("CVAR") == "nobias":
                K.o("act", lambda E: E.activation(ev8[:, :], v8[:, :],
                                                  AF.Exp), [(ac, 1)])
            else:
                K.o("act", lambda E: E.activation(ev8[:, :], v8[:, :], AF.Exp,
                                                  bias=nv0[:, :]), [(ac, 1)])
            K.w("dve", ac, ac.n)
            K.o("dve", lambda E: E.reduce_sum(s8[:, :], ev8[:, :], axis=AX.X),
                [(dve, 1)])
            K.w("dve", dve)
            K.o("dve", lambda E: E.reciprocal(r8[:, :], s8[:, :]), [(dve, 1)])
            K.w("dve", dve)
            K.o("dve", lambda E: E.tensor_scalar(rw8[:, :], ev8[:, :], r8[:, :],
                                                 None, ALU.mult), [(dve, 1)])
            if C_STEPS < 5:
                continue
            K.w("pe", dve, dve.n)
            for k in range(8):
                bank = k % 4
                if k >= 4:
                    K.w("pe", dve, dve.n)
                K.o("pe", lambda E, k=k, b=bank: E.transpose(
                    pmm[b][0:1, 0:128], idxf[:, k:k + 1],
                    ident[:, :]), [(pe, 1)])
                K.w("dve", pe, pe.n)
                K.o("dve", lambda E, k=k, b=bank: E.tensor_copy(
                    idxrow[0:1, 128 * k:128 * k + 128],
                    pmm[b][0:1, 0:128]), [(dve, 1)])
            if C_STEPS < 6:
                continue
            K.w("pe", dve, dve.n)
            K.o("pe", lambda E: E.matmul(pA[0:32, 0:512], onesrow[0:1, 0:32],
                                         idxrow[0:1, 0:512], start=True,
                                         stop=True))
            K.o("pe", lambda E: E.matmul(pA[0:32, 512:1024], onesrow[0:1, 0:32],
                                         idxrow[0:1, 512:1024], start=True,
                                         stop=True), [(pe, 1)])
            K.w("dve", pe, pe.n)
            K.o("dve", lambda E: E.tensor_scalar(
                ohv, pA[0:32, 0:1024].rearrange("p (k t) -> p k t", k=8),
                iotaP[0:32, :], None, ALU.is_equal), [(dve, 1)])
            if C_STEPS < 7:
                continue
            K.w("pe", dve, dve.n)
            for j2 in range(4):
                K.o("pe", lambda E, j2=j2: E.matmul(
                    pB[0:32, 128 * j2:128 * j2 + 128], sim_b[:, :],
                    ohv[:, 4 + j2, :], start=True, stop=True),
                    [(pe, 1)] if j2 == 3 else ())
            K.w("dve", pe, pe.n)
            for j2 in range(4):
                for i2 in range(4):
                    K.o("dve", lambda E, j2=j2, i2=i2: E.tensor_mul(
                        mtv[:, 4 * j2 + i2, :], ohv[:, i2, :],
                        pB[0:32, 128 * j2:128 * j2 + 128]),
                        [(dve, 1)] if (j2 == 3 and i2 == 3) else ())
            K.w("pe", dve, dve.n)
            for r in range(16):
                K.o("pe", lambda E, r=r: E.matmul(
                    pmm[3][0:16, 0:128], colsel_sb[:, 16 * r:16 * r + 16],
                    mtv[:, r, :], start=(r == 0), stop=(r == 15)),
                    [(pe, 1)] if r == 15 else ())
            K.w("dve", pe, pe.n)
            K.o("dve", lambda E: E.tensor_copy(scf[0:16, :], pmm[3][0:16, 0:128]),
                [(dve, 1)])
            K.w("pe", dve, dve.n)
            K.o("pe", lambda E: E.transpose(pmm[2][:, 0:16], scf[0:16, :],
                                            ident[0:16, 0:16]), [(pe, 1)])
            K.w("dve", pe, pe.n)
            K.o("dve", lambda E: E.tensor_copy(sct[:, :], pmm[2][:, 0:16]),
                [(dve, 1)])
            if C_STEPS < 8:
                continue
            K.w("dve", dve)
            K.o("dve", lambda E: E.reduce_max(bs4[:, :], sct3, axis=AX.X),
                [(dve, 1)])
            K.w("dve", dve)
            bs_b = bs4.unsqueeze(2).to_broadcast([128, 4, 4])
            K.o("dve", lambda E: E.tensor_tensor(cand3, sct3, bs_b, ALU.is_lt),
                [(dve, 1)])
            K.w("dve", dve)
            iota4_b = iota4.unsqueeze(1).to_broadcast([128, 4, 4])
            K.o("dve", lambda E: E.scalar_tensor_tensor(
                cand3, cand3, 8.0, iota4_b, ALU.mult, ALU.add), [(dve, 1)])
            K.w("dve", dve)
            K.o("dve", lambda E: E.tensor_reduce(am4[:, :], cand3, AX.X,
                                                 ALU.min), [(dve, 1)])
            K.w("dve", dve)
            am_b = am4.unsqueeze(2).to_broadcast([128, 4, 4])
            K.o("dve", lambda E: E.tensor_tensor(ohm3, am_b, iota4_b,
                                                 ALU.is_equal), [(dve, 1)])
            K.w("dve", dve)
            prim_b = idxf[:, 0:4].unsqueeze(1).to_broadcast([128, 4, 4])
            K.o("dve", lambda E: E.tensor_mul(pm3, ohm3, prim_b), [(dve, 1)])
            K.w("dve", dve)
            K.o("dve", lambda E: E.reduce_sum(bp4[:, :], pm3, axis=AX.X),
                [(dve, 1)])
            K.w("dve", dve)
            K.o("dve", lambda E: E.tensor_scalar(thrU[:, :], bs4[:, :], 0.5,
                                                 None, ALU.is_lt), [(dve, 1)])
            K.w("dve", dve)
            K.o("dve", lambda E: E.tensor_copy(rer8[:, 0:4], idxf[:, 0:4]),
                [(dve, 1)])
            K.w("dve", dve)
            K.o("dve", lambda E: E.select(rer8[:, 4:8], thrU[:, :],
                                          idxf[:, 4:8], bp4[:, :],
                                          add_drain=True), [(dve, 1)])
            if C_STEPS < 9:
                continue
            K.w("dve", dve)
            iotaE_b = iota32.unsqueeze(2).to_broadcast([128, 32, 8])
            rer_b = rer8.unsqueeze(1).to_broadcast([128, 32, 8])
            K.o("dve", lambda E: E.tensor_tensor(ohc3, iotaE_b, rer_b,
                                                 ALU.is_equal), [(dve, 1)])
            K.w("dve", dve)
            rw_b = rw8.unsqueeze(1).to_broadcast([128, 32, 8])
            K.o("dve", lambda E: E.tensor_mul(ohc23, ohc3, rw_b), [(dve, 1)])
            K.w("dve", dve)
            K.o("dve", lambda E: E.reduce_sum(combt[:, :], ohc23, axis=AX.X),
                [(dve, 1)])
            K.w("pe", dve, dve.n)
            K.o("pe", lambda E: E.transpose(pmm[1][0:32, 0:128], combt[:, :],
                                            ident[:, :]), [(pe, 1)])
            K.w("dve", pe, pe.n)
            col = 512 * tt + 128 * j
            K.o("dve", lambda E, c=col: E.tensor_copy(
                comb_sb[:, c:c + 128], pmm[1][0:32, 0:128]), [(dve, 1)])

    K.gate(ALLS)

    if phlim < 5:
        return _finish(nc, K)
    doff = 0
    ysb = vf(doff, [128, 16, 1024]); doff += 16 * 1024 * 4
    x1h = vb(doff, [128, 16, 1024]); doff += 16 * 1024 * 2
    ebase = doff
    hsb = vb(doff, [128, 6, 1024]); doff += 6 * 1024 * 2
    gub = [vb(doff + i * 2 * 16 * 128 * 2, [128, 2, 16 * 128]) for i in range(2)]
    doff += 2 * 2 * 16 * 128 * 2
    dnb = [vb(doff + i * 8 * 6 * 128 * 2, [128, 8 * 6 * 128]) for i in range(2)]
    doff += 2 * 8 * 6 * 128 * 2
    dexp_end = doff
    cwsb = vf(doff, [128, 1024]); doff += 1024 * 4
    hg = vf(doff, [128, 2, 512]); doff += 2 * 512 * 4
    hx = vf(doff, [128, 2, 512]); doff += 2 * 512 * 4
    assert doff <= AREN_B, (doff, AREN_B)
    ybd = vb(ebase, [128, 16, 512])
    sqd = vb(ebase + 16 * 512 * 2, [128, 16, 512])
    assert ebase + 2 * 16 * 512 * 2 <= dexp_end

    outr = out.rearrange("(dc p) t -> p dc t", p=128)

    gu_dg = {}
    gu_pe = {}
    dn_dg = {}
    dn_pe = {}
    dn_g = 0
    ydve = {}
    dgu = [dgu0, dgu1]
    ddn = [ddn0, ddn1]
    for hh in range(2):
        tt0, tt1 = 2 * hh, 2 * hh + 1
        K.w("sp", dst0, b_dst0)
        K.w("sp", dxb, b_dxb)
        K.w("sp", dsp, dsp.n)
        dma("sp", dsp, ysb[:, :, 0:512], x1fr[:, :, tt0, :])
        dma("sp", dsp, ysb[:, :, 512:1024], x1fr[:, :, tt1, :])
        dma("sp", dsp, x1h[:, :, 0:512], x1br[:, :, tt0, :])
        dma("sp", dsp, x1h[:, :, 512:1024], x1br[:, :, tt1, :])
        sp_ready = dsp.n

        prev_y_add = dve.n

        def emit_gu_dma(si, guard_val):
            e_, c_ = divmod(si, 6)
            if guard_val is not None:
                K.w("po", pe, guard_val)
            sgu = dgu[si % 2]
            K.w("po", sgu, sgu.n)
            for gi in range(2):
                fc = c_ + 6 * gi
                K.o("po", lambda E, e=e_, f=fc, b=si % 2, gi=gi: E.dma_start(
                    out=gub[b][:, gi, :].rearrange("p (dc j) -> p dc j", dc=16),
                    in_=gup[e, f]), [(sgu, 16)])
            gu_dg[(hh, si)] = (sgu, sgu.n)

        def emit_dn_dma(e_, dh, guard_val):
            if guard_val is not None:
                K.w("po", pe, guard_val)
            sdn = ddn[dh]
            K.w("po", sdn, sdn.n)
            K.o("po", lambda E, e=e_, d=dh: E.dma_start(
                out=dnb[d][:, :].rearrange("p (dc fc j) -> p dc fc j",
                                           dc=8, fc=6),
                in_=gdn[e][:, 8 * d:8 * d + 8]), [(sdn, 16)])
            dn_dg[(hh, e_, dh)] = (sdn, sdn.n)

        emit_gu_dma(0, None)
        emit_gu_dma(1, None)
        emit_dn_dma(0, 0, None)
        emit_dn_dma(0, 1, None)
        for e in range(NEXP):
            K.w("pe", dsp, sp_ready)
            if e >= 1:
                K.w("pe", po, po.n)
            cw0 = 1024 * hh
            K.o("pe", lambda E, c0=cw0, e=e: E.matmul(
                pA[:, 0:512], ehot_sb[:, 128 * e:128 * e + 128],
                comb_sb[:, c0:c0 + 512], start=True, stop=True))
            K.o("pe", lambda E, c0=cw0, e=e: E.matmul(
                pA[:, 512:1024], ehot_sb[:, 128 * e:128 * e + 128],
                comb_sb[:, c0 + 512:c0 + 1024], start=True, stop=True),
                [(pe, 1)])
            K.w("dve", pe, pe.n)
            K.o("dve", lambda E: E.tensor_copy(cwsb[:, 0:512], pA[:, 0:512]))
            K.o("dve", lambda E: E.tensor_copy(cwsb[:, 512:1024],
                                               pA[:, 512:1024]), [(dve, 1)])
            cw_done = dve.n
            for c in range(6):
                si = e * 6 + c
                K.w("pe", gu_dg[(hh, si)][0], gu_dg[(hh, si)][1])
                if (hh, e, c) != (0, 0, 0):
                    K.w("pe", ac, ac.n)
                for tt2 in range(2):
                    for kc in range(16):
                        K.o("pe", lambda E, b=tt2, k=kc, t2=tt2,
                            bu=c % 2: E.matmul(
                                pmm[b][:, :],
                                gub[bu][:, 0, :].rearrange(
                                    "p (dc j) -> p dc j", dc=16)[:, k, :],
                                x1h[:, k, 512 * t2:512 * t2 + 512],
                                start=(k == 0), stop=(k == 15)),
                            [(pe, 1)] if kc == 15 else ())
                g_pe = pe.n
                if (hh, e, c) != (0, 0, 0):
                    K.w("pe", dve, dve.n)
                for tt2 in range(2):
                    for kc in range(16):
                        K.o("pe", lambda E, b=2 + tt2, k=kc, t2=tt2,
                            bu=c % 2: E.matmul(
                                pmm[b][:, :],
                                gub[bu][:, 1, :].rearrange(
                                    "p (dc j) -> p dc j", dc=16)[:, k, :],
                                x1h[:, k, 512 * t2:512 * t2 + 512],
                                start=(k == 0), stop=(k == 15)),
                            [(pe, 1)] if kc == 15 else ())
                gu_pe[(hh, si)] = pe.n
                if si + 2 < NEXP * 6:
                    emit_gu_dma(si + 2, gu_pe[(hh, si)])
                K.w("act", pe, g_pe)
                K.w("act", dve, dve.n)
                for tt2 in range(2):
                    K.o("act", lambda E, t2=tt2: E.activation(
                        hg[:, t2, :], pmm[t2][:, :], AF.Silu),
                        [(ac, 1)] if tt2 == 1 else ())
                K.w("dve", ac, ac.n)
                K.w("dve", pe, pe.n)
                K.w("dve", po, po.n)
                for tt2 in range(2):
                    K.o("dve", lambda E, t2=tt2: E.tensor_mul(
                        hx[:, t2, :], hg[:, t2, :], pmm[2 + t2][:, :]),
                        [(dve, 1)] if tt2 == 1 else ())
                K.w("po", dve, dve.n)
                K.w("po", dve, cw_done)
                for tt2 in range(2):
                    K.o("po", lambda E, t2=tt2, c=c: E.tensor_mul(
                        hsb[:, c, 512 * t2:512 * t2 + 512], hx[:, t2, :],
                        cwsb[:, 512 * t2:512 * t2 + 512]),
                        [(po, 1)] if tt2 == 1 else ())
            h_done = po.n
            K.w("pe", po, h_done)
            K.w("dve", dve, prev_y_add)
            for dh in range(2):
                K.w("pe", dn_dg[(hh, e, dh)][0], dn_dg[(hh, e, dh)][1])
                for dcl in range(8):
                    dc = 8 * dh + dcl
                    for tt2 in range(2):
                        q = dn_g % 2
                        dtgt = pB[:, 512 * q:512 * q + 512]
                        if dn_g >= 2:
                            K.w("pe", dve, ydve[dn_g - 2])
                        for fc in range(6):
                            K.o("pe", lambda E, dt=dtgt, b=dh, dl=dcl, f=fc,
                                t2=tt2: E.matmul(
                                    dt,
                                    dnb[b][:, :].rearrange(
                                        "p (dc fc j) -> p dc fc j", dc=8,
                                        fc=6)[:, dl, f, :],
                                    hsb[:, f, 512 * t2:512 * t2 + 512],
                                    start=(f == 0), stop=(f == 5)),
                                [(pe, 1)] if fc == 5 else ())
                        K.w("dve", pe, pe.n)
                        K.o("dve", lambda E, dt=dtgt, d=dc, t2=tt2: E.tensor_add(
                            ysb[:, d, 512 * t2:512 * t2 + 512],
                            ysb[:, d, 512 * t2:512 * t2 + 512],
                            dt), [(dve, 1)])
                        ydve[dn_g] = dve.n
                        dn_g += 1
                dn_pe[(hh, e, dh)] = pe.n
                if e + 1 < NEXP:
                    emit_dn_dma(e + 1, dh, dn_pe[(hh, e, dh)])
            prev_y_add = dve.n

        for tt2 in range(2):
            yv = ysb[:, :, 512 * tt2:512 * tt2 + 512]
            K.w("dve", dve)
            K.w("act", dve, dve.n)
            K.w("pe", dve, dve.n)
            for s_ in (dgu0, dgu1, ddn0, ddn1):
                K.w("pe", s_, s_.n)
            emit_ln(yv, ybd, sqd)
            emit_ln_transform(yv)
            K.w("act", dve, dve.n)
            K.w("act", dst0, dst0.n)
            tcol = 1024 * hh + 512 * tt2
            dma("act", dst0, outr[:, :, tcol:tcol + 512], yv)
        K.gate(ALLS)

    return _finish(nc, K)


def _finish(nc, K):
    with nc.Block() as block:
        @block.tensor
        def _(tensor):
            for f in K.q["pe"]:
                f(tensor)

        @block.vector
        def _(vector):
            for f in K.q["dve"]:
                f(vector)

        @block.scalar
        def _(scalar):
            for f in K.q["act"]:
                f(scalar)

        @block.gpsimd
        def _(gpsimd):
            for f in K.q["po"]:
                f(gpsimd)

        @block.sync
        def _(sync):
            for f in K.q["sp"]:
                f(sync)
    return nc



def _host_prep(W_in, b_in, W_out, b_out, gate_w, gate_up, down, similarity):
    perm = np.concatenate([np.arange(0, ROT, 2), np.arange(1, ROT, 2),
                           np.arange(ROT, D)]).astype(np.int64)
    rows = np.concatenate([perm, D + perm, 2 * D + perm])
    W_in_p = np.asarray(W_in)[rows]
    b_in_p = np.asarray(b_in)[rows]
    win_t = np.ascontiguousarray(
        W_in_p.reshape(48, 128, 16, 128).transpose(0, 3, 2, 1))
    bin_a = np.ascontiguousarray(b_in_p.reshape(48, 128).T)
    WoutT_p = np.ascontiguousarray(np.asarray(W_out).T[perm])
    wout_t = np.ascontiguousarray(
        WoutT_p.reshape(16, 128, 16, 128).transpose(2, 1, 0, 3))
    bout_a = np.ascontiguousarray(np.asarray(b_out).reshape(16, 128).T)
    gw_t = np.ascontiguousarray(
        np.asarray(gate_w).T.reshape(16, 128, 32).transpose(1, 0, 2))
    gup_t = np.ascontiguousarray(
        np.asarray(gate_up).reshape(32, 12, 128, 16, 128)
        .transpose(0, 1, 4, 3, 2))
    gdn_t = np.ascontiguousarray(
        np.asarray(down).reshape(32, 16, 128, 6, 128)
        .transpose(0, 4, 1, 3, 2))
    sim_a = np.ascontiguousarray(np.asarray(similarity), dtype=np.float32)

    inv = (10000.0 ** (-np.arange(0, ROT, 2, dtype=np.float64) / ROT))
    ang = np.arange(4096, dtype=np.float64)[None, :] * inv[:, None]
    cos_t = np.cos(ang).astype(np.float32)
    sin_t = np.sin(ang).astype(np.float32)

    cstv = np.zeros((128, 200), dtype=np.float32)
    cstv[:, 0:128] = np.eye(128, dtype=np.float32)
    cstv[:, 128] = np.arange(128, dtype=np.float32)
    cstv[:, 129] = 1.0
    cstv[:, 130:162] = np.arange(32, dtype=np.float32)[None, :]
    cstv[:, 162:166] = np.arange(4, dtype=np.float32)[None, :]
    crowv = np.ones((1, 128), dtype=np.float32)
    eh = np.zeros((32, 4096), dtype=np.float32)
    for e in range(32):
        eh[e, 128 * e:128 * (e + 1)] = 1.0
    csel = np.zeros((32, 256), dtype=np.float32)
    for r in range(16):
        csel[:, 16 * r + r] = 1.0
    return dict(win=win_t, binp=bin_a, wout=wout_t, boutp=bout_a, gw=gw_t,
                gup=gup_t, gdn=gdn_t, simm=sim_a, cost=cos_t, sint=sin_t,
                cst=cstv, crow=crowv, ehot=eh, colsel=csel)


def make_in_maps(inputs):
    common = _host_prep(
        inputs["L0_attn_W_in"], inputs["L0_attn_b_in"], inputs["L0_attn_W_out"],
        inputs["L0_attn_b_out"], inputs["L0_ffn_gate_weight"],
        inputs["L0_ffn_gate_up_proj"], inputs["L0_ffn_down_proj"],
        inputs["L0_ffn_similarity"])
    X = np.asarray(inputs["L0_attn_X"])
    in_maps = []
    for core in range(8):
        b, h = core // 2, core % 2
        xt = np.ascontiguousarray(X[b].T)
        own = xt[:, 2048 * h:2048 * (h + 1)]
        par = xt[:, 2048 * (1 - h):2048 * (2 - h)]
        m = dict(common)
        m["xb"] = np.ascontiguousarray(np.concatenate([own, par], axis=1))
        for key in ("cost", "sint"):
            t = common[key]
            t_own = t[:, 2048 * h:2048 * (h + 1)]
            t_par = t[:, 2048 * (1 - h):2048 * (2 - h)]
            m[key] = np.ascontiguousarray(np.concatenate([t_own, t_par], axis=1))
        in_maps.append(m)
    return in_maps


def kernel(**inputs):
    in_maps = make_in_maps(inputs)
    nc = build_nc()
    res = run_bass_kernel_spmd(nc, in_maps, core_ids=list(range(8)))
    X = np.asarray(inputs["L0_attn_X"])
    out = np.empty_like(X)
    for core in range(8):
        b, h = core // 2, core % 2
        oc = np.asarray(res.results[core]["out"])
        out[b, 2048 * h:2048 * (h + 1), :] = oc.T
    return out


# version 08
# speedup vs baseline: 1.0953x; 1.0860x over previous
import numpy as np
import concourse.bass as bass
import concourse.mybir as mybir
from concourse.bass_utils import run_bass_kernel_spmd

F32 = mybir.dt.float32
BF16 = mybir.dt.bfloat16
U16 = mybir.dt.uint16
U8 = mybir.dt.uint8
AF = mybir.ActivationFunctionType
ALU = mybir.AluOpType
AX = mybir.AxisListType

C_STEPS = 99
D = 2048
NSLAB = 8
NOWN = 4
NEXP = 32
ROT = 1024


class Sem:
    def __init__(self, nc, name):
        self._cm = nc.semaphore(name)
        self.h = self._cm.__enter__()
        self.n = 0


class KB:
    ENGS = ("pe", "dve", "act", "po", "sp")

    def __init__(self, nc):
        self.nc = nc
        self.q = {e: [] for e in self.ENGS}

    def w(self, eng, sem, val=None):
        v = sem.n if val is None else val
        self.q[eng].append(lambda E: E.wait_ge(sem.h, v))

    def o(self, eng, fn, inc=()):
        inc = tuple(inc)
        for s, k in inc:
            s.n += k

        def run(E):
            ins = fn(E)
            for s, k in inc:
                ins.then_inc(s.h, k)
        self.q[eng].append(run)

    def gate(self, sems):
        vals = [(s, s.n) for s in sems]
        for e in self.ENGS:
            for s, v in vals:
                self.w(e, s, v)


def build_nc(phlim=9):
    nc = bass.Bass("TRN2", target_bir_lowering=False)

    xb = nc.dram_tensor("xb", [D, 4096], F32, kind="ExternalInput").ap()
    win = nc.dram_tensor("win", [48, 128, 16, 128], F32, kind="ExternalInput").ap()
    binp = nc.dram_tensor("binp", [128, 48], F32, kind="ExternalInput").ap()
    wout = nc.dram_tensor("wout", [16, 128, 16, 128], F32, kind="ExternalInput").ap()
    boutp = nc.dram_tensor("boutp", [128, 16], F32, kind="ExternalInput").ap()
    gw = nc.dram_tensor("gw", [128, 16, 32], F32, kind="ExternalInput").ap()
    gup = nc.dram_tensor("gup", [32, 12, 128, 16, 128], F32, kind="ExternalInput").ap()
    gdn = nc.dram_tensor("gdn", [32, 128, 16, 6, 128], F32, kind="ExternalInput").ap()
    simm = nc.dram_tensor("simm", [32, 32], F32, kind="ExternalInput").ap()
    cost = nc.dram_tensor("cost", [512, 4096], F32, kind="ExternalInput").ap()
    sint = nc.dram_tensor("sint", [512, 4096], F32, kind="ExternalInput").ap()
    cst = nc.dram_tensor("cst", [128, 200], F32, kind="ExternalInput").ap()
    crow = nc.dram_tensor("crow", [1, 128], F32, kind="ExternalInput").ap()
    ehot = nc.dram_tensor("ehot", [32, 4096], F32, kind="ExternalInput").ap()
    colsel = nc.dram_tensor("colsel", [32, 256], F32, kind="ExternalInput").ap()
    out = nc.dram_tensor("out", [D, 2048], F32, kind="ExternalOutput").ap()

    qd = nc.dram_tensor("qd", [16, NOWN, 128, 512], BF16).ap()
    kd = nc.dram_tensor("kd", [16, NSLAB, 128, 512], BF16).ap()
    vd = nc.dram_tensor("vd", [16, NSLAB, 128, 512], BF16).ap()
    x1f = nc.dram_tensor("x1f", [16, NOWN, 128, 512], F32).ap()
    x1bd = nc.dram_tensor("x1bd", [16, NOWN, 128, 512], BF16).ap()

    pmm = [nc.alloc_psum_tensor(f"pmm{i}", [128, 512], F32) for i in range(4)]
    pA = nc.alloc_psum_tensor("pA", [128, 1024], F32)
    pB = nc.alloc_psum_tensor("pB", [128, 1024], F32)

    alloc = nc.alloc_sbuf_tensor
    ident = alloc("ident", [128, 128], F32)
    onesrow = alloc("onesrow", [1, 128], F32)
    onescolb = alloc("onescolb", [128, 1], BF16)
    cst_sb = alloc("cst_sb", [128, 200], F32)
    bin_sb = alloc("bin_sb", [128, 48], F32)
    bout_sb = alloc("bout_sb", [128, 16], F32)
    gw_sb = alloc("gw_sb", [128, 16, 32], BF16)
    sim_f = alloc("sim_f", [32, 32], F32)
    sim_fx = alloc("sim_fx", [32, 32], F32)
    sim_b = alloc("sim_b", [32, 32], BF16)
    ones32 = alloc("ones32", [32, 32], F32)
    mask32 = alloc("mask32", [32, 32], U8)
    ehot_sb = alloc("ehot_sb", [32, 4096], BF16)
    colsel_sb = alloc("colsel_sb", [32, 256], BF16)
    A_acc = alloc("A_acc", [128, 16], F32)
    A_tmp = alloc("A_tmp", [128, 16], F32)
    comb_sb = alloc("comb_sb", [32, 2048], BF16)
    rowsc = alloc("rowsc", [1, 2048], F32)
    idxrow = alloc("idxrow", [1, 1024], F32)
    i8u = alloc("i8u", [128, 8], U16)
    thrU = alloc("thrU", [128, 4], U8)

    iotaP = cst_sb[:, 128:129]
    iota32 = cst_sb[:, 130:162]
    iota4 = cst_sb[:, 162:166]

    AREN_W = (nc.sbuf_bytes_remaining - 2048) // 4
    arena = alloc("arena", [128, AREN_W], F32)
    aren_f = arena[:, :]
    aren_b = arena[:, :].bitcast(BF16)

    def vf(off_b, shape):
        n = int(np.prod(shape[1:]))
        v = aren_f[:, off_b // 4: off_b // 4 + n]
        if len(shape) == 3:
            v = v.rearrange("p (a c) -> p a c", a=shape[1])
        return v

    def vb(off_b, shape):
        n = int(np.prod(shape[1:]))
        v = aren_b[:, off_b // 2: off_b // 2 + n]
        if len(shape) == 3:
            v = v.rearrange("p (a c) -> p a c", a=shape[1])
        return v

    AREN_B = AREN_W * 4

    K = KB(nc)
    dsp = Sem(nc, "dsp")
    dst0 = Sem(nc, "dst0")
    dst1 = Sem(nc, "dst1")
    dpre = Sem(nc, "dpre")
    dx = Sem(nc, "dx")
    dwa = Sem(nc, "dwa")
    dwb = Sem(nc, "dwb")
    dgu0 = Sem(nc, "dgu0")
    dgu1 = Sem(nc, "dgu1")
    ddn0 = Sem(nc, "ddn0")
    ddn1 = Sem(nc, "ddn1")
    dxb = Sem(nc, "dxb")
    dsx = Sem(nc, "dsx")
    pe = Sem(nc, "pe")
    dve = Sem(nc, "dve")
    ac = Sem(nc, "ac")
    po = Sem(nc, "po")
    ALLS = [dsp, dst0, dst1, dpre, dx, dwa, dwb, dgu0, dgu1, ddn0, ddn1,
            dxb, dsx, pe, dve, ac, po]

    def dma(eng, sem, outp, inp, slow=False):
        K.o(eng, lambda E, o=outp, i=inp, s=slow: E.dma_start(
            out=o, in_=i, allow_slow_non_contiguous=s), [(sem, 16)])

    dma("sp", dsp, ident[:, :], cst[:, 0:128])
    dma("sp", dsp, cst_sb[:, :], cst)
    dma("sp", dsp, onesrow[:, :], crow)
    dma("sp", dsp, bin_sb[:, :], binp)
    dma("sp", dsp, bout_sb[:, :], boutp)
    dma("sp", dsp, sim_f[:, :], simm)
    dma("po", dpre, onescolb[:, :], cst[:, 129:130], slow=True)
    dma("po", dpre, gw_sb[:, :, :], gw)
    dma("po", dpre, ehot_sb[:, :], ehot)
    dma("po", dpre, colsel_sb[:, :], colsel)
    pre_dsp, pre_dg = dsp.n, dpre.n

    K.w("dve", dsp, pre_dsp)
    K.o("dve", lambda E: E.memset(ones32[:, :], 1.0))
    K.o("dve", lambda E: E.tensor_scalar(mask32[:, :], ident[0:32, 0:32], 0.5,
                                         None, ALU.is_gt), [(dve, 1)])
    K.w("dve", dve)
    K.o("dve", lambda E: E.select(sim_fx[:, :], mask32[:, :], ones32[:, :],
                                  sim_f[:, :], add_drain=True), [(dve, 1)])
    K.w("dve", dve)
    K.o("dve", lambda E: E.tensor_copy(sim_b[:, :], sim_fx[:, :]), [(dve, 1)])
    K.o("dve", lambda E: E.memset(A_acc[:, :], 0.0), [(dve, 1)])

    aoff = 0
    xall = vb(aoff, [128, 16, 4096]); aoff += 16 * 4096 * 2
    winb = [vb(aoff + i * 16 * 128 * 2, [128, 16, 128]) for i in range(2)]
    aoff += 2 * 16 * 128 * 2
    evb = [vb(aoff + i * 512 * 2, [128, 512]) for i in range(2)]
    aoff += 2 * 512 * 2
    assert aoff <= AREN_B, (aoff, AREN_B)

    xbr = xb.rearrange("(dc p) t -> p dc t", p=128)
    for s in range(NSLAB):
        dma("po", dx, xall[:, :, 512 * s:512 * (s + 1)],
            xbr[:, :, 512 * s:512 * (s + 1)])

    plan = [(ec, list(range(NSLAB))) for ec in range(16, 48)] + \
           [(ec, list(range(NOWN))) for ec in range(16)]

    g = 0
    evac_dve_val = {}
    evstore_val = {}
    pe_after_iter = []
    dw = [dwa, dwb]
    dstp = [dst0, dst1]
    for i, (ec, slabs) in enumerate(plan):
        buf = i % 2
        if i >= 2:
            K.w("po", pe, pe_after_iter[i - 2])
        K.w("po", dw[buf], dw[buf].n)
        dma("po", dw[buf], winb[buf][:, :, :], win[ec])
        K.w("pe", dx, dx.n)
        K.w("pe", dw[buf], dw[buf].n)
        for s in slabs:
            bank = g % 4
            if g >= 4:
                K.w("pe", dve, evac_dve_val[g - 4])
            for kc in range(16):
                K.o("pe", lambda E, b=bank, bu=buf, k=kc, ss=s: E.matmul(
                    pmm[b][:, :], winb[bu][:, k, :],
                    xall[:, k, 512 * ss:512 * ss + 512],
                    start=(k == 0), stop=(k == 15)),
                    [(pe, 1)] if kc == 15 else ())
            K.w("dve", pe, pe.n)
            sb_ = dstp[g % 2]
            if g >= 2:
                K.w("dve", sb_, evstore_val[g - 2])
            eb = evb[g % 2]
            K.o("dve", lambda E, b=bank, e=ec, ebuf=eb: E.tensor_scalar(
                ebuf[:, :], pmm[b][:, :], bin_sb[:, e:e + 1], None, ALU.add),
                [(dve, 1)])
            evac_dve_val[g] = dve.n
            K.w("act", dve, dve.n)
            if ec < 16:
                tgt = qd[ec, s]
            elif ec < 32:
                tgt = kd[ec - 16, s]
            else:
                tgt = vd[ec - 32, s]
            dma("act", sb_, tgt, eb[:, :])
            evstore_val[g] = sb_.n
            g += 1
        pe_after_iter.append(pe.n)
    a1_dst0, a1_dst1 = dst0.n, dst1.n

    K.gate(ALLS)

    if phlim < 2:
        return _finish(nc, K)
    aoff = 0
    Kb = vb(aoff, [128, 16, 512]); aoff += 16 * 512 * 2
    Vb = vb(aoff, [128, 16, 512]); aoff += 16 * 512 * 2
    cosb = vf(aoff, [128, 4, 512]); aoff += 4 * 512 * 4
    sinb = vf(aoff, [128, 4, 512]); aoff += 4 * 512 * 4
    Kr = vf(aoff, [128, 8, 512]); aoff += 8 * 512 * 4
    rt1 = vf(aoff, [128, 4, 512]); aoff += 4 * 512 * 4
    rt2 = vf(aoff, [128, 4, 512]); aoff += 4 * 512 * 4
    sqb = vb(aoff, [128, 16, 512]); aoff += 16 * 512 * 2
    kvb = vf(aoff, [128, 16, 512]); aoff += 16 * 512 * 4
    kv2 = vf(aoff, [128, 16, 512]); aoff += 16 * 512 * 4
    assert aoff <= AREN_B, (aoff, AREN_B)

    kdr = kd.rearrange("ec s p t -> p ec s t")
    vdr = vd.rearrange("ec s p t -> p ec s t")
    costr = cost.rearrange("(cc p) t -> p cc t", p=128)
    sintr = sint.rearrange("(cc p) t -> p cc t", p=128)

    for s in range(NSLAB):
        K.w("sp", dst0, a1_dst0)
        K.w("sp", dst1, a1_dst1)
        K.w("sp", dsp, dsp.n)
        if s >= 1:
            K.w("sp", dve, dve.n)
            K.w("sp", ac, ac.n)
            K.w("sp", po, po.n)
        dma("sp", dsp, Kb[:, :, :], kdr[:, :, s, :])
        dma("sp", dsp, Vb[:, :, :], vdr[:, :, s, :])
        dma("sp", dsp, cosb[:, :, :], costr[:, :, 512 * s:512 * s + 512])
        dma("sp", dsp, sinb[:, :, :], sintr[:, :, 512 * s:512 * s + 512])
        K.w("dve", dsp, dsp.n)
        K.w("po", dsp, dsp.n)
        for c in range(4):
            K.o("dve", lambda E, c=c: E.tensor_mul(rt1[:, c, :], Kb[:, c, :],
                                                   cosb[:, c, :]))
            K.o("dve", lambda E, c=c: E.tensor_mul(rt2[:, c, :], Kb[:, c + 4, :],
                                                   sinb[:, c, :]),
                [(dve, 1)] if c == 3 else ())
        for c in range(4):
            K.o("po", lambda E, c=c: E.tensor_mul(kvb[:, c, :], Kb[:, c, :],
                                                  sinb[:, c, :]))
            K.o("po", lambda E, c=c: E.tensor_mul(kvb[:, c + 4, :], Kb[:, c + 4, :],
                                                  cosb[:, c, :]),
                [(po, 1)] if c == 3 else ())
        K.w("dve", dve)
        K.w("dve", po)
        for c in range(4):
            K.o("dve", lambda E, c=c: E.tensor_sub(Kr[:, c, :], rt1[:, c, :],
                                                   rt2[:, c, :]))
            K.o("dve", lambda E, c=c: E.tensor_add(Kr[:, c + 4, :], kvb[:, c + 4, :],
                                                   kvb[:, c, :]),
                [(dve, 1)] if c == 3 else ())
        K.w("act", dve, dve.n)
        for c in range(16):
            src = Kr[:, c, :] if c < 8 else Kb[:, c, :]
            K.o("act", lambda E, s_=src, c=c: E.activation(sqb[:, c, :], s_,
                                                           AF.Square),
                [(ac, 1)] if c == 15 else ())
        K.w("pe", ac, ac.n)
        K.w("pe", dve, dve.n)
        for c in range(16):
            K.o("pe", lambda E, c=c: E.matmul(pA[0:1, 0:512], onescolb[:, :],
                                              sqb[:, c, :], start=(c == 0),
                                              stop=(c == 15)),
                [(pe, 1)] if c == 15 else ())
        K.w("act", pe, pe.n)
        K.o("act", lambda E: E.activation(rowsc[0:1, 0:512], pA[0:1, 0:512],
                                          AF.Sqrt), [(ac, 1)])
        K.w("dve", ac, ac.n)
        K.o("dve", lambda E: E.tensor_scalar(rowsc[0:1, 512:1024],
                                             rowsc[0:1, 0:512], 1e-12, None,
                                             ALU.max), [(dve, 1)])
        K.w("dve", dve)
        K.o("dve", lambda E: E.reciprocal(rowsc[0:1, 0:512],
                                          rowsc[0:1, 512:1024]), [(dve, 1)])
        K.w("pe", dve, dve.n)
        K.o("pe", lambda E: E.matmul(pB[:, 0:512], onesrow[:, :],
                                     rowsc[0:1, 0:512], start=True, stop=True),
            [(pe, 1)])
        K.w("dve", pe, pe.n)
        K.w("dve", dve)
        for c in range(16):
            ksrc = Kr[:, c, :] if c < 8 else Kb[:, c, :]
            K.o("dve", lambda E, k=ksrc, c=c: E.tensor_mul(kvb[:, c, :], k,
                                                           Vb[:, c, :]),
                [(dve, 1)] if c == 15 else ())
        K.w("dve", dve)
        for c in range(16):
            K.o("dve", lambda E, c=c: E.tensor_mul(kv2[:, c, :], kvb[:, c, :],
                                                   pB[:, 0:512]),
                [(dve, 1)] if c == 15 else ())
        K.w("dve", dve)
        K.w("dve", po, po.n)
        for c in range(16):
            K.o("dve", lambda E, c=c: E.reduce_sum(A_tmp[:, c:c + 1],
                                                   kv2[:, c, :], axis=AX.X),
                [(dve, 1)] if c == 15 else ())
        K.w("po", dve, dve.n)
        K.o("po", lambda E: E.tensor_add(A_acc[:, :], A_acc[:, :], A_tmp[:, :]),
            [(po, 1)])

    K.gate(ALLS)

    if phlim < 3:
        return _finish(nc, K)
    boff = 0
    wob = [vb(boff + i * 16 * 128 * 2, [128, 16, 128]) for i in range(2)]
    boff += 2 * 16 * 128 * 2
    Qb = vb(boff, [128, 16, 512]); boff += 16 * 512 * 2
    Qr = vf(boff, [128, 8, 512]); boff += 8 * 512 * 4
    qa = vb(boff, [128, 16, 512]); boff += 16 * 512 * 2
    cosb2 = vf(boff, [128, 4, 512]); boff += 4 * 512 * 4
    sinb2 = vf(boff, [128, 4, 512]); boff += 4 * 512 * 4
    rtb1 = vf(boff, [128, 4, 512])
    sqb2 = vb(boff, [128, 16, 512])
    boff += 4 * 512 * 4
    rtb2 = vf(boff, [128, 4, 512]); boff += 4 * 512 * 4
    Xf = vf(boff, [128, 16, 512]); boff += 16 * 512 * 4
    out1 = vf(boff, [128, 16, 512]); boff += 16 * 512 * 4
    o1b = vb(boff, [128, 16, 512]); boff += 16 * 512 * 2
    assert boff <= AREN_B, (boff, AREN_B)

    qdr = qd.rearrange("ec s p t -> p ec s t")
    x1fr = x1f.rearrange("ec s p t -> p ec s t")
    x1br = x1bd.rearrange("ec s p t -> p ec s t")

    def emit_ln(src3, bconv, sqv, eps=1e-5):
        K.w("act", dve, dve.n)
        for c in range(16):
            K.o("act", lambda E, c=c: E.activation(sqv[:, c, :], src3[:, c, :],
                                                   AF.Square),
                [(ac, 1)] if c == 15 else ())
        K.w("dve", dve)
        for c in range(16):
            K.o("dve", lambda E, c=c: E.tensor_copy(bconv[:, c, :],
                                                    src3[:, c, :]),
                [(dve, 1)] if c == 15 else ())
        K.w("pe", ac, ac.n)
        K.w("pe", dve, dve.n)
        for c in range(16):
            K.o("pe", lambda E, c=c: E.matmul(pA[0:1, 0:512], onescolb[:, :],
                                              bconv[:, c, :], start=(c == 0),
                                              stop=(c == 15)),
                [(pe, 1)] if c == 15 else ())
        for c in range(16):
            K.o("pe", lambda E, c=c: E.matmul(pA[0:1, 512:1024], onescolb[:, :],
                                              sqv[:, c, :], start=(c == 0),
                                              stop=(c == 15)),
                [(pe, 1)] if c == 15 else ())
        K.w("dve", pe, pe.n)
        K.o("dve", lambda E: E.tensor_scalar(rowsc[0:1, 0:512], pA[0:1, 0:512],
                                             1.0 / D, None, ALU.mult), [(dve, 1)])
        K.w("act", dve, dve.n)
        K.o("act", lambda E: E.activation(rowsc[0:1, 512:1024], rowsc[0:1, 0:512],
                                          AF.Square), [(ac, 1)])
        K.w("dve", ac, ac.n)
        K.o("dve", lambda E: E.scalar_tensor_tensor(
            rowsc[0:1, 1024:1536], pA[0:1, 512:1024], 1.0 / D,
            rowsc[0:1, 512:1024], ALU.mult, ALU.subtract), [(dve, 1)])
        K.w("dve", dve)
        K.o("dve", lambda E: E.tensor_scalar(rowsc[0:1, 512:1024],
                                             rowsc[0:1, 1024:1536], eps, None,
                                             ALU.add), [(dve, 1)])
        K.w("act", dve, dve.n)
        K.o("act", lambda E: E.activation(rowsc[0:1, 1024:1536],
                                          rowsc[0:1, 512:1024], AF.Sqrt),
            [(ac, 1)])
        K.w("dve", ac, ac.n)
        K.o("dve", lambda E: E.reciprocal(rowsc[0:1, 1536:2048],
                                          rowsc[0:1, 1024:1536]), [(dve, 1)])
        K.w("pe", dve, dve.n)
        K.o("pe", lambda E: E.matmul(pB[:, 0:512], onesrow[:, :],
                                     rowsc[0:1, 0:512], start=True, stop=True))
        K.o("pe", lambda E: E.matmul(pB[:, 512:1024], onesrow[:, :],
                                     rowsc[0:1, 1536:2048], start=True,
                                     stop=True), [(pe, 1)])
        K.w("dve", pe, pe.n)
        for c in range(16):
            K.o("dve", lambda E, c=c: E.tensor_sub(src3[:, c, :], src3[:, c, :],
                                                   pB[:, 0:512]),
                [(dve, 1)] if c == 15 else ())
        K.w("dve", dve)
        for c in range(16):
            K.o("dve", lambda E, c=c: E.tensor_mul(src3[:, c, :], src3[:, c, :],
                                                   pB[:, 512:1024]),
                [(dve, 1)] if c == 15 else ())

    wout_g = 0
    woutev = {}
    wo_plan_pe = []
    for tt in range(NOWN):
        K.w("sp", dst0, a1_dst0)
        K.w("sp", dst1, a1_dst1)
        K.w("sp", dsp, dsp.n)
        if tt >= 1:
            K.w("sp", dve, dve.n)
            K.w("sp", ac, ac.n)
        dma("sp", dsp, Qb[:, :, :], qdr[:, :, tt, :])
        dma("sp", dsp, cosb2[:, :, :], costr[:, :, 512 * tt:512 * tt + 512])
        dma("sp", dsp, sinb2[:, :, :], sintr[:, :, 512 * tt:512 * tt + 512])
        dma("sp", dsp, Xf[:, :, :], xbr[:, :, 512 * tt:512 * tt + 512])
        K.w("dve", dsp, dsp.n)
        K.w("dve", ac, ac.n)
        K.w("dve", dst0, dst0.n)
        K.w("dve", dxb, dxb.n)
        for c in range(4):
            K.o("dve", lambda E, c=c: E.tensor_mul(rtb1[:, c, :], Qb[:, c, :],
                                                   cosb2[:, c, :]))
            K.o("dve", lambda E, c=c: E.tensor_mul(rtb2[:, c, :], Qb[:, c + 4, :],
                                                   sinb2[:, c, :]),
                [(dve, 1)] if c == 3 else ())
        K.w("dve", dve)
        for c in range(4):
            K.o("dve", lambda E, c=c: E.tensor_sub(Qr[:, c, :], rtb1[:, c, :],
                                                   rtb2[:, c, :]),
                [(dve, 1)] if c == 3 else ())
        K.w("dve", dve)
        for c in range(4):
            K.o("dve", lambda E, c=c: E.tensor_mul(rtb1[:, c, :], Qb[:, c, :],
                                                   sinb2[:, c, :]))
            K.o("dve", lambda E, c=c: E.tensor_mul(rtb2[:, c, :], Qb[:, c + 4, :],
                                                   cosb2[:, c, :]),
                [(dve, 1)] if c == 3 else ())
        K.w("dve", dve)
        for c in range(4):
            K.o("dve", lambda E, c=c: E.tensor_add(Qr[:, c + 4, :], rtb2[:, c, :],
                                                   rtb1[:, c, :]),
                [(dve, 1)] if c == 3 else ())
        K.w("dve", dve)
        for c in range(16):
            src = Qr[:, c, :] if c < 8 else Qb[:, c, :]
            K.o("dve", lambda E, s_=src, c=c: E.tensor_scalar(
                qa[:, c, :], s_, A_acc[:, c:c + 1], None, ALU.mult),
                [(dve, 1)] if c == 15 else ())
        qa_done = dve.n
        for ec in range(16):
            wbuf = ec % 2
            widx = tt * 16 + ec
            if widx >= 2:
                K.w("po", pe, wo_plan_pe[widx - 2])
            K.w("po", dw[wbuf], dw[wbuf].n)
            dma("po", dw[wbuf], wob[wbuf][:, :, :], wout[ec])
            bank = wout_g % 4
            K.w("pe", dw[wbuf], dw[wbuf].n)
            K.w("pe", dve, qa_done)
            if wout_g >= 4:
                K.w("pe", dve, woutev[wout_g - 4])
            for kc in range(16):
                K.o("pe", lambda E, b=bank, wb=wbuf, k=kc: E.matmul(
                    pmm[b][:, :], wob[wb][:, k, :], qa[:, k, :],
                    start=(k == 0), stop=(k == 15)),
                    [(pe, 1)] if kc == 15 else ())
            wo_plan_pe.append(pe.n)
            K.w("dve", pe, pe.n)
            K.o("dve", lambda E, b=bank, e=ec: E.scalar_tensor_tensor(
                out1[:, e, :], pmm[b][:, :], bout_sb[:, e:e + 1], Xf[:, e, :],
                ALU.add, ALU.add), [(dve, 1)])
            K.w("dve", dve)
            K.o("dve", lambda E, e=ec: E.tensor_add(out1[:, e, :],
                                                    out1[:, e, :], Xf[:, e, :]),
                [(dve, 1)])
            woutev[wout_g] = dve.n
            wout_g += 1
        emit_ln(out1, o1b, sqb2)
        K.w("act", dve, dve.n)
        K.w("act", dst0, dst0.n)
        dma("act", dst0, x1fr[:, :, tt, :], out1[:, :, :])
        K.w("po", dve, dve.n)
        K.w("po", dxb, dxb.n)
        dma("po", dxb, x1br[:, :, tt, :], out1[:, :, :])

    b_dst0 = dst0.n
    b_dxb = dxb.n
    K.gate(ALLS)

    if phlim < 4:
        return _finish(nc, K)
    doff = 0
    ysb = vf(doff, [128, 16, 1024]); doff += 16 * 1024 * 4
    x1h = vb(doff, [128, 16, 1024]); doff += 16 * 1024 * 2
    ebase = doff
    hsb = vb(doff, [128, 6, 1024]); doff += 6 * 1024 * 2
    gub = [vb(doff + i * 2 * 16 * 128 * 2, [128, 2, 16 * 128]) for i in range(2)]
    doff += 2 * 2 * 16 * 128 * 2
    dnb = [vb(doff + i * 8 * 6 * 128 * 2, [128, 8 * 6 * 128]) for i in range(2)]
    doff += 2 * 8 * 6 * 128 * 2
    dexp_end = doff
    cwsb = vf(doff, [128, 1024]); doff += 1024 * 4
    hg = vf(doff, [128, 2, 512]); doff += 2 * 512 * 4
    hx = vf(doff, [128, 2, 512]); doff += 2 * 512 * 4
    assert doff <= AREN_B, (doff, AREN_B)
    ybd = vb(ebase, [128, 16, 512])
    sqd = vb(ebase + 16 * 512 * 2, [128, 16, 512])
    assert ebase + 2 * 16 * 512 * 2 <= dexp_end

    outr = out.rearrange("(dc p) t -> p dc t", p=128)
    gu_dg = {}
    gu_pe = {}
    dn_dg = {}
    dn_pe = {}
    dn_g = 0
    ydve = {}
    dgu = [dgu0, dgu1]
    ddn = [ddn0, ddn1]

    def emit_gu_dma(hh_, si, guard_val):
        e_, c_ = divmod(si, 6)
        if guard_val is not None:
            K.w("po", pe, guard_val)
        sgu = dgu[si % 2]
        K.w("po", sgu, sgu.n)
        for gi in range(2):
            fc = c_ + 6 * gi
            K.o("po", lambda E, e=e_, f=fc, b=si % 2, gi=gi: E.dma_start(
                out=gub[b][:, gi, :].rearrange("p (dc j) -> p dc j", dc=16),
                in_=gup[e, f]), [(sgu, 16)])
        gu_dg[(hh_, si)] = (sgu, sgu.n)

    def emit_dn_dma(hh_, e_, dh, guard_val):
        if guard_val is not None:
            K.w("po", pe, guard_val)
        sdn = ddn[dh]
        K.w("po", sdn, sdn.n)
        K.o("po", lambda E, e=e_, d=dh: E.dma_start(
            out=dnb[d][:, :].rearrange("p (dc fc j) -> p dc fc j",
                                       dc=8, fc=6),
            in_=gdn[e][:, 8 * d:8 * d + 8]), [(sdn, 16)])
        dn_dg[(hh_, e_, dh)] = (sdn, sdn.n)

    def emit_x1h_loads(hh_):
        tt0_, tt1_ = 2 * hh_, 2 * hh_ + 1
        K.w("sp", dst0, b_dst0)
        K.w("sp", dxb, b_dxb)
        K.w("sp", dsx, dsx.n)
        dma("sp", dsx, x1h[:, :, 0:512], x1br[:, :, tt0_, :])
        dma("sp", dsx, x1h[:, :, 512:1024], x1br[:, :, tt1_, :])
        return dsx.n

    def emit_y_loads(hh_):
        tt0_, tt1_ = 2 * hh_, 2 * hh_ + 1
        K.w("sp", dst0, b_dst0)
        K.w("sp", dsp, dsp.n)
        dma("sp", dsp, ysb[:, :, 0:512], x1fr[:, :, tt0_, :])
        dma("sp", dsp, ysb[:, :, 512:1024], x1fr[:, :, tt1_, :])
        return dsp.n

    pre0 = emit_x1h_loads(0)
    emit_gu_dma(0, 0, None)
    emit_gu_dma(0, 1, None)
    emit_dn_dma(0, 0, 0, None)
    emit_dn_dma(0, 0, 1, None)

    _co = [0]

    def cvf(shape):
        v = vf(_co[0], shape)
        n = int(np.prod(shape[1:])) * 4
        _co[0] += (n + 63) & ~63
        return v

    def cvb(shape):
        v = vb(_co[0], shape)
        n = int(np.prod(shape[1:])) * 2
        _co[0] += (n + 63) & ~63
        return v

    x1t = cvb([128, 16, 512])
    lgf = cvf([128, 512])
    ltok = cvf([128, 32])
    v8 = cvf([128, 8])
    idxf = cvf([128, 8])
    nv0 = cvf([128, 1])
    ev8 = cvf([128, 8])
    s8 = cvf([128, 1])
    r8 = cvf([128, 1])
    rw8 = cvf([128, 8])
    oh = cvb([128, 8 * 128])
    Mt = cvb([128, 16 * 128])
    scf = cvf([128, 128])
    sct = cvf([128, 16])
    bs4 = cvf([128, 4])
    cand = cvf([128, 16])
    am4 = cvf([128, 4])
    ohm = cvf([128, 16])
    pm16 = cvf([128, 16])
    bp4 = cvf([128, 4])
    rer8 = cvf([128, 8])
    ohc = cvf([128, 32 * 8])
    ohc2 = cvf([128, 32 * 8])
    combt = cvf([128, 32])
    assert _co[0] <= AREN_B, (_co[0], AREN_B)

    ohv = oh.rearrange("p (k t) -> p k t", k=8)[0:32, :, :]
    mtv = Mt.rearrange("p (r t) -> p r t", r=16)[0:32, :, :]
    sct3 = sct.rearrange("p (j i) -> p j i", i=4)
    cand3 = cand.rearrange("p (j i) -> p j i", i=4)
    ohm3 = ohm.rearrange("p (j i) -> p j i", i=4)
    pm3 = pm16.rearrange("p (j i) -> p j i", i=4)
    ohc3 = ohc.rearrange("p (e k) -> p e k", k=8)
    ohc23 = ohc2.rearrange("p (e k) -> p e k", k=8)

    for tt in range(NOWN):
        K.w("sp", dxb, b_dxb)
        K.w("sp", dsp, dsp.n)
        if tt >= 1:
            K.w("sp", pe, pe.n)
        dma("sp", dsp, x1t[:, :, :], x1br[:, :, tt, :])
        K.w("pe", dsp, dsp.n)
        K.w("pe", dpre, pre_dg)
        if tt >= 1:
            K.w("pe", dve, dve.n)
        for kc in range(16):
            K.o("pe", lambda E, k=kc: E.matmul(pmm[0][0:32, :], gw_sb[:, k, :],
                                               x1t[:, k, :], start=(k == 0),
                                               stop=(k == 15)),
                [(pe, 1)] if kc == 15 else ())
        K.w("dve", pe, pe.n)
        K.o("dve", lambda E: E.tensor_copy(lgf[0:32, :], pmm[0][0:32, :]),
            [(dve, 1)])
        for j in range(4):
            if C_STEPS < 2:
                continue
            K.w("pe", dve, dve.n)
            K.o("pe", lambda E, j=j: E.transpose(
                pmm[1][:, 0:32], lgf[0:32, 128 * j:128 * j + 128],
                ident[0:32, 0:32]), [(pe, 1)])
            K.w("dve", pe, pe.n)
            K.o("dve", lambda E: E.tensor_copy(ltok[:, :], pmm[1][:, 0:32]),
                [(dve, 1)])
            if C_STEPS < 3:
                continue
            K.w("dve", dve)
            K.o("dve", lambda E: E.max(out=v8[:, :], in_=ltok[:, :]), [(dve, 1)])
            K.w("dve", dve)
            K.o("dve", lambda E: E.max_index(i8u[:, :], v8[:, :], ltok[:, :]),
                [(dve, 1)])
            K.w("dve", dve)
            K.o("dve", lambda E: E.tensor_copy(idxf[:, :], i8u[:, :]))
            K.o("dve", lambda E: E.tensor_scalar(nv0[:, :], v8[:, 0:1], -1.0,
                                                 None, ALU.mult), [(dve, 1)])
            if C_STEPS < 4:
                continue
            K.w("act", dve, dve.n)
            import os as _os
            if _os.environ.get("CVAR") == "sq":
                K.o("act", lambda E: E.activation(ev8[:, :], v8[:, :],
                                                  AF.Square), [(ac, 1)])
            elif _os.environ.get("CVAR") == "nobias":
                K.o("act", lambda E: E.activation(ev8[:, :], v8[:, :],
                                                  AF.Exp), [(ac, 1)])
            else:
                K.o("act", lambda E: E.activation(ev8[:, :], v8[:, :], AF.Exp,
                                                  bias=nv0[:, :]), [(ac, 1)])
            K.w("dve", ac, ac.n)
            K.o("dve", lambda E: E.reduce_sum(s8[:, :], ev8[:, :], axis=AX.X),
                [(dve, 1)])
            K.w("dve", dve)
            K.o("dve", lambda E: E.reciprocal(r8[:, :], s8[:, :]), [(dve, 1)])
            K.w("dve", dve)
            K.o("dve", lambda E: E.tensor_scalar(rw8[:, :], ev8[:, :], r8[:, :],
                                                 None, ALU.mult), [(dve, 1)])
            if C_STEPS < 5:
                continue
            K.w("pe", dve, dve.n)
            for k in range(8):
                bank = k % 4
                if k >= 4:
                    K.w("pe", dve, dve.n)
                K.o("pe", lambda E, k=k, b=bank: E.transpose(
                    pmm[b][0:1, 0:128], idxf[:, k:k + 1],
                    ident[:, :]), [(pe, 1)])
                K.w("dve", pe, pe.n)
                K.o("dve", lambda E, k=k, b=bank: E.tensor_copy(
                    idxrow[0:1, 128 * k:128 * k + 128],
                    pmm[b][0:1, 0:128]), [(dve, 1)])
            if C_STEPS < 6:
                continue
            K.w("pe", dve, dve.n)
            K.o("pe", lambda E: E.matmul(pA[0:32, 0:512], onesrow[0:1, 0:32],
                                         idxrow[0:1, 0:512], start=True,
                                         stop=True))
            K.o("pe", lambda E: E.matmul(pA[0:32, 512:1024], onesrow[0:1, 0:32],
                                         idxrow[0:1, 512:1024], start=True,
                                         stop=True), [(pe, 1)])
            K.w("dve", pe, pe.n)
            K.o("dve", lambda E: E.tensor_scalar(
                ohv, pA[0:32, 0:1024].rearrange("p (k t) -> p k t", k=8),
                iotaP[0:32, :], None, ALU.is_equal), [(dve, 1)])
            if C_STEPS < 7:
                continue
            K.w("pe", dve, dve.n)
            for j2 in range(4):
                K.o("pe", lambda E, j2=j2: E.matmul(
                    pB[0:32, 128 * j2:128 * j2 + 128], sim_b[:, :],
                    ohv[:, 4 + j2, :], start=True, stop=True),
                    [(pe, 1)] if j2 == 3 else ())
            K.w("dve", pe, pe.n)
            for j2 in range(4):
                for i2 in range(4):
                    K.o("dve", lambda E, j2=j2, i2=i2: E.tensor_mul(
                        mtv[:, 4 * j2 + i2, :], ohv[:, i2, :],
                        pB[0:32, 128 * j2:128 * j2 + 128]),
                        [(dve, 1)] if (j2 == 3 and i2 == 3) else ())
            K.w("pe", dve, dve.n)
            for r in range(16):
                K.o("pe", lambda E, r=r: E.matmul(
                    pmm[3][0:16, 0:128], colsel_sb[:, 16 * r:16 * r + 16],
                    mtv[:, r, :], start=(r == 0), stop=(r == 15)),
                    [(pe, 1)] if r == 15 else ())
            K.w("dve", pe, pe.n)
            K.o("dve", lambda E: E.tensor_copy(scf[0:16, :], pmm[3][0:16, 0:128]),
                [(dve, 1)])
            K.w("pe", dve, dve.n)
            K.o("pe", lambda E: E.transpose(pmm[2][:, 0:16], scf[0:16, :],
                                            ident[0:16, 0:16]), [(pe, 1)])
            K.w("dve", pe, pe.n)
            K.o("dve", lambda E: E.tensor_copy(sct[:, :], pmm[2][:, 0:16]),
                [(dve, 1)])
            if C_STEPS < 8:
                continue
            K.w("dve", dve)
            K.o("dve", lambda E: E.reduce_max(bs4[:, :], sct3, axis=AX.X),
                [(dve, 1)])
            K.w("dve", dve)
            bs_b = bs4.unsqueeze(2).to_broadcast([128, 4, 4])
            K.o("dve", lambda E: E.tensor_tensor(cand3, sct3, bs_b, ALU.is_lt),
                [(dve, 1)])
            K.w("dve", dve)
            iota4_b = iota4.unsqueeze(1).to_broadcast([128, 4, 4])
            K.o("dve", lambda E: E.scalar_tensor_tensor(
                cand3, cand3, 8.0, iota4_b, ALU.mult, ALU.add), [(dve, 1)])
            K.w("dve", dve)
            K.o("dve", lambda E: E.tensor_reduce(am4[:, :], cand3, AX.X,
                                                 ALU.min), [(dve, 1)])
            K.w("dve", dve)
            am_b = am4.unsqueeze(2).to_broadcast([128, 4, 4])
            K.o("dve", lambda E: E.tensor_tensor(ohm3, am_b, iota4_b,
                                                 ALU.is_equal), [(dve, 1)])
            K.w("dve", dve)
            prim_b = idxf[:, 0:4].unsqueeze(1).to_broadcast([128, 4, 4])
            K.o("dve", lambda E: E.tensor_mul(pm3, ohm3, prim_b), [(dve, 1)])
            K.w("dve", dve)
            K.o("dve", lambda E: E.reduce_sum(bp4[:, :], pm3, axis=AX.X),
                [(dve, 1)])
            K.w("dve", dve)
            K.o("dve", lambda E: E.tensor_scalar(thrU[:, :], bs4[:, :], 0.5,
                                                 None, ALU.is_lt), [(dve, 1)])
            K.w("dve", dve)
            K.o("dve", lambda E: E.tensor_copy(rer8[:, 0:4], idxf[:, 0:4]),
                [(dve, 1)])
            K.w("dve", dve)
            K.o("dve", lambda E: E.select(rer8[:, 4:8], thrU[:, :],
                                          idxf[:, 4:8], bp4[:, :],
                                          add_drain=True), [(dve, 1)])
            if C_STEPS < 9:
                continue
            K.w("dve", dve)
            iotaE_b = iota32.unsqueeze(2).to_broadcast([128, 32, 8])
            rer_b = rer8.unsqueeze(1).to_broadcast([128, 32, 8])
            K.o("dve", lambda E: E.tensor_tensor(ohc3, iotaE_b, rer_b,
                                                 ALU.is_equal), [(dve, 1)])
            K.w("dve", dve)
            rw_b = rw8.unsqueeze(1).to_broadcast([128, 32, 8])
            K.o("dve", lambda E: E.tensor_mul(ohc23, ohc3, rw_b), [(dve, 1)])
            K.w("dve", dve)
            K.o("dve", lambda E: E.reduce_sum(combt[:, :], ohc23, axis=AX.X),
                [(dve, 1)])
            K.w("pe", dve, dve.n)
            K.o("pe", lambda E: E.transpose(pmm[1][0:32, 0:128], combt[:, :],
                                            ident[:, :]), [(pe, 1)])
            K.w("dve", pe, pe.n)
            col = 512 * tt + 128 * j
            K.o("dve", lambda E, c=col: E.tensor_copy(
                comb_sb[:, c:c + 128], pmm[1][0:32, 0:128]), [(dve, 1)])

    K.gate(ALLS)

    if phlim < 5:
        return _finish(nc, K)

    for hh in range(2):
        tt0, tt1 = 2 * hh, 2 * hh + 1
        if hh == 0:
            x1_ready = pre0
        else:
            x1_ready = emit_x1h_loads(hh)
            emit_gu_dma(hh, 0, None)
            emit_gu_dma(hh, 1, None)
            emit_dn_dma(hh, 0, 0, None)
            emit_dn_dma(hh, 0, 1, None)
        y_ready = emit_y_loads(hh)

        prev_y_add = dve.n
        for e in range(NEXP):
            K.w("pe", dsx, x1_ready)
            if e >= 1:
                K.w("pe", po, po.n)
            cw0 = 1024 * hh
            K.o("pe", lambda E, c0=cw0, e=e: E.matmul(
                pA[:, 0:512], ehot_sb[:, 128 * e:128 * e + 128],
                comb_sb[:, c0:c0 + 512], start=True, stop=True))
            K.o("pe", lambda E, c0=cw0, e=e: E.matmul(
                pA[:, 512:1024], ehot_sb[:, 128 * e:128 * e + 128],
                comb_sb[:, c0 + 512:c0 + 1024], start=True, stop=True),
                [(pe, 1)])
            K.w("dve", pe, pe.n)
            K.o("dve", lambda E: E.tensor_copy(cwsb[:, 0:512], pA[:, 0:512]))
            K.o("dve", lambda E: E.tensor_copy(cwsb[:, 512:1024],
                                               pA[:, 512:1024]), [(dve, 1)])
            cw_done = dve.n
            for c in range(6):
                si = e * 6 + c
                K.w("pe", gu_dg[(hh, si)][0], gu_dg[(hh, si)][1])
                if (hh, e, c) != (0, 0, 0):
                    K.w("pe", ac, ac.n)
                for tt2 in range(2):
                    for kc in range(16):
                        K.o("pe", lambda E, b=tt2, k=kc, t2=tt2,
                            bu=c % 2: E.matmul(
                                pmm[b][:, :],
                                gub[bu][:, 0, :].rearrange(
                                    "p (dc j) -> p dc j", dc=16)[:, k, :],
                                x1h[:, k, 512 * t2:512 * t2 + 512],
                                start=(k == 0), stop=(k == 15)),
                            [(pe, 1)] if kc == 15 else ())
                g_pe = pe.n
                if (hh, e, c) != (0, 0, 0):
                    K.w("pe", dve, dve.n)
                for tt2 in range(2):
                    for kc in range(16):
                        K.o("pe", lambda E, b=2 + tt2, k=kc, t2=tt2,
                            bu=c % 2: E.matmul(
                                pmm[b][:, :],
                                gub[bu][:, 1, :].rearrange(
                                    "p (dc j) -> p dc j", dc=16)[:, k, :],
                                x1h[:, k, 512 * t2:512 * t2 + 512],
                                start=(k == 0), stop=(k == 15)),
                            [(pe, 1)] if kc == 15 else ())
                gu_pe[(hh, si)] = pe.n
                if si + 2 < NEXP * 6:
                    emit_gu_dma(hh, si + 2, gu_pe[(hh, si)])
                K.w("act", pe, g_pe)
                K.w("act", dve, dve.n)
                for tt2 in range(2):
                    K.o("act", lambda E, t2=tt2: E.activation(
                        hg[:, t2, :], pmm[t2][:, :], AF.Silu),
                        [(ac, 1)] if tt2 == 1 else ())
                K.w("dve", ac, ac.n)
                K.w("dve", pe, pe.n)
                K.w("dve", po, po.n)
                for tt2 in range(2):
                    K.o("dve", lambda E, t2=tt2: E.tensor_mul(
                        hx[:, t2, :], hg[:, t2, :], pmm[2 + t2][:, :]),
                        [(dve, 1)] if tt2 == 1 else ())
                K.w("po", dve, dve.n)
                K.w("po", dve, cw_done)
                for tt2 in range(2):
                    K.o("po", lambda E, t2=tt2, c=c: E.tensor_mul(
                        hsb[:, c, 512 * t2:512 * t2 + 512], hx[:, t2, :],
                        cwsb[:, 512 * t2:512 * t2 + 512]),
                        [(po, 1)] if tt2 == 1 else ())
            h_done = po.n
            K.w("pe", po, h_done)
            K.w("dve", dve, prev_y_add)
            if e == 0:
                K.w("dve", dsp, y_ready)
            for dh in range(2):
                K.w("pe", dn_dg[(hh, e, dh)][0], dn_dg[(hh, e, dh)][1])
                for dcl in range(8):
                    dc = 8 * dh + dcl
                    for tt2 in range(2):
                        q = dn_g % 2
                        dtgt = pB[:, 512 * q:512 * q + 512]
                        if dn_g >= 2:
                            K.w("pe", dve, ydve[dn_g - 2])
                        for fc in range(6):
                            K.o("pe", lambda E, dt=dtgt, b=dh, dl=dcl, f=fc,
                                t2=tt2: E.matmul(
                                    dt,
                                    dnb[b][:, :].rearrange(
                                        "p (dc fc j) -> p dc fc j", dc=8,
                                        fc=6)[:, dl, f, :],
                                    hsb[:, f, 512 * t2:512 * t2 + 512],
                                    start=(f == 0), stop=(f == 5)),
                                [(pe, 1)] if fc == 5 else ())
                        K.w("dve", pe, pe.n)
                        K.o("dve", lambda E, dt=dtgt, d=dc, t2=tt2: E.tensor_add(
                            ysb[:, d, 512 * t2:512 * t2 + 512],
                            ysb[:, d, 512 * t2:512 * t2 + 512],
                            dt), [(dve, 1)])
                        ydve[dn_g] = dve.n
                        dn_g += 1
                dn_pe[(hh, e, dh)] = pe.n
                if e + 1 < NEXP:
                    emit_dn_dma(hh, e + 1, dh, dn_pe[(hh, e, dh)])
            prev_y_add = dve.n

        for tt2 in range(2):
            yv = ysb[:, :, 512 * tt2:512 * tt2 + 512]
            K.w("dve", dve)
            K.w("act", dve, dve.n)
            K.w("pe", dve, dve.n)
            for s_ in (dgu0, dgu1, ddn0, ddn1):
                K.w("pe", s_, s_.n)
            emit_ln(yv, ybd, sqd)
            K.w("act", dve, dve.n)
            K.w("act", dst0, dst0.n)
            tcol = 1024 * hh + 512 * tt2
            dma("act", dst0, outr[:, :, tcol:tcol + 512], yv)
        K.gate(ALLS)

    return _finish(nc, K)


def _finish(nc, K):
    with nc.Block() as block:
        @block.tensor
        def _(tensor):
            for f in K.q["pe"]:
                f(tensor)

        @block.vector
        def _(vector):
            for f in K.q["dve"]:
                f(vector)

        @block.scalar
        def _(scalar):
            for f in K.q["act"]:
                f(scalar)

        @block.gpsimd
        def _(gpsimd):
            for f in K.q["po"]:
                f(gpsimd)

        @block.sync
        def _(sync):
            for f in K.q["sp"]:
                f(sync)
    return nc



def _host_prep(W_in, b_in, W_out, b_out, gate_w, gate_up, down, similarity):
    perm = np.concatenate([np.arange(0, ROT, 2), np.arange(1, ROT, 2),
                           np.arange(ROT, D)]).astype(np.int64)
    rows = np.concatenate([perm, D + perm, 2 * D + perm])
    W_in_p = np.asarray(W_in)[rows]
    b_in_p = np.asarray(b_in)[rows]
    win_t = np.ascontiguousarray(
        W_in_p.reshape(48, 128, 16, 128).transpose(0, 3, 2, 1))
    bin_a = np.ascontiguousarray(b_in_p.reshape(48, 128).T)
    WoutT_p = np.ascontiguousarray(np.asarray(W_out).T[perm])
    wout_t = np.ascontiguousarray(
        WoutT_p.reshape(16, 128, 16, 128).transpose(2, 1, 0, 3))
    bout_a = np.ascontiguousarray(np.asarray(b_out).reshape(16, 128).T)
    gw_t = np.ascontiguousarray(
        np.asarray(gate_w).T.reshape(16, 128, 32).transpose(1, 0, 2))
    gup_t = np.ascontiguousarray(
        np.asarray(gate_up).reshape(32, 12, 128, 16, 128)
        .transpose(0, 1, 4, 3, 2))
    gdn_t = np.ascontiguousarray(
        np.asarray(down).reshape(32, 16, 128, 6, 128)
        .transpose(0, 4, 1, 3, 2))
    sim_a = np.ascontiguousarray(np.asarray(similarity), dtype=np.float32)

    inv = (10000.0 ** (-np.arange(0, ROT, 2, dtype=np.float64) / ROT))
    ang = np.arange(4096, dtype=np.float64)[None, :] * inv[:, None]
    cos_t = np.cos(ang).astype(np.float32)
    sin_t = np.sin(ang).astype(np.float32)

    cstv = np.zeros((128, 200), dtype=np.float32)
    cstv[:, 0:128] = np.eye(128, dtype=np.float32)
    cstv[:, 128] = np.arange(128, dtype=np.float32)
    cstv[:, 129] = 1.0
    cstv[:, 130:162] = np.arange(32, dtype=np.float32)[None, :]
    cstv[:, 162:166] = np.arange(4, dtype=np.float32)[None, :]
    crowv = np.ones((1, 128), dtype=np.float32)
    eh = np.zeros((32, 4096), dtype=np.float32)
    for e in range(32):
        eh[e, 128 * e:128 * (e + 1)] = 1.0
    csel = np.zeros((32, 256), dtype=np.float32)
    for r in range(16):
        csel[:, 16 * r + r] = 1.0
    return dict(win=win_t, binp=bin_a, wout=wout_t, boutp=bout_a, gw=gw_t,
                gup=gup_t, gdn=gdn_t, simm=sim_a, cost=cos_t, sint=sin_t,
                cst=cstv, crow=crowv, ehot=eh, colsel=csel)


def make_in_maps(inputs):
    common = _host_prep(
        inputs["L0_attn_W_in"], inputs["L0_attn_b_in"], inputs["L0_attn_W_out"],
        inputs["L0_attn_b_out"], inputs["L0_ffn_gate_weight"],
        inputs["L0_ffn_gate_up_proj"], inputs["L0_ffn_down_proj"],
        inputs["L0_ffn_similarity"])
    X = np.asarray(inputs["L0_attn_X"])
    in_maps = []
    for core in range(8):
        b, h = core // 2, core % 2
        xt = np.ascontiguousarray(X[b].T)
        own = xt[:, 2048 * h:2048 * (h + 1)]
        par = xt[:, 2048 * (1 - h):2048 * (2 - h)]
        m = dict(common)
        m["xb"] = np.ascontiguousarray(np.concatenate([own, par], axis=1))
        for key in ("cost", "sint"):
            t = common[key]
            t_own = t[:, 2048 * h:2048 * (h + 1)]
            t_par = t[:, 2048 * (1 - h):2048 * (2 - h)]
            m[key] = np.ascontiguousarray(np.concatenate([t_own, t_par], axis=1))
        in_maps.append(m)
    return in_maps


def kernel(**inputs):
    in_maps = make_in_maps(inputs)
    nc = build_nc()
    res = run_bass_kernel_spmd(nc, in_maps, core_ids=list(range(8)))
    X = np.asarray(inputs["L0_attn_X"])
    out = np.empty_like(X)
    for core in range(8):
        b, h = core // 2, core % 2
        oc = np.asarray(res.results[core]["out"])
        out[b, 2048 * h:2048 * (h + 1), :] = oc.T
    return out


# version 10
# speedup vs baseline: 1.0978x; 1.0023x over previous
import numpy as np
import concourse.bass as bass
import concourse.mybir as mybir
from concourse.bass_utils import run_bass_kernel_spmd

F32 = mybir.dt.float32
BF16 = mybir.dt.bfloat16
U16 = mybir.dt.uint16
U8 = mybir.dt.uint8
AF = mybir.ActivationFunctionType
ALU = mybir.AluOpType
AX = mybir.AxisListType

C_STEPS = 99
D = 2048
NSLAB = 8
NOWN = 4
NEXP = 32
ROT = 1024


class Sem:
    def __init__(self, nc, name):
        self._cm = nc.semaphore(name)
        self.h = self._cm.__enter__()
        self.n = 0


class KB:
    ENGS = ("pe", "dve", "act", "po", "sp")

    def __init__(self, nc):
        self.nc = nc
        self.q = {e: [] for e in self.ENGS}

    def w(self, eng, sem, val=None):
        v = sem.n if val is None else val
        self.q[eng].append(lambda E: E.wait_ge(sem.h, v))

    def o(self, eng, fn, inc=()):
        inc = tuple(inc)
        for s, k in inc:
            s.n += k

        def run(E):
            ins = fn(E)
            for s, k in inc:
                ins.then_inc(s.h, k)
        self.q[eng].append(run)

    def gate(self, sems):
        vals = [(s, s.n) for s in sems]
        for e in self.ENGS:
            for s, v in vals:
                self.w(e, s, v)


def build_nc(phlim=9):
    nc = bass.Bass("TRN2", target_bir_lowering=False)

    xb = nc.dram_tensor("xb", [D, 4096], F32, kind="ExternalInput").ap()
    win = nc.dram_tensor("win", [48, 128, 16, 128], F32, kind="ExternalInput").ap()
    binp = nc.dram_tensor("binp", [128, 48], F32, kind="ExternalInput").ap()
    wout = nc.dram_tensor("wout", [16, 128, 16, 128], F32, kind="ExternalInput").ap()
    boutp = nc.dram_tensor("boutp", [128, 16], F32, kind="ExternalInput").ap()
    gw = nc.dram_tensor("gw", [128, 16, 32], F32, kind="ExternalInput").ap()
    gup = nc.dram_tensor("gup", [32, 12, 128, 16, 128], F32, kind="ExternalInput").ap()
    gdn = nc.dram_tensor("gdn", [32, 128, 16, 6, 128], F32, kind="ExternalInput").ap()
    simm = nc.dram_tensor("simm", [32, 32], F32, kind="ExternalInput").ap()
    cost = nc.dram_tensor("cost", [512, 4096], F32, kind="ExternalInput").ap()
    sint = nc.dram_tensor("sint", [512, 4096], F32, kind="ExternalInput").ap()
    cst = nc.dram_tensor("cst", [128, 200], F32, kind="ExternalInput").ap()
    crow = nc.dram_tensor("crow", [1, 128], F32, kind="ExternalInput").ap()
    ehot = nc.dram_tensor("ehot", [32, 4096], F32, kind="ExternalInput").ap()
    colsel = nc.dram_tensor("colsel", [32, 256], F32, kind="ExternalInput").ap()
    out = nc.dram_tensor("out", [D, 2048], F32, kind="ExternalOutput").ap()

    qd = nc.dram_tensor("qd", [16, NOWN, 128, 512], BF16).ap()
    kd = nc.dram_tensor("kd", [16, NSLAB, 128, 512], BF16).ap()
    vd = nc.dram_tensor("vd", [16, NSLAB, 128, 512], BF16).ap()
    x1f = nc.dram_tensor("x1f", [16, NOWN, 128, 512], F32).ap()
    x1bd = nc.dram_tensor("x1bd", [16, NOWN, 128, 512], BF16).ap()

    pmm = [nc.alloc_psum_tensor(f"pmm{i}", [128, 512], F32) for i in range(4)]
    pA = nc.alloc_psum_tensor("pA", [128, 1024], F32)
    pB = nc.alloc_psum_tensor("pB", [128, 1024], F32)

    alloc = nc.alloc_sbuf_tensor
    ident = alloc("ident", [128, 128], F32)
    onesrow = alloc("onesrow", [1, 128], F32)
    onescolb = alloc("onescolb", [128, 1], BF16)
    cst_sb = alloc("cst_sb", [128, 200], F32)
    bin_sb = alloc("bin_sb", [128, 48], F32)
    bout_sb = alloc("bout_sb", [128, 16], F32)
    gw_sb = alloc("gw_sb", [128, 16, 32], BF16)
    sim_f = alloc("sim_f", [32, 32], F32)
    sim_fx = alloc("sim_fx", [32, 32], F32)
    sim_b = alloc("sim_b", [32, 32], BF16)
    ones32 = alloc("ones32", [32, 32], F32)
    mask32 = alloc("mask32", [32, 32], U8)
    ehot_sb = alloc("ehot_sb", [32, 4096], BF16)
    colsel_sb = alloc("colsel_sb", [32, 256], BF16)
    A_acc = alloc("A_acc", [128, 16], F32)
    A_tmp = alloc("A_tmp", [128, 16], F32)
    comb_sb = alloc("comb_sb", [32, 2048], BF16)
    rowsc = alloc("rowsc", [1, 2048], F32)
    idxrow = alloc("idxrow", [1, 1024], F32)
    i8u = alloc("i8u", [128, 8], U16)
    thrU = alloc("thrU", [128, 4], U8)

    iotaP = cst_sb[:, 128:129]
    iota32 = cst_sb[:, 130:162]
    iota4 = cst_sb[:, 162:166]

    AREN_W = (nc.sbuf_bytes_remaining - 2048) // 4
    arena = alloc("arena", [128, AREN_W], F32)
    aren_f = arena[:, :]
    aren_b = arena[:, :].bitcast(BF16)

    def vf(off_b, shape):
        n = int(np.prod(shape[1:]))
        v = aren_f[:, off_b // 4: off_b // 4 + n]
        if len(shape) == 3:
            v = v.rearrange("p (a c) -> p a c", a=shape[1])
        return v

    def vb(off_b, shape):
        n = int(np.prod(shape[1:]))
        v = aren_b[:, off_b // 2: off_b // 2 + n]
        if len(shape) == 3:
            v = v.rearrange("p (a c) -> p a c", a=shape[1])
        return v

    AREN_B = AREN_W * 4

    K = KB(nc)
    dsp = Sem(nc, "dsp")
    dst0 = Sem(nc, "dst0")
    dst1 = Sem(nc, "dst1")
    dpre = Sem(nc, "dpre")
    dx = Sem(nc, "dx")
    dwa = Sem(nc, "dwa")
    dwb = Sem(nc, "dwb")
    dgu0 = Sem(nc, "dgu0")
    dgu1 = Sem(nc, "dgu1")
    ddn0 = Sem(nc, "ddn0")
    ddn1 = Sem(nc, "ddn1")
    dxb = Sem(nc, "dxb")
    dsx = Sem(nc, "dsx")
    pe = Sem(nc, "pe")
    dve = Sem(nc, "dve")
    ac = Sem(nc, "ac")
    po = Sem(nc, "po")
    ALLS = [dsp, dst0, dst1, dpre, dx, dwa, dwb, dgu0, dgu1, ddn0, ddn1,
            dxb, dsx, pe, dve, ac, po]

    def dma(eng, sem, outp, inp, slow=False):
        K.o(eng, lambda E, o=outp, i=inp, s=slow: E.dma_start(
            out=o, in_=i, allow_slow_non_contiguous=s), [(sem, 16)])

    dma("sp", dsp, ident[:, :], cst[:, 0:128])
    dma("sp", dsp, cst_sb[:, :], cst)
    dma("sp", dsp, onesrow[:, :], crow)
    dma("sp", dsp, bin_sb[:, :], binp)
    dma("sp", dsp, bout_sb[:, :], boutp)
    dma("sp", dsp, sim_f[:, :], simm)
    dma("po", dpre, onescolb[:, :], cst[:, 129:130], slow=True)
    dma("po", dpre, gw_sb[:, :, :], gw)
    dma("po", dpre, ehot_sb[:, :], ehot)
    dma("po", dpre, colsel_sb[:, :], colsel)
    pre_dsp, pre_dg = dsp.n, dpre.n

    K.w("dve", dsp, pre_dsp)
    K.o("dve", lambda E: E.memset(ones32[:, :], 1.0))
    K.o("dve", lambda E: E.tensor_scalar(mask32[:, :], ident[0:32, 0:32], 0.5,
                                         None, ALU.is_gt), [(dve, 1)])
    K.w("dve", dve)
    K.o("dve", lambda E: E.select(sim_fx[:, :], mask32[:, :], ones32[:, :],
                                  sim_f[:, :], add_drain=True), [(dve, 1)])
    K.w("dve", dve)
    K.o("dve", lambda E: E.tensor_copy(sim_b[:, :], sim_fx[:, :]), [(dve, 1)])
    K.o("dve", lambda E: E.memset(A_acc[:, :], 0.0), [(dve, 1)])

    aoff = 0
    winb = [vb(aoff + i * 16 * 128 * 2, [128, 16, 128]) for i in range(2)]
    aoff += 2 * 16 * 128 * 2
    evb = [vb(aoff + i * 512 * 2, [128, 512]) for i in range(2)]
    aoff += 2 * 512 * 2
    xall_own = vb(aoff, [128, 16, 2048])
    xall_par = vb(aoff + 16 * 2048 * 2, [128, 16, 2048])
    xall_own_end = aoff + 16 * 2048 * 2
    aoff += 16 * 4096 * 2
    assert aoff <= AREN_B, (aoff, AREN_B)

    def xslab(s):
        return (xall_own, s) if s < 4 else (xall_par, s - 4)
    a2off = xall_own_end
    Kb = vb(a2off, [128, 16, 512]); a2off += 16 * 512 * 2
    Vb = vb(a2off, [128, 16, 512]); a2off += 16 * 512 * 2
    cosb = vf(a2off, [128, 4, 512]); a2off += 4 * 512 * 4
    sinb = vf(a2off, [128, 4, 512]); a2off += 4 * 512 * 4
    Kr = vf(a2off, [128, 8, 512]); a2off += 8 * 512 * 4
    rt1 = vf(a2off, [128, 4, 512]); a2off += 4 * 512 * 4
    rt2 = vf(a2off, [128, 4, 512]); a2off += 4 * 512 * 4
    sqb = vb(a2off, [128, 16, 512]); a2off += 16 * 512 * 2
    kvsc = [vf(a2off + i * 512 * 4, [128, 512]) for i in range(2)]
    a2off += 2 * 512 * 4
    junk2 = [rt1[:, 0, :], rt1[:, 1, :]]
    assert a2off <= AREN_B, (a2off, AREN_B)

    kdr = kd.rearrange("ec s p t -> p ec s t")
    vdr = vd.rearrange("ec s p t -> p ec s t")
    costr = cost.rearrange("(cc p) t -> p cc t", p=128)
    sintr = sint.rearrange("(cc p) t -> p cc t", p=128)

    xbr = xb.rearrange("(dc p) t -> p dc t", p=128)
    for s in range(NSLAB):
        xt, sl = xslab(s)
        dma("po", dx, xt[:, :, 512 * sl:512 * (sl + 1)],
            xbr[:, :, 512 * s:512 * (s + 1)])

    plan = [(ec, list(range(NSLAB))) for ec in range(16, 48)] + \
           [(ec, list(range(NOWN))) for ec in range(16)]

    g = 0
    evac_dve_val = {}
    evstore_val = {}
    pe_after_iter = []
    dw = [dwa, dwb]
    dstp = [dst0, dst1]
    a2_state = {}

    def emit_a2_loads(s, kvst0, kvst1):
        K.w("sp", dst0, kvst0)
        K.w("sp", dst1, kvst1)
        K.w("sp", dsp, dsp.n)
        if s >= 1:
            K.w("sp", dve, a2_state.get("consumed", 0))
            K.w("sp", ac, a2_state.get("ac_consumed", 0))
        dma("sp", dsp, Kb[:, :, :], kdr[:, :, s, :])
        dma("sp", dsp, Vb[:, :, :], vdr[:, :, s, :])
        dma("sp", dsp, cosb[:, :, :], costr[:, :, 512 * s:512 * s + 512])
        dma("sp", dsp, sinb[:, :, :], sintr[:, :, 512 * s:512 * s + 512])
        return dsp.n

    def emit_a2_compute(s, load_val):
        K.w("dve", dsp, load_val)
        K.w("po", dsp, load_val)
        K.w("dve", dve)
        K.w("po", po)
        for c in range(4):
            K.o("dve", lambda E, c=c: E.tensor_mul(rt1[:, c, :], Kb[:, c, :],
                                                   cosb[:, c, :]))
            K.o("dve", lambda E, c=c: E.tensor_mul(rt2[:, c, :], Kb[:, c + 4, :],
                                                   sinb[:, c, :]),
                [(dve, 1)] if c == 3 else ())
        for c in range(4):
            K.o("po", lambda E, c=c: E.tensor_mul(Kr[:, c, :], Kb[:, c, :],
                                                  sinb[:, c, :]))
            K.o("po", lambda E, c=c: E.tensor_mul(Kr[:, c + 4, :],
                                                  Kb[:, c + 4, :],
                                                  cosb[:, c, :]),
                [(po, 1)] if c == 3 else ())
        K.w("dve", dve)
        K.w("dve", po, po.n)
        for c in range(4):
            K.o("dve", lambda E, c=c: E.tensor_add(Kr[:, c + 4, :],
                                                   Kr[:, c + 4, :],
                                                   Kr[:, c, :]),
                [(dve, 1)] if c == 3 else ())
        K.w("dve", dve)
        for c in range(4):
            K.o("dve", lambda E, c=c: E.tensor_sub(Kr[:, c, :], rt1[:, c, :],
                                                   rt2[:, c, :]),
                [(dve, 1)] if c == 3 else ())
        K.w("act", dve, dve.n)
        for c in range(16):
            src_ = Kr[:, c, :] if c < 8 else Kb[:, c, :]
            K.o("act", lambda E, s2=src_, c=c: E.activation(sqb[:, c, :], s2,
                                                            AF.Square),
                [(ac, 1)] if c == 15 else ())
        sq_done = ac.n
        K.w("pe", ac, sq_done)
        K.w("pe", dve, dve.n)
        K.w("pe", dpre, pre_dg)
        K.w("pe", dsp, pre_dsp)
        for c in range(16):
            K.o("pe", lambda E, c=c: E.matmul(pA[0:1, 0:512], onescolb[:, :],
                                              sqb[:, c, :], start=(c == 0),
                                              stop=(c == 15),
                                              skip_group_check=True),
                [(pe, 1)] if c == 15 else ())
        K.w("act", pe, pe.n)
        K.o("act", lambda E: E.activation(rowsc[0:1, 0:512], pA[0:1, 0:512],
                                          AF.Sqrt), [(ac, 1)])
        K.w("dve", ac, ac.n)
        K.o("dve", lambda E: E.tensor_scalar(rowsc[0:1, 512:1024],
                                             rowsc[0:1, 0:512], 1e-12, None,
                                             ALU.max), [(dve, 1)])
        K.w("dve", dve)
        K.o("dve", lambda E: E.reciprocal(rowsc[0:1, 0:512],
                                          rowsc[0:1, 512:1024]), [(dve, 1)])
        K.w("pe", dve, dve.n)
        K.o("pe", lambda E: E.matmul(pB[:, 0:512], onesrow[:, :],
                                     rowsc[0:1, 0:512], start=True, stop=True,
                                     skip_group_check=True),
            [(pe, 1)])
        K.w("po", po)
        K.w("po", dve, dve.n)
        K.w("dve", pe, pe.n)
        K.w("dve", dve)
        kv_val = {}
        stt_val = {}

        def emit_kv(c):
            ks = Kr[:, c, :] if c < 8 else Kb[:, c, :]
            if c >= 2:
                K.w("po", dve, stt_val[c - 2])
            K.o("po", lambda E, k2=ks, c=c: E.tensor_mul(kvsc[c % 2][:, :], k2,
                                                         Vb[:, c, :]),
                [(po, 1)])
            kv_val[c] = po.n

        def emit_stt(c):
            K.w("dve", po, kv_val[c])
            K.o("dve", lambda E, c=c: E.scalar_tensor_tensor(
                junk2[c % 2], kvsc[c % 2][:, :], 1.0, pB[:, 0:512],
                ALU.mult, ALU.mult, accum_out=A_tmp[:, c:c + 1]), [(dve, 1)])
            stt_val[c] = dve.n

        emit_kv(0)
        emit_kv(1)
        for c in range(16):
            emit_stt(c)
            if c + 2 < 16:
                emit_kv(c + 2)
        a2_state["consumed"] = dve.n
        a2_state["ac_consumed"] = sq_done
        K.w("po", dve, dve.n)
        K.o("po", lambda E: E.tensor_add(A_acc[:, :], A_acc[:, :], A_tmp[:, :]),
            [(po, 1)])

    a2_load_val = {}
    kv_st = [0, 0]
    for i, (ec, slabs) in enumerate(plan):
        buf = i % 2
        if i >= 2:
            K.w("po", pe, pe_after_iter[i - 2])
        K.w("po", dw[buf], dw[buf].n)
        dma("po", dw[buf], winb[buf][:, :, :], win[ec])
        K.w("pe", dx, dx.n)
        K.w("pe", dw[buf], dw[buf].n)
        for s in slabs:
            bank = g % 4
            if g >= 4:
                K.w("pe", dve, evac_dve_val[g - 4])
            xt, sl = xslab(s)
            for kc in range(16):
                K.o("pe", lambda E, b=bank, bu=buf, k=kc, ss=sl, x2=xt: E.matmul(
                    pmm[b][:, :], winb[bu][:, k, :],
                    x2[:, k, 512 * ss:512 * ss + 512],
                    start=(k == 0), stop=(k == 15)),
                    [(pe, 1)] if kc == 15 else ())
            K.w("dve", pe, pe.n)
            sb_ = dstp[g % 2]
            if g >= 2:
                K.w("dve", sb_, evstore_val[g - 2])
            eb = evb[g % 2]
            K.o("dve", lambda E, b=bank, e=ec, ebuf=eb: E.tensor_scalar(
                ebuf[:, :], pmm[b][:, :], bin_sb[:, e:e + 1], None, ALU.add),
                [(dve, 1)])
            evac_dve_val[g] = dve.n
            K.w("act", dve, dve.n)
            if ec < 16:
                tgt = qd[ec, s]
            elif ec < 32:
                tgt = kd[ec - 16, s]
            else:
                tgt = vd[ec - 32, s]
            dma("act", sb_, tgt, eb[:, :])
            evstore_val[g] = sb_.n
            g += 1
        pe_after_iter.append(pe.n)
        if i == 31:
            kv_st = [dst0.n, dst1.n]
        if i >= 32:
            k2 = i - 32
            if k2 % 2 == 0 and k2 // 2 < NSLAB:
                a2_load_val[k2 // 2] = emit_a2_loads(k2 // 2, kv_st[0], kv_st[1])
            elif k2 % 2 == 1 and k2 // 2 < NSLAB:
                emit_a2_compute(k2 // 2, a2_load_val[k2 // 2])
    a1_dst0, a1_dst1 = dst0.n, dst1.n

    K.gate(ALLS)

    if phlim < 2:
        if phlim == 1.5:
            pass
        K.w("act", po, po.n)
        dma("act", dst0, out.rearrange("(dc p) t -> p dc t", p=128)[:, 0, 0:16],
            A_acc[:, :])
        K.gate(ALLS)
        return _finish(nc, K)
    boff = 0
    wob = [vb(boff + i * 16 * 128 * 2, [128, 16, 128]) for i in range(2)]
    boff += 2 * 16 * 128 * 2
    Qb = vb(boff, [128, 16, 512]); boff += 16 * 512 * 2
    Qr = vf(boff, [128, 8, 512]); boff += 8 * 512 * 4
    qa = vb(boff, [128, 16, 512]); boff += 16 * 512 * 2
    cosb2 = vf(boff, [128, 4, 512]); boff += 4 * 512 * 4
    sinb2 = vf(boff, [128, 4, 512]); boff += 4 * 512 * 4
    rtb1 = vf(boff, [128, 4, 512])
    sqb2 = vb(boff, [128, 16, 512])
    boff += 4 * 512 * 4
    rtb2 = vf(boff, [128, 4, 512]); boff += 4 * 512 * 4
    Xf = vf(boff, [128, 16, 512]); boff += 16 * 512 * 4
    out1 = vf(boff, [128, 16, 512]); boff += 16 * 512 * 4
    o1b = vb(boff, [128, 16, 512]); boff += 16 * 512 * 2
    assert boff <= AREN_B, (boff, AREN_B)

    qdr = qd.rearrange("ec s p t -> p ec s t")
    x1fr = x1f.rearrange("ec s p t -> p ec s t")
    x1br = x1bd.rearrange("ec s p t -> p ec s t")

    def emit_ln(src3, bconv, sqv, eps=1e-5):
        K.w("act", dve, dve.n)
        for c in range(16):
            K.o("act", lambda E, c=c: E.activation(sqv[:, c, :], src3[:, c, :],
                                                   AF.Square),
                [(ac, 1)] if c == 15 else ())
        K.w("dve", dve)
        for c in range(16):
            K.o("dve", lambda E, c=c: E.tensor_copy(bconv[:, c, :],
                                                    src3[:, c, :]),
                [(dve, 1)] if c == 15 else ())
        K.w("pe", ac, ac.n)
        K.w("pe", dve, dve.n)
        for c in range(16):
            K.o("pe", lambda E, c=c: E.matmul(pA[0:1, 0:512], onescolb[:, :],
                                              bconv[:, c, :], start=(c == 0),
                                              stop=(c == 15)),
                [(pe, 1)] if c == 15 else ())
        for c in range(16):
            K.o("pe", lambda E, c=c: E.matmul(pA[0:1, 512:1024], onescolb[:, :],
                                              sqv[:, c, :], start=(c == 0),
                                              stop=(c == 15)),
                [(pe, 1)] if c == 15 else ())
        K.w("dve", pe, pe.n)
        K.o("dve", lambda E: E.tensor_scalar(rowsc[0:1, 0:512], pA[0:1, 0:512],
                                             1.0 / D, None, ALU.mult), [(dve, 1)])
        K.w("act", dve, dve.n)
        K.o("act", lambda E: E.activation(rowsc[0:1, 512:1024], rowsc[0:1, 0:512],
                                          AF.Square), [(ac, 1)])
        K.w("dve", ac, ac.n)
        K.o("dve", lambda E: E.scalar_tensor_tensor(
            rowsc[0:1, 1024:1536], pA[0:1, 512:1024], 1.0 / D,
            rowsc[0:1, 512:1024], ALU.mult, ALU.subtract), [(dve, 1)])
        K.w("dve", dve)
        K.o("dve", lambda E: E.tensor_scalar(rowsc[0:1, 512:1024],
                                             rowsc[0:1, 1024:1536], eps, None,
                                             ALU.add), [(dve, 1)])
        K.w("act", dve, dve.n)
        K.o("act", lambda E: E.activation(rowsc[0:1, 1024:1536],
                                          rowsc[0:1, 512:1024], AF.Sqrt),
            [(ac, 1)])
        K.w("dve", ac, ac.n)
        K.o("dve", lambda E: E.reciprocal(rowsc[0:1, 1536:2048],
                                          rowsc[0:1, 1024:1536]), [(dve, 1)])
        K.w("pe", dve, dve.n)
        K.o("pe", lambda E: E.matmul(pB[:, 0:512], onesrow[:, :],
                                     rowsc[0:1, 0:512], start=True, stop=True))
        K.o("pe", lambda E: E.matmul(pB[:, 512:1024], onesrow[:, :],
                                     rowsc[0:1, 1536:2048], start=True,
                                     stop=True), [(pe, 1)])
        K.w("dve", pe, pe.n)
        for c in range(16):
            K.o("dve", lambda E, c=c: E.tensor_sub(src3[:, c, :], src3[:, c, :],
                                                   pB[:, 0:512]),
                [(dve, 1)] if c == 15 else ())
        K.w("dve", dve)
        for c in range(16):
            K.o("dve", lambda E, c=c: E.tensor_mul(src3[:, c, :], src3[:, c, :],
                                                   pB[:, 512:1024]),
                [(dve, 1)] if c == 15 else ())

    wout_g = 0
    woutev = {}
    wo_plan_pe = []
    for tt in range(NOWN):
        K.w("sp", dst0, a1_dst0)
        K.w("sp", dst1, a1_dst1)
        K.w("sp", dsp, dsp.n)
        if tt >= 1:
            K.w("sp", dve, dve.n)
            K.w("sp", ac, ac.n)
        dma("sp", dsp, Qb[:, :, :], qdr[:, :, tt, :])
        dma("sp", dsp, cosb2[:, :, :], costr[:, :, 512 * tt:512 * tt + 512])
        dma("sp", dsp, sinb2[:, :, :], sintr[:, :, 512 * tt:512 * tt + 512])
        dma("sp", dsp, Xf[:, :, :], xbr[:, :, 512 * tt:512 * tt + 512])
        K.w("dve", dsp, dsp.n)
        K.w("dve", ac, ac.n)
        K.w("dve", dst0, dst0.n)
        K.w("dve", dxb, dxb.n)
        for c in range(4):
            K.o("dve", lambda E, c=c: E.tensor_mul(rtb1[:, c, :], Qb[:, c, :],
                                                   cosb2[:, c, :]))
            K.o("dve", lambda E, c=c: E.tensor_mul(rtb2[:, c, :], Qb[:, c + 4, :],
                                                   sinb2[:, c, :]),
                [(dve, 1)] if c == 3 else ())
        K.w("dve", dve)
        for c in range(4):
            K.o("dve", lambda E, c=c: E.tensor_sub(Qr[:, c, :], rtb1[:, c, :],
                                                   rtb2[:, c, :]),
                [(dve, 1)] if c == 3 else ())
        K.w("dve", dve)
        for c in range(4):
            K.o("dve", lambda E, c=c: E.tensor_mul(rtb1[:, c, :], Qb[:, c, :],
                                                   sinb2[:, c, :]))
            K.o("dve", lambda E, c=c: E.tensor_mul(rtb2[:, c, :], Qb[:, c + 4, :],
                                                   cosb2[:, c, :]),
                [(dve, 1)] if c == 3 else ())
        K.w("dve", dve)
        for c in range(4):
            K.o("dve", lambda E, c=c: E.tensor_add(Qr[:, c + 4, :], rtb2[:, c, :],
                                                   rtb1[:, c, :]),
                [(dve, 1)] if c == 3 else ())
        K.w("dve", dve)
        for c in range(16):
            src = Qr[:, c, :] if c < 8 else Qb[:, c, :]
            K.o("dve", lambda E, s_=src, c=c: E.tensor_scalar(
                qa[:, c, :], s_, A_acc[:, c:c + 1], None, ALU.mult),
                [(dve, 1)] if c == 15 else ())
        qa_done = dve.n
        for ec in range(16):
            wbuf = ec % 2
            widx = tt * 16 + ec
            if widx >= 2:
                K.w("po", pe, wo_plan_pe[widx - 2])
            K.w("po", dw[wbuf], dw[wbuf].n)
            dma("po", dw[wbuf], wob[wbuf][:, :, :], wout[ec])
            bank = wout_g % 4
            K.w("pe", dw[wbuf], dw[wbuf].n)
            K.w("pe", dve, qa_done)
            if wout_g >= 4:
                K.w("pe", dve, woutev[wout_g - 4])
            for kc in range(16):
                K.o("pe", lambda E, b=bank, wb=wbuf, k=kc: E.matmul(
                    pmm[b][:, :], wob[wb][:, k, :], qa[:, k, :],
                    start=(k == 0), stop=(k == 15)),
                    [(pe, 1)] if kc == 15 else ())
            wo_plan_pe.append(pe.n)
            K.w("dve", pe, pe.n)
            K.o("dve", lambda E, b=bank, e=ec: E.scalar_tensor_tensor(
                out1[:, e, :], pmm[b][:, :], bout_sb[:, e:e + 1], Xf[:, e, :],
                ALU.add, ALU.add), [(dve, 1)])
            K.w("dve", dve)
            K.o("dve", lambda E, e=ec: E.tensor_add(out1[:, e, :],
                                                    out1[:, e, :], Xf[:, e, :]),
                [(dve, 1)])
            woutev[wout_g] = dve.n
            wout_g += 1
        emit_ln(out1, o1b, sqb2)
        K.w("act", dve, dve.n)
        K.w("act", dst0, dst0.n)
        dma("act", dst0, x1fr[:, :, tt, :], out1[:, :, :])
        K.w("po", dve, dve.n)
        K.w("po", dxb, dxb.n)
        dma("po", dxb, x1br[:, :, tt, :], out1[:, :, :])

    b_dst0 = dst0.n
    b_dxb = dxb.n
    K.gate(ALLS)

    if phlim < 4:
        K.w("sp", dst0, dst0.n)
        K.w("sp", dxb, dxb.n)
        _or = out.rearrange("(dc p) t -> p dc t", p=128)
        _xr = x1f.rearrange("ec s p t -> p ec s t")
        for t4 in range(4):
            K.o("sp", lambda E, t4=t4, o2=_or, x2=_xr: E.dma_start(
                out=o2[:, :, 512 * t4:512 * t4 + 512],
                in_=x2[:, :, t4, :]), [(dsp, 16)])
        K.gate(ALLS)
        return _finish(nc, K)
    doff = 0
    ysb = vf(doff, [128, 16, 1024]); doff += 16 * 1024 * 4
    x1h = vb(doff, [128, 16, 1024]); doff += 16 * 1024 * 2
    ebase = doff
    hsb = vb(doff, [128, 6, 1024]); doff += 6 * 1024 * 2
    gub = [vb(doff + i * 2 * 16 * 128 * 2, [128, 2, 16 * 128]) for i in range(2)]
    doff += 2 * 2 * 16 * 128 * 2
    dnb = [vb(doff + i * 8 * 6 * 128 * 2, [128, 8 * 6 * 128]) for i in range(2)]
    doff += 2 * 8 * 6 * 128 * 2
    dexp_end = doff
    cwsb = vf(doff, [128, 1024]); doff += 1024 * 4
    hg = vf(doff, [128, 2, 512]); doff += 2 * 512 * 4
    hx = vf(doff, [128, 2, 512]); doff += 2 * 512 * 4
    assert doff <= AREN_B, (doff, AREN_B)
    ybd = vb(ebase, [128, 16, 512])
    sqd = vb(ebase + 16 * 512 * 2, [128, 16, 512])
    assert ebase + 2 * 16 * 512 * 2 <= dexp_end

    outr = out.rearrange("(dc p) t -> p dc t", p=128)
    gu_dg = {}
    gu_pe = {}
    dn_dg = {}
    dn_pe = {}
    dn_g = 0
    ydve = {}
    dgu = [dgu0, dgu1]
    ddn = [ddn0, ddn1]

    def emit_gu_dma(hh_, si, guard_val):
        e_, c_ = divmod(si, 6)
        if guard_val is not None:
            K.w("po", pe, guard_val)
        sgu = dgu[si % 2]
        K.w("po", sgu, sgu.n)
        for gi in range(2):
            fc = c_ + 6 * gi
            K.o("po", lambda E, e=e_, f=fc, b=si % 2, gi=gi: E.dma_start(
                out=gub[b][:, gi, :].rearrange("p (dc j) -> p dc j", dc=16),
                in_=gup[e, f]), [(sgu, 16)])
        gu_dg[(hh_, si)] = (sgu, sgu.n)

    def emit_dn_dma(hh_, e_, dh, guard_val):
        if guard_val is not None:
            K.w("po", pe, guard_val)
        sdn = ddn[dh]
        K.w("po", sdn, sdn.n)
        K.o("po", lambda E, e=e_, d=dh: E.dma_start(
            out=dnb[d][:, :].rearrange("p (dc fc j) -> p dc fc j",
                                       dc=8, fc=6),
            in_=gdn[e][:, 8 * d:8 * d + 8]), [(sdn, 16)])
        dn_dg[(hh_, e_, dh)] = (sdn, sdn.n)

    def emit_x1h_loads(hh_):
        tt0_, tt1_ = 2 * hh_, 2 * hh_ + 1
        K.w("sp", dst0, b_dst0)
        K.w("sp", dxb, b_dxb)
        K.w("sp", dsx, dsx.n)
        dma("sp", dsx, x1h[:, :, 0:512], x1br[:, :, tt0_, :])
        dma("sp", dsx, x1h[:, :, 512:1024], x1br[:, :, tt1_, :])
        return dsx.n

    def emit_y_loads(hh_):
        tt0_, tt1_ = 2 * hh_, 2 * hh_ + 1
        K.w("sp", dst0, b_dst0)
        K.w("sp", dsp, dsp.n)
        dma("sp", dsp, ysb[:, :, 0:512], x1fr[:, :, tt0_, :])
        dma("sp", dsp, ysb[:, :, 512:1024], x1fr[:, :, tt1_, :])
        return dsp.n

    pre0 = emit_x1h_loads(0)
    emit_gu_dma(0, 0, None)
    emit_gu_dma(0, 1, None)
    emit_dn_dma(0, 0, 0, None)
    emit_dn_dma(0, 0, 1, None)

    _co = [0]

    def cvf(shape):
        v = vf(_co[0], shape)
        n = int(np.prod(shape[1:])) * 4
        _co[0] += (n + 63) & ~63
        return v

    def cvb(shape):
        v = vb(_co[0], shape)
        n = int(np.prod(shape[1:])) * 2
        _co[0] += (n + 63) & ~63
        return v

    x1t = cvb([128, 16, 512])
    lgf = cvf([128, 512])
    ltok = cvf([128, 32])
    v8 = cvf([128, 8])
    idxf = cvf([128, 8])
    nv0 = cvf([128, 1])
    ev8 = cvf([128, 8])
    s8 = cvf([128, 1])
    r8 = cvf([128, 1])
    rw8 = cvf([128, 8])
    oh = cvb([128, 8 * 128])
    Mt = cvb([128, 16 * 128])
    scf = cvf([128, 128])
    sct = cvf([128, 16])
    bs4 = cvf([128, 4])
    cand = cvf([128, 16])
    am4 = cvf([128, 4])
    ohm = cvf([128, 16])
    pm16 = cvf([128, 16])
    bp4 = cvf([128, 4])
    rer8 = cvf([128, 8])
    ohc = cvf([128, 32 * 8])
    ohc2 = cvf([128, 32 * 8])
    combt = cvf([128, 32])
    assert _co[0] <= AREN_B, (_co[0], AREN_B)

    ohv = oh.rearrange("p (k t) -> p k t", k=8)[0:32, :, :]
    mtv = Mt.rearrange("p (r t) -> p r t", r=16)[0:32, :, :]
    sct3 = sct.rearrange("p (j i) -> p j i", i=4)
    cand3 = cand.rearrange("p (j i) -> p j i", i=4)
    ohm3 = ohm.rearrange("p (j i) -> p j i", i=4)
    pm3 = pm16.rearrange("p (j i) -> p j i", i=4)
    ohc3 = ohc.rearrange("p (e k) -> p e k", k=8)
    ohc23 = ohc2.rearrange("p (e k) -> p e k", k=8)

    for tt in range(NOWN):
        K.w("sp", dxb, b_dxb)
        K.w("sp", dsp, dsp.n)
        if tt >= 1:
            K.w("sp", pe, pe.n)
        dma("sp", dsp, x1t[:, :, :], x1br[:, :, tt, :])
        K.w("pe", dsp, dsp.n)
        K.w("pe", dpre, pre_dg)
        if tt >= 1:
            K.w("pe", dve, dve.n)
        for kc in range(16):
            K.o("pe", lambda E, k=kc: E.matmul(pmm[0][0:32, :], gw_sb[:, k, :],
                                               x1t[:, k, :], start=(k == 0),
                                               stop=(k == 15)),
                [(pe, 1)] if kc == 15 else ())
        K.w("dve", pe, pe.n)
        K.o("dve", lambda E: E.tensor_copy(lgf[0:32, :], pmm[0][0:32, :]),
            [(dve, 1)])
        for j in range(4):
            if C_STEPS < 2:
                continue
            K.w("pe", dve, dve.n)
            K.o("pe", lambda E, j=j: E.transpose(
                pmm[1][:, 0:32], lgf[0:32, 128 * j:128 * j + 128],
                ident[0:32, 0:32]), [(pe, 1)])
            K.w("dve", pe, pe.n)
            K.o("dve", lambda E: E.tensor_copy(ltok[:, :], pmm[1][:, 0:32]),
                [(dve, 1)])
            if C_STEPS < 3:
                continue
            K.w("dve", dve)
            K.o("dve", lambda E: E.max(out=v8[:, :], in_=ltok[:, :]), [(dve, 1)])
            K.w("dve", dve)
            K.o("dve", lambda E: E.max_index(i8u[:, :], v8[:, :], ltok[:, :]),
                [(dve, 1)])
            K.w("dve", dve)
            K.o("dve", lambda E: E.tensor_copy(idxf[:, :], i8u[:, :]))
            K.o("dve", lambda E: E.tensor_scalar(nv0[:, :], v8[:, 0:1], -1.0,
                                                 None, ALU.mult), [(dve, 1)])
            if C_STEPS < 4:
                continue
            K.w("act", dve, dve.n)
            import os as _os
            if _os.environ.get("CVAR") == "sq":
                K.o("act", lambda E: E.activation(ev8[:, :], v8[:, :],
                                                  AF.Square), [(ac, 1)])
            elif _os.environ.get("CVAR") == "nobias":
                K.o("act", lambda E: E.activation(ev8[:, :], v8[:, :],
                                                  AF.Exp), [(ac, 1)])
            else:
                K.o("act", lambda E: E.activation(ev8[:, :], v8[:, :], AF.Exp,
                                                  bias=nv0[:, :]), [(ac, 1)])
            K.w("dve", ac, ac.n)
            K.o("dve", lambda E: E.reduce_sum(s8[:, :], ev8[:, :], axis=AX.X),
                [(dve, 1)])
            K.w("dve", dve)
            K.o("dve", lambda E: E.reciprocal(r8[:, :], s8[:, :]), [(dve, 1)])
            K.w("dve", dve)
            K.o("dve", lambda E: E.tensor_scalar(rw8[:, :], ev8[:, :], r8[:, :],
                                                 None, ALU.mult), [(dve, 1)])
            if C_STEPS < 5:
                continue
            K.w("pe", dve, dve.n)
            for k in range(8):
                bank = k % 4
                if k >= 4:
                    K.w("pe", dve, dve.n)
                K.o("pe", lambda E, k=k, b=bank: E.transpose(
                    pmm[b][0:1, 0:128], idxf[:, k:k + 1],
                    ident[:, :]), [(pe, 1)])
                K.w("dve", pe, pe.n)
                K.o("dve", lambda E, k=k, b=bank: E.tensor_copy(
                    idxrow[0:1, 128 * k:128 * k + 128],
                    pmm[b][0:1, 0:128]), [(dve, 1)])
            if C_STEPS < 6:
                continue
            K.w("pe", dve, dve.n)
            K.o("pe", lambda E: E.matmul(pA[0:32, 0:512], onesrow[0:1, 0:32],
                                         idxrow[0:1, 0:512], start=True,
                                         stop=True))
            K.o("pe", lambda E: E.matmul(pA[0:32, 512:1024], onesrow[0:1, 0:32],
                                         idxrow[0:1, 512:1024], start=True,
                                         stop=True), [(pe, 1)])
            K.w("dve", pe, pe.n)
            K.o("dve", lambda E: E.tensor_scalar(
                ohv, pA[0:32, 0:1024].rearrange("p (k t) -> p k t", k=8),
                iotaP[0:32, :], None, ALU.is_equal), [(dve, 1)])
            if C_STEPS < 7:
                continue
            K.w("pe", dve, dve.n)
            for j2 in range(4):
                K.o("pe", lambda E, j2=j2: E.matmul(
                    pB[0:32, 128 * j2:128 * j2 + 128], sim_b[:, :],
                    ohv[:, 4 + j2, :], start=True, stop=True),
                    [(pe, 1)] if j2 == 3 else ())
            K.w("dve", pe, pe.n)
            for j2 in range(4):
                for i2 in range(4):
                    K.o("dve", lambda E, j2=j2, i2=i2: E.tensor_mul(
                        mtv[:, 4 * j2 + i2, :], ohv[:, i2, :],
                        pB[0:32, 128 * j2:128 * j2 + 128]),
                        [(dve, 1)] if (j2 == 3 and i2 == 3) else ())
            K.w("pe", dve, dve.n)
            for r in range(16):
                K.o("pe", lambda E, r=r: E.matmul(
                    pmm[3][0:16, 0:128], colsel_sb[:, 16 * r:16 * r + 16],
                    mtv[:, r, :], start=(r == 0), stop=(r == 15)),
                    [(pe, 1)] if r == 15 else ())
            K.w("dve", pe, pe.n)
            K.o("dve", lambda E: E.tensor_copy(scf[0:16, :], pmm[3][0:16, 0:128]),
                [(dve, 1)])
            K.w("pe", dve, dve.n)
            K.o("pe", lambda E: E.transpose(pmm[2][:, 0:16], scf[0:16, :],
                                            ident[0:16, 0:16]), [(pe, 1)])
            K.w("dve", pe, pe.n)
            K.o("dve", lambda E: E.tensor_copy(sct[:, :], pmm[2][:, 0:16]),
                [(dve, 1)])
            if C_STEPS < 8:
                continue
            K.w("dve", dve)
            K.o("dve", lambda E: E.reduce_max(bs4[:, :], sct3, axis=AX.X),
                [(dve, 1)])
            K.w("dve", dve)
            bs_b = bs4.unsqueeze(2).to_broadcast([128, 4, 4])
            K.o("dve", lambda E: E.tensor_tensor(cand3, sct3, bs_b, ALU.is_lt),
                [(dve, 1)])
            K.w("dve", dve)
            iota4_b = iota4.unsqueeze(1).to_broadcast([128, 4, 4])
            K.o("dve", lambda E: E.scalar_tensor_tensor(
                cand3, cand3, 8.0, iota4_b, ALU.mult, ALU.add), [(dve, 1)])
            K.w("dve", dve)
            K.o("dve", lambda E: E.tensor_reduce(am4[:, :], cand3, AX.X,
                                                 ALU.min), [(dve, 1)])
            K.w("dve", dve)
            am_b = am4.unsqueeze(2).to_broadcast([128, 4, 4])
            K.o("dve", lambda E: E.tensor_tensor(ohm3, am_b, iota4_b,
                                                 ALU.is_equal), [(dve, 1)])
            K.w("dve", dve)
            prim_b = idxf[:, 0:4].unsqueeze(1).to_broadcast([128, 4, 4])
            K.o("dve", lambda E: E.tensor_mul(pm3, ohm3, prim_b), [(dve, 1)])
            K.w("dve", dve)
            K.o("dve", lambda E: E.reduce_sum(bp4[:, :], pm3, axis=AX.X),
                [(dve, 1)])
            K.w("dve", dve)
            K.o("dve", lambda E: E.tensor_scalar(thrU[:, :], bs4[:, :], 0.5,
                                                 None, ALU.is_lt), [(dve, 1)])
            K.w("dve", dve)
            K.o("dve", lambda E: E.tensor_copy(rer8[:, 0:4], idxf[:, 0:4]),
                [(dve, 1)])
            K.w("dve", dve)
            K.o("dve", lambda E: E.select(rer8[:, 4:8], thrU[:, :],
                                          idxf[:, 4:8], bp4[:, :],
                                          add_drain=True), [(dve, 1)])
            if C_STEPS < 9:
                continue
            K.w("dve", dve)
            iotaE_b = iota32.unsqueeze(2).to_broadcast([128, 32, 8])
            rer_b = rer8.unsqueeze(1).to_broadcast([128, 32, 8])
            K.o("dve", lambda E: E.tensor_tensor(ohc3, iotaE_b, rer_b,
                                                 ALU.is_equal), [(dve, 1)])
            K.w("dve", dve)
            rw_b = rw8.unsqueeze(1).to_broadcast([128, 32, 8])
            K.o("dve", lambda E: E.tensor_mul(ohc23, ohc3, rw_b), [(dve, 1)])
            K.w("dve", dve)
            K.o("dve", lambda E: E.reduce_sum(combt[:, :], ohc23, axis=AX.X),
                [(dve, 1)])
            K.w("pe", dve, dve.n)
            K.o("pe", lambda E: E.transpose(pmm[1][0:32, 0:128], combt[:, :],
                                            ident[:, :]), [(pe, 1)])
            K.w("dve", pe, pe.n)
            col = 512 * tt + 128 * j
            K.o("dve", lambda E, c=col: E.tensor_copy(
                comb_sb[:, c:c + 128], pmm[1][0:32, 0:128]), [(dve, 1)])

    K.gate(ALLS)

    if phlim < 5:
        K.w("po", dve, dve.n)
        K.o("po", lambda E: E.dma_start(
            out=out[0:32, :], in_=comb_sb[:, :]), [(dxb, 16)])
        K.gate(ALLS)
        return _finish(nc, K)

    for hh in range(2):
        tt0, tt1 = 2 * hh, 2 * hh + 1
        if hh == 0:
            x1_ready = pre0
        else:
            x1_ready = emit_x1h_loads(hh)
            emit_gu_dma(hh, 0, None)
            emit_gu_dma(hh, 1, None)
            emit_dn_dma(hh, 0, 0, None)
            emit_dn_dma(hh, 0, 1, None)
        y_ready = emit_y_loads(hh)

        prev_y_add = dve.n
        for e in range(NEXP):
            K.w("pe", dsx, x1_ready)
            if e >= 1:
                K.w("pe", po, po.n)
            cw0 = 1024 * hh
            K.o("pe", lambda E, c0=cw0, e=e: E.matmul(
                pA[:, 0:512], ehot_sb[:, 128 * e:128 * e + 128],
                comb_sb[:, c0:c0 + 512], start=True, stop=True))
            K.o("pe", lambda E, c0=cw0, e=e: E.matmul(
                pA[:, 512:1024], ehot_sb[:, 128 * e:128 * e + 128],
                comb_sb[:, c0 + 512:c0 + 1024], start=True, stop=True),
                [(pe, 1)])
            K.w("dve", pe, pe.n)
            K.o("dve", lambda E: E.tensor_copy(cwsb[:, 0:512], pA[:, 0:512]))
            K.o("dve", lambda E: E.tensor_copy(cwsb[:, 512:1024],
                                               pA[:, 512:1024]), [(dve, 1)])
            cw_done = dve.n
            for c in range(6):
                si = e * 6 + c
                K.w("pe", gu_dg[(hh, si)][0], gu_dg[(hh, si)][1])
                if (hh, e, c) != (0, 0, 0):
                    K.w("pe", ac, ac.n)
                for tt2 in range(2):
                    for kc in range(16):
                        K.o("pe", lambda E, b=tt2, k=kc, t2=tt2,
                            bu=c % 2: E.matmul(
                                pmm[b][:, :],
                                gub[bu][:, 0, :].rearrange(
                                    "p (dc j) -> p dc j", dc=16)[:, k, :],
                                x1h[:, k, 512 * t2:512 * t2 + 512],
                                start=(k == 0), stop=(k == 15)),
                            [(pe, 1)] if kc == 15 else ())
                g_pe = pe.n
                if (hh, e, c) != (0, 0, 0):
                    K.w("pe", dve, dve.n)
                for tt2 in range(2):
                    for kc in range(16):
                        K.o("pe", lambda E, b=2 + tt2, k=kc, t2=tt2,
                            bu=c % 2: E.matmul(
                                pmm[b][:, :],
                                gub[bu][:, 1, :].rearrange(
                                    "p (dc j) -> p dc j", dc=16)[:, k, :],
                                x1h[:, k, 512 * t2:512 * t2 + 512],
                                start=(k == 0), stop=(k == 15)),
                            [(pe, 1)] if kc == 15 else ())
                gu_pe[(hh, si)] = pe.n
                if si + 2 < NEXP * 6:
                    emit_gu_dma(hh, si + 2, gu_pe[(hh, si)])
                K.w("act", pe, g_pe)
                K.w("act", dve, dve.n)
                for tt2 in range(2):
                    K.o("act", lambda E, t2=tt2: E.activation(
                        hg[:, t2, :], pmm[t2][:, :], AF.Silu),
                        [(ac, 1)] if tt2 == 1 else ())
                K.w("dve", ac, ac.n)
                K.w("dve", pe, pe.n)
                K.w("dve", po, po.n)
                for tt2 in range(2):
                    K.o("dve", lambda E, t2=tt2: E.tensor_mul(
                        hx[:, t2, :], hg[:, t2, :], pmm[2 + t2][:, :]),
                        [(dve, 1)] if tt2 == 1 else ())
                K.w("po", dve, dve.n)
                K.w("po", dve, cw_done)
                for tt2 in range(2):
                    K.o("po", lambda E, t2=tt2, c=c: E.tensor_mul(
                        hsb[:, c, 512 * t2:512 * t2 + 512], hx[:, t2, :],
                        cwsb[:, 512 * t2:512 * t2 + 512]),
                        [(po, 1)] if tt2 == 1 else ())
            h_done = po.n
            K.w("pe", po, h_done)
            K.w("dve", dve, prev_y_add)
            if e == 0:
                K.w("dve", dsp, y_ready)
            for dh in range(2):
                K.w("pe", dn_dg[(hh, e, dh)][0], dn_dg[(hh, e, dh)][1])
                for dcl in range(8):
                    dc = 8 * dh + dcl
                    for tt2 in range(2):
                        q = dn_g % 2
                        dtgt = pB[:, 512 * q:512 * q + 512]
                        if dn_g >= 2:
                            K.w("pe", dve, ydve[dn_g - 2])
                        for fc in range(6):
                            K.o("pe", lambda E, dt=dtgt, b=dh, dl=dcl, f=fc,
                                t2=tt2: E.matmul(
                                    dt,
                                    dnb[b][:, :].rearrange(
                                        "p (dc fc j) -> p dc fc j", dc=8,
                                        fc=6)[:, dl, f, :],
                                    hsb[:, f, 512 * t2:512 * t2 + 512],
                                    start=(f == 0), stop=(f == 5)),
                                [(pe, 1)] if fc == 5 else ())
                        K.w("dve", pe, pe.n)
                        K.o("dve", lambda E, dt=dtgt, d=dc, t2=tt2: E.tensor_add(
                            ysb[:, d, 512 * t2:512 * t2 + 512],
                            ysb[:, d, 512 * t2:512 * t2 + 512],
                            dt), [(dve, 1)])
                        ydve[dn_g] = dve.n
                        dn_g += 1
                dn_pe[(hh, e, dh)] = pe.n
                if e + 1 < NEXP:
                    emit_dn_dma(hh, e + 1, dh, dn_pe[(hh, e, dh)])
            prev_y_add = dve.n

        for tt2 in range(2):
            yv = ysb[:, :, 512 * tt2:512 * tt2 + 512]
            K.w("dve", dve)
            K.w("act", dve, dve.n)
            K.w("pe", dve, dve.n)
            for s_ in (dgu0, dgu1, ddn0, ddn1):
                K.w("pe", s_, s_.n)
            emit_ln(yv, ybd, sqd)
            K.w("act", dve, dve.n)
            K.w("act", dst0, dst0.n)
            tcol = 1024 * hh + 512 * tt2
            dma("act", dst0, outr[:, :, tcol:tcol + 512], yv)
        K.gate(ALLS)

    return _finish(nc, K)


def _finish(nc, K):
    with nc.Block() as block:
        @block.tensor
        def _(tensor):
            for f in K.q["pe"]:
                f(tensor)

        @block.vector
        def _(vector):
            for f in K.q["dve"]:
                f(vector)

        @block.scalar
        def _(scalar):
            for f in K.q["act"]:
                f(scalar)

        @block.gpsimd
        def _(gpsimd):
            for f in K.q["po"]:
                f(gpsimd)

        @block.sync
        def _(sync):
            for f in K.q["sp"]:
                f(sync)
    return nc



def _host_prep(W_in, b_in, W_out, b_out, gate_w, gate_up, down, similarity):
    perm = np.concatenate([np.arange(0, ROT, 2), np.arange(1, ROT, 2),
                           np.arange(ROT, D)]).astype(np.int64)
    rows = np.concatenate([perm, D + perm, 2 * D + perm])
    W_in_p = np.asarray(W_in)[rows]
    b_in_p = np.asarray(b_in)[rows]
    win_t = np.ascontiguousarray(
        W_in_p.reshape(48, 128, 16, 128).transpose(0, 3, 2, 1))
    bin_a = np.ascontiguousarray(b_in_p.reshape(48, 128).T)
    WoutT_p = np.ascontiguousarray(np.asarray(W_out).T[perm])
    wout_t = np.ascontiguousarray(
        WoutT_p.reshape(16, 128, 16, 128).transpose(2, 1, 0, 3))
    bout_a = np.ascontiguousarray(np.asarray(b_out).reshape(16, 128).T)
    gw_t = np.ascontiguousarray(
        np.asarray(gate_w).T.reshape(16, 128, 32).transpose(1, 0, 2))
    gup_t = np.ascontiguousarray(
        np.asarray(gate_up).reshape(32, 12, 128, 16, 128)
        .transpose(0, 1, 4, 3, 2))
    gdn_t = np.ascontiguousarray(
        np.asarray(down).reshape(32, 16, 128, 6, 128)
        .transpose(0, 4, 1, 3, 2))
    sim_a = np.ascontiguousarray(np.asarray(similarity), dtype=np.float32)

    inv = (10000.0 ** (-np.arange(0, ROT, 2, dtype=np.float64) / ROT))
    ang = np.arange(4096, dtype=np.float64)[None, :] * inv[:, None]
    cos_t = np.cos(ang).astype(np.float32)
    sin_t = np.sin(ang).astype(np.float32)

    cstv = np.zeros((128, 200), dtype=np.float32)
    cstv[:, 0:128] = np.eye(128, dtype=np.float32)
    cstv[:, 128] = np.arange(128, dtype=np.float32)
    cstv[:, 129] = 1.0
    cstv[:, 130:162] = np.arange(32, dtype=np.float32)[None, :]
    cstv[:, 162:166] = np.arange(4, dtype=np.float32)[None, :]
    crowv = np.ones((1, 128), dtype=np.float32)
    eh = np.zeros((32, 4096), dtype=np.float32)
    for e in range(32):
        eh[e, 128 * e:128 * (e + 1)] = 1.0
    csel = np.zeros((32, 256), dtype=np.float32)
    for r in range(16):
        csel[:, 16 * r + r] = 1.0
    return dict(win=win_t, binp=bin_a, wout=wout_t, boutp=bout_a, gw=gw_t,
                gup=gup_t, gdn=gdn_t, simm=sim_a, cost=cos_t, sint=sin_t,
                cst=cstv, crow=crowv, ehot=eh, colsel=csel)


def make_in_maps(inputs):
    common = _host_prep(
        inputs["L0_attn_W_in"], inputs["L0_attn_b_in"], inputs["L0_attn_W_out"],
        inputs["L0_attn_b_out"], inputs["L0_ffn_gate_weight"],
        inputs["L0_ffn_gate_up_proj"], inputs["L0_ffn_down_proj"],
        inputs["L0_ffn_similarity"])
    X = np.asarray(inputs["L0_attn_X"])
    in_maps = []
    for core in range(8):
        b, h = core // 2, core % 2
        xt = np.ascontiguousarray(X[b].T)
        own = xt[:, 2048 * h:2048 * (h + 1)]
        par = xt[:, 2048 * (1 - h):2048 * (2 - h)]
        m = dict(common)
        m["xb"] = np.ascontiguousarray(np.concatenate([own, par], axis=1))
        for key in ("cost", "sint"):
            t = common[key]
            t_own = t[:, 2048 * h:2048 * (h + 1)]
            t_par = t[:, 2048 * (1 - h):2048 * (2 - h)]
            m[key] = np.ascontiguousarray(np.concatenate([t_own, t_par], axis=1))
        in_maps.append(m)
    return in_maps


def kernel(**inputs):
    in_maps = make_in_maps(inputs)
    nc = build_nc()
    res = run_bass_kernel_spmd(nc, in_maps, core_ids=list(range(8)))
    X = np.asarray(inputs["L0_attn_X"])
    out = np.empty_like(X)
    for core in range(8):
        b, h = core // 2, core % 2
        oc = np.asarray(res.results[core]["out"])
        out[b, 2048 * h:2048 * (h + 1), :] = oc.T
    return out


# version 12 (final)
# speedup vs baseline: 1.1573x; 1.0543x over previous
import numpy as np
import concourse.bass as bass
import concourse.mybir as mybir
from concourse.bass_utils import run_bass_kernel_spmd

F32 = mybir.dt.float32
BF16 = mybir.dt.bfloat16
U16 = mybir.dt.uint16
U8 = mybir.dt.uint8
AF = mybir.ActivationFunctionType
ALU = mybir.AluOpType
AX = mybir.AxisListType

C_STEPS = 99
D = 2048
NSLAB = 8
NOWN = 4
NEXP = 32
ROT = 1024


class Sem:
    def __init__(self, nc, name):
        self._cm = nc.semaphore(name)
        self.h = self._cm.__enter__()
        self.n = 0


class KB:
    ENGS = ("pe", "dve", "act", "po", "sp")

    def __init__(self, nc):
        self.nc = nc
        self.q = {e: [] for e in self.ENGS}

    def w(self, eng, sem, val=None):
        v = sem.n if val is None else val
        self.q[eng].append(lambda E: E.wait_ge(sem.h, v))

    def o(self, eng, fn, inc=()):
        inc = tuple(inc)
        for s, k in inc:
            s.n += k

        def run(E):
            ins = fn(E)
            for s, k in inc:
                ins.then_inc(s.h, k)
        self.q[eng].append(run)

    def gate(self, sems):
        vals = [(s, s.n) for s in sems]
        for e in self.ENGS:
            for s, v in vals:
                self.w(e, s, v)


def build_nc(phlim=9):
    nc = bass.Bass("TRN2", target_bir_lowering=False)

    xb = nc.dram_tensor("xb", [D, 4096], F32, kind="ExternalInput").ap()
    win = nc.dram_tensor("win", [48, 128, 16, 128], F32, kind="ExternalInput").ap()
    binp = nc.dram_tensor("binp", [128, 48], F32, kind="ExternalInput").ap()
    wout = nc.dram_tensor("wout", [16, 128, 16, 128], F32, kind="ExternalInput").ap()
    boutp = nc.dram_tensor("boutp", [128, 16], F32, kind="ExternalInput").ap()
    gw = nc.dram_tensor("gw", [128, 16, 32], F32, kind="ExternalInput").ap()
    gup = nc.dram_tensor("gup", [32, 12, 128, 16, 128], F32, kind="ExternalInput").ap()
    gdn = nc.dram_tensor("gdn", [32, 128, 16, 6, 128], F32, kind="ExternalInput").ap()
    simm = nc.dram_tensor("simm", [32, 32], F32, kind="ExternalInput").ap()
    cost = nc.dram_tensor("cost", [512, 4096], F32, kind="ExternalInput").ap()
    sint = nc.dram_tensor("sint", [512, 4096], F32, kind="ExternalInput").ap()
    cst = nc.dram_tensor("cst", [128, 200], F32, kind="ExternalInput").ap()
    crow = nc.dram_tensor("crow", [1, 128], F32, kind="ExternalInput").ap()
    ehot = nc.dram_tensor("ehot", [32, 4096], F32, kind="ExternalInput").ap()
    colsel = nc.dram_tensor("colsel", [32, 256], F32, kind="ExternalInput").ap()
    out = nc.dram_tensor("out", [D, 2048], F32, kind="ExternalOutput").ap()

    woutb = nc.dram_tensor("woutb", [16, 128, 16, 128], BF16).ap()
    qd = nc.dram_tensor("qd", [16, NOWN, 128, 512], BF16).ap()
    kd = nc.dram_tensor("kd", [16, NSLAB, 128, 512], BF16).ap()
    vd = nc.dram_tensor("vd", [16, NSLAB, 128, 512], BF16).ap()
    x1f = nc.dram_tensor("x1f", [16, NOWN, 128, 512], F32).ap()
    x1bd = nc.dram_tensor("x1bd", [16, NOWN, 128, 512], BF16).ap()

    pmm = [nc.alloc_psum_tensor(f"pmm{i}", [128, 512], F32) for i in range(4)]
    pA = nc.alloc_psum_tensor("pA", [128, 1024], F32)
    pB = nc.alloc_psum_tensor("pB", [128, 1024], F32)

    alloc = nc.alloc_sbuf_tensor
    ident = alloc("ident", [128, 128], F32)
    onesrow = alloc("onesrow", [1, 128], F32)
    onescolb = alloc("onescolb", [128, 1], BF16)
    cst_sb = alloc("cst_sb", [128, 200], F32)
    bin_sb = alloc("bin_sb", [128, 48], F32)
    bout_sb = alloc("bout_sb", [128, 16], F32)
    gw_sb = alloc("gw_sb", [128, 16, 32], BF16)
    sim_f = alloc("sim_f", [32, 32], F32)
    sim_fx = alloc("sim_fx", [32, 32], F32)
    sim_b = alloc("sim_b", [32, 32], BF16)
    ones32 = alloc("ones32", [32, 32], F32)
    mask32 = alloc("mask32", [32, 32], U8)
    ehot_sb = alloc("ehot_sb", [32, 4096], BF16)
    colsel_sb = alloc("colsel_sb", [32, 256], BF16)
    A_acc = alloc("A_acc", [128, 16], F32)
    A_tmp = alloc("A_tmp", [128, 16], F32)
    comb_sb = alloc("comb_sb", [32, 2048], BF16)
    rowsc = alloc("rowsc", [1, 2048], F32)
    idxrow = alloc("idxrow", [1, 1024], F32)
    i8u = alloc("i8u", [128, 8], U16)
    thrU = alloc("thrU", [128, 4], U8)

    iotaP = cst_sb[:, 128:129]
    iota32 = cst_sb[:, 130:162]
    iota4 = cst_sb[:, 162:166]

    AREN_W = (nc.sbuf_bytes_remaining - 2048) // 4
    arena = alloc("arena", [128, AREN_W], F32)
    aren_f = arena[:, :]
    aren_b = arena[:, :].bitcast(BF16)

    def vf(off_b, shape):
        n = int(np.prod(shape[1:]))
        v = aren_f[:, off_b // 4: off_b // 4 + n]
        if len(shape) == 3:
            v = v.rearrange("p (a c) -> p a c", a=shape[1])
        return v

    def vb(off_b, shape):
        n = int(np.prod(shape[1:]))
        v = aren_b[:, off_b // 2: off_b // 2 + n]
        if len(shape) == 3:
            v = v.rearrange("p (a c) -> p a c", a=shape[1])
        return v

    AREN_B = AREN_W * 4

    K = KB(nc)
    dsp = Sem(nc, "dsp")
    dst0 = Sem(nc, "dst0")
    dst1 = Sem(nc, "dst1")
    dpre = Sem(nc, "dpre")
    dx = Sem(nc, "dx")
    dwa = Sem(nc, "dwa")
    dwb = Sem(nc, "dwb")
    dgu0 = Sem(nc, "dgu0")
    dgu1 = Sem(nc, "dgu1")
    ddn0 = Sem(nc, "ddn0")
    ddn1 = Sem(nc, "ddn1")
    dxb = Sem(nc, "dxb")
    dsx = Sem(nc, "dsx")
    dwcl = Sem(nc, "dwcl")
    dwcs = Sem(nc, "dwcs")
    pe = Sem(nc, "pe")
    dve = Sem(nc, "dve")
    ac = Sem(nc, "ac")
    po = Sem(nc, "po")
    ALLS = [dsp, dst0, dst1, dpre, dx, dwa, dwb, dgu0, dgu1, ddn0, ddn1,
            dxb, dsx, dwcl, dwcs, pe, dve, ac, po]

    def dma(eng, sem, outp, inp, slow=False):
        K.o(eng, lambda E, o=outp, i=inp, s=slow: E.dma_start(
            out=o, in_=i, allow_slow_non_contiguous=s), [(sem, 16)])

    dma("sp", dsp, ident[:, :], cst[:, 0:128])
    dma("sp", dsp, cst_sb[:, :], cst)
    dma("sp", dsp, onesrow[:, :], crow)
    dma("sp", dsp, bin_sb[:, :], binp)
    dma("sp", dsp, bout_sb[:, :], boutp)
    dma("sp", dsp, sim_f[:, :], simm)
    dma("po", dpre, onescolb[:, :], cst[:, 129:130], slow=True)
    dma("po", dpre, gw_sb[:, :, :], gw)
    dma("po", dpre, ehot_sb[:, :], ehot)
    dma("po", dpre, colsel_sb[:, :], colsel)
    pre_dsp, pre_dg = dsp.n, dpre.n

    K.w("dve", dsp, pre_dsp)
    K.o("dve", lambda E: E.memset(ones32[:, :], 1.0))
    K.o("dve", lambda E: E.tensor_scalar(mask32[:, :], ident[0:32, 0:32], 0.5,
                                         None, ALU.is_gt), [(dve, 1)])
    K.w("dve", dve)
    K.o("dve", lambda E: E.select(sim_fx[:, :], mask32[:, :], ones32[:, :],
                                  sim_f[:, :], add_drain=True), [(dve, 1)])
    K.w("dve", dve)
    K.o("dve", lambda E: E.tensor_copy(sim_b[:, :], sim_fx[:, :]), [(dve, 1)])
    K.o("dve", lambda E: E.memset(A_acc[:, :], 0.0), [(dve, 1)])

    aoff = 0
    winb = [vb(aoff + i * 16 * 128 * 2, [128, 16, 128]) for i in range(2)]
    aoff += 2 * 16 * 128 * 2
    evb = [vb(aoff + i * 512 * 2, [128, 512]) for i in range(2)]
    aoff += 2 * 512 * 2
    xall_own = vb(aoff, [128, 16, 2048])
    xall_par = vb(aoff + 16 * 2048 * 2, [128, 16, 2048])
    xall_own_end = aoff + 16 * 2048 * 2
    aoff += 16 * 4096 * 2
    assert aoff <= AREN_B, (aoff, AREN_B)

    def xslab(s):
        return (xall_own, s) if s < 4 else (xall_par, s - 4)
    a2off = xall_own_end
    Kb = vb(a2off, [128, 16, 512]); a2off += 16 * 512 * 2
    Vb = vb(a2off, [128, 16, 512]); a2off += 16 * 512 * 2
    cosb = vf(a2off, [128, 4, 512]); a2off += 4 * 512 * 4
    sinb = vf(a2off, [128, 4, 512]); a2off += 4 * 512 * 4
    Kr = vf(a2off, [128, 8, 512]); a2off += 8 * 512 * 4
    wcvt_off = a2off
    rt1 = vf(a2off, [128, 4, 512]); a2off += 4 * 512 * 4
    rt2 = vf(a2off, [128, 4, 512]); a2off += 4 * 512 * 4
    sqb = vb(a2off, [128, 16, 512]); a2off += 16 * 512 * 2
    kvsc = [vf(a2off + i * 512 * 4, [128, 512]) for i in range(2)]
    a2off += 2 * 512 * 4
    junk2 = [rt1[:, 0, :], rt1[:, 1, :]]
    assert a2off <= AREN_B, (a2off, AREN_B)
    wcvt = [vb(wcvt_off + i * 16 * 128 * 2, [128, 16, 128])
            for i in range(2)]

    kdr = kd.rearrange("ec s p t -> p ec s t")
    vdr = vd.rearrange("ec s p t -> p ec s t")
    costr = cost.rearrange("(cc p) t -> p cc t", p=128)
    sintr = sint.rearrange("(cc p) t -> p cc t", p=128)

    xbr = xb.rearrange("(dc p) t -> p dc t", p=128)
    for s in range(NSLAB):
        xt, sl = xslab(s)
        dma("po", dx, xt[:, :, 512 * sl:512 * (sl + 1)],
            xbr[:, :, 512 * s:512 * (s + 1)])

    plan = [(ec, list(range(NSLAB))) for ec in range(16, 48)] + \
           [(ec, list(range(NOWN))) for ec in range(16)]

    g = 0
    evac_dve_val = {}
    evstore_val = {}
    pe_after_iter = []
    dw = [dwa, dwb]
    dstp = [dst0, dst1]
    a2_state = {}

    def emit_a2_loads(s, kvst0, kvst1):
        K.w("sp", dst0, kvst0)
        K.w("sp", dst1, kvst1)
        K.w("sp", dsp, dsp.n)
        if s == 0:
            K.w("sp", dwcs, dwcs.n)
        if s >= 1:
            K.w("sp", dve, a2_state.get("consumed", 0))
            K.w("sp", ac, a2_state.get("ac_consumed", 0))
        dma("sp", dsp, Kb[:, :, :], kdr[:, :, s, :])
        dma("sp", dsp, Vb[:, :, :], vdr[:, :, s, :])
        dma("sp", dsp, cosb[:, :, :], costr[:, :, 512 * s:512 * s + 512])
        dma("sp", dsp, sinb[:, :, :], sintr[:, :, 512 * s:512 * s + 512])
        return dsp.n

    def emit_a2_compute(s, load_val):
        K.w("dve", dsp, load_val)
        K.w("po", dsp, load_val)
        if s == 0:
            K.w("dve", dwcs, dwcs.n)
        K.w("dve", dve)
        K.w("po", po)
        for c in range(4):
            K.o("dve", lambda E, c=c: E.tensor_mul(rt1[:, c, :], Kb[:, c, :],
                                                   cosb[:, c, :]))
            K.o("dve", lambda E, c=c: E.tensor_mul(rt2[:, c, :], Kb[:, c + 4, :],
                                                   sinb[:, c, :]),
                [(dve, 1)] if c == 3 else ())
        for c in range(4):
            K.o("po", lambda E, c=c: E.tensor_mul(Kr[:, c, :], Kb[:, c, :],
                                                  sinb[:, c, :]))
            K.o("po", lambda E, c=c: E.tensor_mul(Kr[:, c + 4, :],
                                                  Kb[:, c + 4, :],
                                                  cosb[:, c, :]),
                [(po, 1)] if c == 3 else ())
        K.w("dve", dve)
        K.w("dve", po, po.n)
        for c in range(4):
            K.o("dve", lambda E, c=c: E.tensor_add(Kr[:, c + 4, :],
                                                   Kr[:, c + 4, :],
                                                   Kr[:, c, :]),
                [(dve, 1)] if c == 3 else ())
        K.w("dve", dve)
        for c in range(4):
            K.o("dve", lambda E, c=c: E.tensor_sub(Kr[:, c, :], rt1[:, c, :],
                                                   rt2[:, c, :]),
                [(dve, 1)] if c == 3 else ())
        K.w("act", dve, dve.n)
        for c in range(16):
            src_ = Kr[:, c, :] if c < 8 else Kb[:, c, :]
            K.o("act", lambda E, s2=src_, c=c: E.activation(sqb[:, c, :], s2,
                                                            AF.Square),
                [(ac, 1)] if c == 15 else ())
        sq_done = ac.n
        K.w("pe", ac, sq_done)
        K.w("pe", dve, dve.n)
        K.w("pe", dpre, pre_dg)
        K.w("pe", dsp, pre_dsp)
        for c in range(16):
            K.o("pe", lambda E, c=c: E.matmul(pA[0:1, 0:512], onescolb[:, :],
                                              sqb[:, c, :], start=(c == 0),
                                              stop=(c == 15),
                                              skip_group_check=True),
                [(pe, 1)] if c == 15 else ())
        K.w("act", pe, pe.n)
        K.o("act", lambda E: E.activation(rowsc[0:1, 0:512], pA[0:1, 0:512],
                                          AF.Sqrt), [(ac, 1)])
        K.w("dve", ac, ac.n)
        K.o("dve", lambda E: E.tensor_scalar(rowsc[0:1, 512:1024],
                                             rowsc[0:1, 0:512], 1e-12, None,
                                             ALU.max), [(dve, 1)])
        K.w("dve", dve)
        K.o("dve", lambda E: E.reciprocal(rowsc[0:1, 0:512],
                                          rowsc[0:1, 512:1024]), [(dve, 1)])
        K.w("pe", dve, dve.n)
        K.o("pe", lambda E: E.matmul(pB[:, 0:512], onesrow[:, :],
                                     rowsc[0:1, 0:512], start=True, stop=True,
                                     skip_group_check=True),
            [(pe, 1)])
        K.w("po", po)
        K.w("po", dve, dve.n)
        K.w("dve", pe, pe.n)
        K.w("dve", dve)
        kv_val = {}
        stt_val = {}

        def emit_kv(c):
            ks = Kr[:, c, :] if c < 8 else Kb[:, c, :]
            if c >= 2:
                K.w("po", dve, stt_val[c - 2])
            K.o("po", lambda E, k2=ks, c=c: E.tensor_mul(kvsc[c % 2][:, :], k2,
                                                         Vb[:, c, :]),
                [(po, 1)])
            kv_val[c] = po.n

        def emit_stt(c):
            K.w("dve", po, kv_val[c])
            K.o("dve", lambda E, c=c: E.scalar_tensor_tensor(
                junk2[c % 2], kvsc[c % 2][:, :], 1.0, pB[:, 0:512],
                ALU.mult, ALU.mult, accum_out=A_tmp[:, c:c + 1]), [(dve, 1)])
            stt_val[c] = dve.n

        emit_kv(0)
        emit_kv(1)
        for c in range(16):
            emit_stt(c)
            if c + 2 < 16:
                emit_kv(c + 2)
        a2_state["consumed"] = dve.n
        a2_state["ac_consumed"] = sq_done
        K.w("po", dve, dve.n)
        K.o("po", lambda E: E.tensor_add(A_acc[:, :], A_acc[:, :], A_tmp[:, :]),
            [(po, 1)])

    a2_load_val = {}
    kv_st = [0, 0]
    wcs_val = {}
    wcl_val = {}
    for i, (ec, slabs) in enumerate(plan):
        if 9 <= i <= 24:
            K.w("sp", dwcl, wcl_val[i - 9])
            dma("sp", dwcs, woutb[i - 9], wcvt[(i - 9) % 2][:, :, :])
            wcs_val[i - 9] = dwcs.n
        buf = i % 2
        if i >= 2:
            K.w("po", pe, pe_after_iter[i - 2])
        K.w("po", dw[buf], dw[buf].n)
        dma("po", dw[buf], winb[buf][:, :, :], win[ec])
        if 8 <= i <= 23:
            sl_ = i - 8
            if sl_ >= 2:
                K.w("po", dwcs, wcs_val[sl_ - 2])
            dma("po", dwcl, wcvt[sl_ % 2][:, :, :], wout[sl_])
            wcl_val[sl_] = dwcl.n
        K.w("pe", dx, dx.n)
        K.w("pe", dw[buf], dw[buf].n)
        for s in slabs:
            bank = g % 4
            if g >= 4:
                K.w("pe", dve, evac_dve_val[g - 4])
            xt, sl = xslab(s)
            for kc in range(16):
                K.o("pe", lambda E, b=bank, bu=buf, k=kc, ss=sl, x2=xt: E.matmul(
                    pmm[b][:, :], winb[bu][:, k, :],
                    x2[:, k, 512 * ss:512 * ss + 512],
                    start=(k == 0), stop=(k == 15)),
                    [(pe, 1)] if kc == 15 else ())
            K.w("dve", pe, pe.n)
            sb_ = dstp[g % 2]
            if g >= 2:
                K.w("dve", sb_, evstore_val[g - 2])
            eb = evb[g % 2]
            K.o("dve", lambda E, b=bank, e=ec, ebuf=eb: E.tensor_scalar(
                ebuf[:, :], pmm[b][:, :], bin_sb[:, e:e + 1], None, ALU.add),
                [(dve, 1)])
            evac_dve_val[g] = dve.n
            K.w("act", dve, dve.n)
            if ec < 16:
                tgt = qd[ec, s]
            elif ec < 32:
                tgt = kd[ec - 16, s]
            else:
                tgt = vd[ec - 32, s]
            dma("act", sb_, tgt, eb[:, :])
            evstore_val[g] = sb_.n
            g += 1
        pe_after_iter.append(pe.n)
        if i == 31:
            kv_st = [dst0.n, dst1.n]
        if i >= 32:
            k2 = i - 32
            if k2 % 2 == 0 and k2 // 2 < NSLAB:
                a2_load_val[k2 // 2] = emit_a2_loads(k2 // 2, kv_st[0], kv_st[1])
            elif k2 % 2 == 1 and k2 // 2 < NSLAB:
                emit_a2_compute(k2 // 2, a2_load_val[k2 // 2])
    a1_dst0, a1_dst1 = dst0.n, dst1.n

    K.gate(ALLS)

    if phlim < 2:
        if phlim == 1.5:
            pass
        K.w("act", po, po.n)
        dma("act", dst0, out.rearrange("(dc p) t -> p dc t", p=128)[:, 0, 0:16],
            A_acc[:, :])
        K.gate(ALLS)
        return _finish(nc, K)
    boff = 0
    wob = [vb(boff + i * 16 * 128 * 2, [128, 16, 128]) for i in range(2)]
    boff += 2 * 16 * 128 * 2
    Qb = vb(boff, [128, 16, 512]); boff += 16 * 512 * 2
    Qr = vf(boff, [128, 8, 512]); boff += 8 * 512 * 4
    qa = vb(boff, [128, 16, 512]); boff += 16 * 512 * 2
    cosb2 = vf(boff, [128, 4, 512]); boff += 4 * 512 * 4
    sinb2 = vf(boff, [128, 4, 512]); boff += 4 * 512 * 4
    rtb1 = vf(boff, [128, 4, 512])
    sqb2 = vb(boff, [128, 16, 512])
    boff += 4 * 512 * 4
    rtb2 = vf(boff, [128, 4, 512]); boff += 4 * 512 * 4
    Xf = vf(boff, [128, 16, 512]); boff += 16 * 512 * 4
    out1 = vf(boff, [128, 16, 512]); boff += 16 * 512 * 4
    o1b = vb(boff, [128, 16, 512]); boff += 16 * 512 * 2
    assert boff <= AREN_B, (boff, AREN_B)

    qdr = qd.rearrange("ec s p t -> p ec s t")
    x1fr = x1f.rearrange("ec s p t -> p ec s t")
    x1br = x1bd.rearrange("ec s p t -> p ec s t")

    def emit_ln(src3, bconv, sqv, eps=1e-5):
        K.w("act", dve, dve.n)
        for c in range(16):
            K.o("act", lambda E, c=c: E.activation(sqv[:, c, :], src3[:, c, :],
                                                   AF.Square),
                [(ac, 1)] if c == 15 else ())
        K.w("dve", dve)
        for c in range(16):
            K.o("dve", lambda E, c=c: E.tensor_copy(bconv[:, c, :],
                                                    src3[:, c, :]),
                [(dve, 1)] if c == 15 else ())
        K.w("pe", ac, ac.n)
        K.w("pe", dve, dve.n)
        for c in range(16):
            K.o("pe", lambda E, c=c: E.matmul(pA[0:1, 0:512], onescolb[:, :],
                                              bconv[:, c, :], start=(c == 0),
                                              stop=(c == 15)),
                [(pe, 1)] if c == 15 else ())
        for c in range(16):
            K.o("pe", lambda E, c=c: E.matmul(pA[0:1, 512:1024], onescolb[:, :],
                                              sqv[:, c, :], start=(c == 0),
                                              stop=(c == 15)),
                [(pe, 1)] if c == 15 else ())
        K.w("dve", pe, pe.n)
        K.o("dve", lambda E: E.tensor_scalar(rowsc[0:1, 0:512], pA[0:1, 0:512],
                                             1.0 / D, None, ALU.mult), [(dve, 1)])
        K.w("act", dve, dve.n)
        K.o("act", lambda E: E.activation(rowsc[0:1, 512:1024], rowsc[0:1, 0:512],
                                          AF.Square), [(ac, 1)])
        K.w("dve", ac, ac.n)
        K.o("dve", lambda E: E.scalar_tensor_tensor(
            rowsc[0:1, 1024:1536], pA[0:1, 512:1024], 1.0 / D,
            rowsc[0:1, 512:1024], ALU.mult, ALU.subtract), [(dve, 1)])
        K.w("dve", dve)
        K.o("dve", lambda E: E.tensor_scalar(rowsc[0:1, 512:1024],
                                             rowsc[0:1, 1024:1536], eps, None,
                                             ALU.add), [(dve, 1)])
        K.w("act", dve, dve.n)
        K.o("act", lambda E: E.activation(rowsc[0:1, 1024:1536],
                                          rowsc[0:1, 512:1024], AF.Sqrt),
            [(ac, 1)])
        K.w("dve", ac, ac.n)
        K.o("dve", lambda E: E.reciprocal(rowsc[0:1, 1536:2048],
                                          rowsc[0:1, 1024:1536]), [(dve, 1)])
        K.w("pe", dve, dve.n)
        K.o("pe", lambda E: E.matmul(pB[:, 0:512], onesrow[:, :],
                                     rowsc[0:1, 0:512], start=True, stop=True))
        K.o("pe", lambda E: E.matmul(pB[:, 512:1024], onesrow[:, :],
                                     rowsc[0:1, 1536:2048], start=True,
                                     stop=True), [(pe, 1)])
        K.w("dve", pe, pe.n)
        for c in range(16):
            K.o("dve", lambda E, c=c: E.tensor_sub(src3[:, c, :], src3[:, c, :],
                                                   pB[:, 0:512]),
                [(dve, 1)] if c == 15 else ())
        K.w("dve", dve)
        for c in range(16):
            K.o("dve", lambda E, c=c: E.tensor_mul(src3[:, c, :], src3[:, c, :],
                                                   pB[:, 512:1024]),
                [(dve, 1)] if c == 15 else ())

    wout_g = 0
    woutev = {}
    wo_plan_pe = []
    for tt in range(NOWN):
        K.w("sp", dst0, a1_dst0)
        K.w("sp", dst1, a1_dst1)
        K.w("sp", dsp, dsp.n)
        if tt >= 1:
            K.w("sp", dve, dve.n)
            K.w("sp", ac, ac.n)
        dma("sp", dsp, Qb[:, :, :], qdr[:, :, tt, :])
        dma("sp", dsp, cosb2[:, :, :], costr[:, :, 512 * tt:512 * tt + 512])
        dma("sp", dsp, sinb2[:, :, :], sintr[:, :, 512 * tt:512 * tt + 512])
        dma("sp", dsp, Xf[:, :, :], xbr[:, :, 512 * tt:512 * tt + 512])
        K.w("dve", dsp, dsp.n)
        K.w("dve", ac, ac.n)
        K.w("dve", dst0, dst0.n)
        K.w("dve", dxb, dxb.n)
        for c in range(4):
            K.o("dve", lambda E, c=c: E.tensor_mul(rtb1[:, c, :], Qb[:, c, :],
                                                   cosb2[:, c, :]))
            K.o("dve", lambda E, c=c: E.tensor_mul(rtb2[:, c, :], Qb[:, c + 4, :],
                                                   sinb2[:, c, :]),
                [(dve, 1)] if c == 3 else ())
        K.w("dve", dve)
        for c in range(4):
            K.o("dve", lambda E, c=c: E.tensor_sub(Qr[:, c, :], rtb1[:, c, :],
                                                   rtb2[:, c, :]),
                [(dve, 1)] if c == 3 else ())
        K.w("dve", dve)
        for c in range(4):
            K.o("dve", lambda E, c=c: E.tensor_mul(rtb1[:, c, :], Qb[:, c, :],
                                                   sinb2[:, c, :]))
            K.o("dve", lambda E, c=c: E.tensor_mul(rtb2[:, c, :], Qb[:, c + 4, :],
                                                   cosb2[:, c, :]),
                [(dve, 1)] if c == 3 else ())
        K.w("dve", dve)
        for c in range(4):
            K.o("dve", lambda E, c=c: E.tensor_add(Qr[:, c + 4, :], rtb2[:, c, :],
                                                   rtb1[:, c, :]),
                [(dve, 1)] if c == 3 else ())
        K.w("dve", dve)
        for c in range(16):
            src = Qr[:, c, :] if c < 8 else Qb[:, c, :]
            K.o("dve", lambda E, s_=src, c=c: E.tensor_scalar(
                qa[:, c, :], s_, A_acc[:, c:c + 1], None, ALU.mult),
                [(dve, 1)] if c == 15 else ())
        qa_done = dve.n
        for ec in range(16):
            wbuf = ec % 2
            widx = tt * 16 + ec
            if widx >= 2:
                K.w("po", pe, wo_plan_pe[widx - 2])
            K.w("po", dw[wbuf], dw[wbuf].n)
            dma("po", dw[wbuf], wob[wbuf][:, :, :], woutb[ec])
            bank = wout_g % 4
            K.w("pe", dw[wbuf], dw[wbuf].n)
            K.w("pe", dve, qa_done)
            if wout_g >= 4:
                K.w("pe", dve, woutev[wout_g - 4])
            for kc in range(16):
                K.o("pe", lambda E, b=bank, wb=wbuf, k=kc: E.matmul(
                    pmm[b][:, :], wob[wb][:, k, :], qa[:, k, :],
                    start=(k == 0), stop=(k == 15)),
                    [(pe, 1)] if kc == 15 else ())
            wo_plan_pe.append(pe.n)
            K.w("dve", pe, pe.n)
            K.o("dve", lambda E, b=bank, e=ec: E.scalar_tensor_tensor(
                out1[:, e, :], pmm[b][:, :], bout_sb[:, e:e + 1], Xf[:, e, :],
                ALU.add, ALU.add), [(dve, 1)])
            K.w("dve", dve)
            K.o("dve", lambda E, e=ec: E.tensor_add(out1[:, e, :],
                                                    out1[:, e, :], Xf[:, e, :]),
                [(dve, 1)])
            woutev[wout_g] = dve.n
            wout_g += 1
        emit_ln(out1, o1b, sqb2)
        K.w("act", dve, dve.n)
        K.w("act", dst0, dst0.n)
        dma("act", dst0, x1fr[:, :, tt, :], out1[:, :, :])
        K.w("po", dve, dve.n)
        K.w("po", dxb, dxb.n)
        dma("po", dxb, x1br[:, :, tt, :], out1[:, :, :])

    b_dst0 = dst0.n
    b_dxb = dxb.n
    K.gate(ALLS)

    if phlim < 4:
        K.w("sp", dst0, dst0.n)
        K.w("sp", dxb, dxb.n)
        _or = out.rearrange("(dc p) t -> p dc t", p=128)
        _xr = x1f.rearrange("ec s p t -> p ec s t")
        for t4 in range(4):
            K.o("sp", lambda E, t4=t4, o2=_or, x2=_xr: E.dma_start(
                out=o2[:, :, 512 * t4:512 * t4 + 512],
                in_=x2[:, :, t4, :]), [(dsp, 16)])
        K.gate(ALLS)
        return _finish(nc, K)
    doff = 0
    ysb = vf(doff, [128, 16, 1024]); doff += 16 * 1024 * 4
    x1h = vb(doff, [128, 16, 1024]); doff += 16 * 1024 * 2
    ebase = doff
    hsb = vb(doff, [128, 6, 1024]); doff += 6 * 1024 * 2
    gub = [vb(doff + i * 2 * 16 * 128 * 2, [128, 2, 16 * 128]) for i in range(2)]
    doff += 2 * 2 * 16 * 128 * 2
    dnb = [vb(doff + i * 8 * 6 * 128 * 2, [128, 8 * 6 * 128]) for i in range(2)]
    doff += 2 * 8 * 6 * 128 * 2
    dexp_end = doff
    cwsb = vf(doff, [128, 1024]); doff += 1024 * 4
    hg = vf(doff, [128, 2, 512]); doff += 2 * 512 * 4
    hx = vf(doff, [128, 2, 512]); doff += 2 * 512 * 4
    assert doff <= AREN_B, (doff, AREN_B)
    ybd = vb(ebase, [128, 16, 512])
    sqd = vb(ebase + 16 * 512 * 2, [128, 16, 512])
    assert ebase + 2 * 16 * 512 * 2 <= dexp_end

    outr = out.rearrange("(dc p) t -> p dc t", p=128)
    gu_dg = {}
    gu_pe = {}
    dn_dg = {}
    dn_pe = {}
    dn_g = 0
    ydve = {}
    dgu = [dgu0, dgu1]
    ddn = [ddn0, ddn1]

    def emit_gu_dma(hh_, si, guard_val):
        e_, c_ = divmod(si, 6)
        if guard_val is not None:
            K.w("po", pe, guard_val)
        sgu = dgu[si % 2]
        K.w("po", sgu, sgu.n)
        for gi in range(2):
            fc = c_ + 6 * gi
            K.o("po", lambda E, e=e_, f=fc, b=si % 2, gi=gi: E.dma_start(
                out=gub[b][:, gi, :].rearrange("p (dc j) -> p dc j", dc=16),
                in_=gup[e, f]), [(sgu, 16)])
        gu_dg[(hh_, si)] = (sgu, sgu.n)

    def emit_dn_dma(hh_, e_, dh, guard_val):
        if guard_val is not None:
            K.w("po", pe, guard_val)
        sdn = ddn[dh]
        K.w("po", sdn, sdn.n)
        K.o("po", lambda E, e=e_, d=dh: E.dma_start(
            out=dnb[d][:, :].rearrange("p (dc fc j) -> p dc fc j",
                                       dc=8, fc=6),
            in_=gdn[e][:, 8 * d:8 * d + 8]), [(sdn, 16)])
        dn_dg[(hh_, e_, dh)] = (sdn, sdn.n)

    def emit_x1h_loads(hh_):
        tt0_, tt1_ = 2 * hh_, 2 * hh_ + 1
        K.w("sp", dst0, b_dst0)
        K.w("sp", dxb, b_dxb)
        K.w("sp", dsx, dsx.n)
        dma("sp", dsx, x1h[:, :, 0:512], x1br[:, :, tt0_, :])
        dma("sp", dsx, x1h[:, :, 512:1024], x1br[:, :, tt1_, :])
        return dsx.n

    def emit_y_loads(hh_):
        tt0_, tt1_ = 2 * hh_, 2 * hh_ + 1
        K.w("sp", dst0, b_dst0)
        K.w("sp", dsp, dsp.n)
        dma("sp", dsp, ysb[:, :, 0:512], x1fr[:, :, tt0_, :])
        dma("sp", dsp, ysb[:, :, 512:1024], x1fr[:, :, tt1_, :])
        return dsp.n

    pre0 = emit_x1h_loads(0)
    emit_gu_dma(0, 0, None)
    emit_gu_dma(0, 1, None)
    emit_dn_dma(0, 0, 0, None)
    emit_dn_dma(0, 0, 1, None)

    _co = [0]

    def cvf(shape):
        v = vf(_co[0], shape)
        n = int(np.prod(shape[1:])) * 4
        _co[0] += (n + 63) & ~63
        return v

    def cvb(shape):
        v = vb(_co[0], shape)
        n = int(np.prod(shape[1:])) * 2
        _co[0] += (n + 63) & ~63
        return v

    x1t = cvb([128, 16, 512])
    lgf = cvf([128, 512])
    ltok = cvf([128, 32])
    v8 = cvf([128, 8])
    idxf = cvf([128, 8])
    nv0 = cvf([128, 1])
    ev8 = cvf([128, 8])
    s8 = cvf([128, 1])
    r8 = cvf([128, 1])
    rw8 = cvf([128, 8])
    oh = cvb([128, 8 * 128])
    Mt = cvb([128, 16 * 128])
    scf = cvf([128, 128])
    sct = cvf([128, 16])
    bs4 = cvf([128, 4])
    cand = cvf([128, 16])
    am4 = cvf([128, 4])
    ohm = cvf([128, 16])
    pm16 = cvf([128, 16])
    bp4 = cvf([128, 4])
    rer8 = cvf([128, 8])
    ohc = cvf([128, 32 * 8])
    ohc2 = cvf([128, 32 * 8])
    combt = cvf([128, 32])
    assert _co[0] <= AREN_B, (_co[0], AREN_B)

    ohv = oh.rearrange("p (k t) -> p k t", k=8)[0:32, :, :]
    mtv = Mt.rearrange("p (r t) -> p r t", r=16)[0:32, :, :]
    sct3 = sct.rearrange("p (j i) -> p j i", i=4)
    cand3 = cand.rearrange("p (j i) -> p j i", i=4)
    ohm3 = ohm.rearrange("p (j i) -> p j i", i=4)
    pm3 = pm16.rearrange("p (j i) -> p j i", i=4)
    ohc3 = ohc.rearrange("p (e k) -> p e k", k=8)
    ohc23 = ohc2.rearrange("p (e k) -> p e k", k=8)

    for tt in range(NOWN):
        K.w("sp", dxb, b_dxb)
        K.w("sp", dsp, dsp.n)
        if tt >= 1:
            K.w("sp", pe, pe.n)
        dma("sp", dsp, x1t[:, :, :], x1br[:, :, tt, :])
        K.w("pe", dsp, dsp.n)
        K.w("pe", dpre, pre_dg)
        if tt >= 1:
            K.w("pe", dve, dve.n)
        for kc in range(16):
            K.o("pe", lambda E, k=kc: E.matmul(pmm[0][0:32, :], gw_sb[:, k, :],
                                               x1t[:, k, :], start=(k == 0),
                                               stop=(k == 15)),
                [(pe, 1)] if kc == 15 else ())
        K.w("dve", pe, pe.n)
        K.o("dve", lambda E: E.tensor_copy(lgf[0:32, :], pmm[0][0:32, :]),
            [(dve, 1)])
        for j in range(4):
            if C_STEPS < 2:
                continue
            K.w("pe", dve, dve.n)
            K.o("pe", lambda E, j=j: E.transpose(
                pmm[1][:, 0:32], lgf[0:32, 128 * j:128 * j + 128],
                ident[0:32, 0:32]), [(pe, 1)])
            K.w("dve", pe, pe.n)
            K.o("dve", lambda E: E.tensor_copy(ltok[:, :], pmm[1][:, 0:32]),
                [(dve, 1)])
            if C_STEPS < 3:
                continue
            K.w("dve", dve)
            K.o("dve", lambda E: E.max(out=v8[:, :], in_=ltok[:, :]), [(dve, 1)])
            K.w("dve", dve)
            K.o("dve", lambda E: E.max_index(i8u[:, :], v8[:, :], ltok[:, :]),
                [(dve, 1)])
            K.w("dve", dve)
            K.o("dve", lambda E: E.tensor_copy(idxf[:, :], i8u[:, :]))
            K.o("dve", lambda E: E.tensor_scalar(nv0[:, :], v8[:, 0:1], -1.0,
                                                 None, ALU.mult), [(dve, 1)])
            if C_STEPS < 4:
                continue
            K.w("act", dve, dve.n)
            import os as _os
            if _os.environ.get("CVAR") == "sq":
                K.o("act", lambda E: E.activation(ev8[:, :], v8[:, :],
                                                  AF.Square), [(ac, 1)])
            elif _os.environ.get("CVAR") == "nobias":
                K.o("act", lambda E: E.activation(ev8[:, :], v8[:, :],
                                                  AF.Exp), [(ac, 1)])
            else:
                K.o("act", lambda E: E.activation(ev8[:, :], v8[:, :], AF.Exp,
                                                  bias=nv0[:, :]), [(ac, 1)])
            K.w("dve", ac, ac.n)
            K.o("dve", lambda E: E.reduce_sum(s8[:, :], ev8[:, :], axis=AX.X),
                [(dve, 1)])
            K.w("dve", dve)
            K.o("dve", lambda E: E.reciprocal(r8[:, :], s8[:, :]), [(dve, 1)])
            K.w("dve", dve)
            K.o("dve", lambda E: E.tensor_scalar(rw8[:, :], ev8[:, :], r8[:, :],
                                                 None, ALU.mult), [(dve, 1)])
            if C_STEPS < 5:
                continue
            K.w("pe", dve, dve.n)
            for k in range(8):
                bank = k % 4
                if k >= 4:
                    K.w("pe", dve, dve.n)
                K.o("pe", lambda E, k=k, b=bank: E.transpose(
                    pmm[b][0:1, 0:128], idxf[:, k:k + 1],
                    ident[:, :]), [(pe, 1)])
                K.w("dve", pe, pe.n)
                K.o("dve", lambda E, k=k, b=bank: E.tensor_copy(
                    idxrow[0:1, 128 * k:128 * k + 128],
                    pmm[b][0:1, 0:128]), [(dve, 1)])
            if C_STEPS < 6:
                continue
            K.w("pe", dve, dve.n)
            K.o("pe", lambda E: E.matmul(pA[0:32, 0:512], onesrow[0:1, 0:32],
                                         idxrow[0:1, 0:512], start=True,
                                         stop=True))
            K.o("pe", lambda E: E.matmul(pA[0:32, 512:1024], onesrow[0:1, 0:32],
                                         idxrow[0:1, 512:1024], start=True,
                                         stop=True), [(pe, 1)])
            K.w("dve", pe, pe.n)
            K.o("dve", lambda E: E.tensor_scalar(
                ohv, pA[0:32, 0:1024].rearrange("p (k t) -> p k t", k=8),
                iotaP[0:32, :], None, ALU.is_equal), [(dve, 1)])
            if C_STEPS < 7:
                continue
            K.w("pe", dve, dve.n)
            for j2 in range(4):
                K.o("pe", lambda E, j2=j2: E.matmul(
                    pB[0:32, 128 * j2:128 * j2 + 128], sim_b[:, :],
                    ohv[:, 4 + j2, :], start=True, stop=True),
                    [(pe, 1)] if j2 == 3 else ())
            K.w("dve", pe, pe.n)
            for j2 in range(4):
                for i2 in range(4):
                    K.o("dve", lambda E, j2=j2, i2=i2: E.tensor_mul(
                        mtv[:, 4 * j2 + i2, :], ohv[:, i2, :],
                        pB[0:32, 128 * j2:128 * j2 + 128]),
                        [(dve, 1)] if (j2 == 3 and i2 == 3) else ())
            K.w("pe", dve, dve.n)
            for r in range(16):
                K.o("pe", lambda E, r=r: E.matmul(
                    pmm[3][0:16, 0:128], colsel_sb[:, 16 * r:16 * r + 16],
                    mtv[:, r, :], start=(r == 0), stop=(r == 15)),
                    [(pe, 1)] if r == 15 else ())
            K.w("dve", pe, pe.n)
            K.o("dve", lambda E: E.tensor_copy(scf[0:16, :], pmm[3][0:16, 0:128]),
                [(dve, 1)])
            K.w("pe", dve, dve.n)
            K.o("pe", lambda E: E.transpose(pmm[2][:, 0:16], scf[0:16, :],
                                            ident[0:16, 0:16]), [(pe, 1)])
            K.w("dve", pe, pe.n)
            K.o("dve", lambda E: E.tensor_copy(sct[:, :], pmm[2][:, 0:16]),
                [(dve, 1)])
            if C_STEPS < 8:
                continue
            K.w("dve", dve)
            K.o("dve", lambda E: E.reduce_max(bs4[:, :], sct3, axis=AX.X),
                [(dve, 1)])
            K.w("dve", dve)
            bs_b = bs4.unsqueeze(2).to_broadcast([128, 4, 4])
            K.o("dve", lambda E: E.tensor_tensor(cand3, sct3, bs_b, ALU.is_lt),
                [(dve, 1)])
            K.w("dve", dve)
            iota4_b = iota4.unsqueeze(1).to_broadcast([128, 4, 4])
            K.o("dve", lambda E: E.scalar_tensor_tensor(
                cand3, cand3, 8.0, iota4_b, ALU.mult, ALU.add), [(dve, 1)])
            K.w("dve", dve)
            K.o("dve", lambda E: E.tensor_reduce(am4[:, :], cand3, AX.X,
                                                 ALU.min), [(dve, 1)])
            K.w("dve", dve)
            am_b = am4.unsqueeze(2).to_broadcast([128, 4, 4])
            K.o("dve", lambda E: E.tensor_tensor(ohm3, am_b, iota4_b,
                                                 ALU.is_equal), [(dve, 1)])
            K.w("dve", dve)
            prim_b = idxf[:, 0:4].unsqueeze(1).to_broadcast([128, 4, 4])
            K.o("dve", lambda E: E.tensor_mul(pm3, ohm3, prim_b), [(dve, 1)])
            K.w("dve", dve)
            K.o("dve", lambda E: E.reduce_sum(bp4[:, :], pm3, axis=AX.X),
                [(dve, 1)])
            K.w("dve", dve)
            K.o("dve", lambda E: E.tensor_scalar(thrU[:, :], bs4[:, :], 0.5,
                                                 None, ALU.is_lt), [(dve, 1)])
            K.w("dve", dve)
            K.o("dve", lambda E: E.tensor_copy(rer8[:, 0:4], idxf[:, 0:4]),
                [(dve, 1)])
            K.w("dve", dve)
            K.o("dve", lambda E: E.select(rer8[:, 4:8], thrU[:, :],
                                          idxf[:, 4:8], bp4[:, :],
                                          add_drain=True), [(dve, 1)])
            if C_STEPS < 9:
                continue
            K.w("dve", dve)
            iotaE_b = iota32.unsqueeze(2).to_broadcast([128, 32, 8])
            rer_b = rer8.unsqueeze(1).to_broadcast([128, 32, 8])
            K.o("dve", lambda E: E.tensor_tensor(ohc3, iotaE_b, rer_b,
                                                 ALU.is_equal), [(dve, 1)])
            K.w("dve", dve)
            rw_b = rw8.unsqueeze(1).to_broadcast([128, 32, 8])
            K.o("dve", lambda E: E.tensor_mul(ohc23, ohc3, rw_b), [(dve, 1)])
            K.w("dve", dve)
            K.o("dve", lambda E: E.reduce_sum(combt[:, :], ohc23, axis=AX.X),
                [(dve, 1)])
            K.w("pe", dve, dve.n)
            K.o("pe", lambda E: E.transpose(pmm[1][0:32, 0:128], combt[:, :],
                                            ident[:, :]), [(pe, 1)])
            K.w("dve", pe, pe.n)
            col = 512 * tt + 128 * j
            K.o("dve", lambda E, c=col: E.tensor_copy(
                comb_sb[:, c:c + 128], pmm[1][0:32, 0:128]), [(dve, 1)])

    K.gate(ALLS)

    if phlim < 5:
        K.w("po", dve, dve.n)
        K.o("po", lambda E: E.dma_start(
            out=out[0:32, :], in_=comb_sb[:, :]), [(dxb, 16)])
        K.gate(ALLS)
        return _finish(nc, K)

    pre1 = None
    y_next = None
    for hh in range(2):
        tt0, tt1 = 2 * hh, 2 * hh + 1
        if hh == 0:
            x1_ready = pre0
            y_ready = emit_y_loads(0)
        else:
            x1_ready = pre1
            y_ready = y_next
            emit_gu_dma(hh, 0, None)
            emit_gu_dma(hh, 1, None)
            emit_dn_dma(hh, 0, 0, None)
            emit_dn_dma(hh, 0, 1, None)

        prev_y_add = dve.n
        for e in range(NEXP):
            K.w("pe", dsx, x1_ready)
            if e >= 1:
                K.w("pe", po, po.n)
            cw0 = 1024 * hh
            K.o("pe", lambda E, c0=cw0, e=e: E.matmul(
                pA[:, 0:512], ehot_sb[:, 128 * e:128 * e + 128],
                comb_sb[:, c0:c0 + 512], start=True, stop=True))
            K.o("pe", lambda E, c0=cw0, e=e: E.matmul(
                pA[:, 512:1024], ehot_sb[:, 128 * e:128 * e + 128],
                comb_sb[:, c0 + 512:c0 + 1024], start=True, stop=True),
                [(pe, 1)])
            K.w("dve", pe, pe.n)
            K.o("dve", lambda E: E.tensor_copy(cwsb[:, 0:512], pA[:, 0:512]))
            K.o("dve", lambda E: E.tensor_copy(cwsb[:, 512:1024],
                                               pA[:, 512:1024]), [(dve, 1)])
            cw_done = dve.n
            for c in range(6):
                si = e * 6 + c
                K.w("pe", gu_dg[(hh, si)][0], gu_dg[(hh, si)][1])
                if (hh, e, c) != (0, 0, 0):
                    K.w("pe", ac, ac.n)
                for tt2 in range(2):
                    for kc in range(16):
                        K.o("pe", lambda E, b=tt2, k=kc, t2=tt2,
                            bu=c % 2: E.matmul(
                                pmm[b][:, :],
                                gub[bu][:, 0, :].rearrange(
                                    "p (dc j) -> p dc j", dc=16)[:, k, :],
                                x1h[:, k, 512 * t2:512 * t2 + 512],
                                start=(k == 0), stop=(k == 15)),
                            [(pe, 1)] if kc == 15 else ())
                g_pe = pe.n
                if (hh, e, c) != (0, 0, 0):
                    K.w("pe", dve, dve.n)
                for tt2 in range(2):
                    for kc in range(16):
                        K.o("pe", lambda E, b=2 + tt2, k=kc, t2=tt2,
                            bu=c % 2: E.matmul(
                                pmm[b][:, :],
                                gub[bu][:, 1, :].rearrange(
                                    "p (dc j) -> p dc j", dc=16)[:, k, :],
                                x1h[:, k, 512 * t2:512 * t2 + 512],
                                start=(k == 0), stop=(k == 15)),
                            [(pe, 1)] if kc == 15 else ())
                gu_pe[(hh, si)] = pe.n
                if si + 2 < NEXP * 6:
                    emit_gu_dma(hh, si + 2, gu_pe[(hh, si)])
                K.w("act", pe, g_pe)
                K.w("act", dve, dve.n)
                for tt2 in range(2):
                    K.o("act", lambda E, t2=tt2: E.activation(
                        hg[:, t2, :], pmm[t2][:, :], AF.Silu),
                        [(ac, 1)] if tt2 == 1 else ())
                K.w("dve", ac, ac.n)
                K.w("dve", pe, pe.n)
                K.w("dve", po, po.n)
                for tt2 in range(2):
                    K.o("dve", lambda E, t2=tt2: E.tensor_mul(
                        hx[:, t2, :], hg[:, t2, :], pmm[2 + t2][:, :]),
                        [(dve, 1)] if tt2 == 1 else ())
                K.w("po", dve, dve.n)
                K.w("po", dve, cw_done)
                for tt2 in range(2):
                    K.o("po", lambda E, t2=tt2, c=c: E.tensor_mul(
                        hsb[:, c, 512 * t2:512 * t2 + 512], hx[:, t2, :],
                        cwsb[:, 512 * t2:512 * t2 + 512]),
                        [(po, 1)] if tt2 == 1 else ())
            h_done = po.n
            K.w("pe", po, h_done)
            K.w("dve", dve, prev_y_add)
            if e == 0:
                K.w("dve", dsp, y_ready)
            for dh in range(2):
                K.w("pe", dn_dg[(hh, e, dh)][0], dn_dg[(hh, e, dh)][1])
                for dcl in range(8):
                    dc = 8 * dh + dcl
                    for tt2 in range(2):
                        q = dn_g % 2
                        dtgt = pB[:, 512 * q:512 * q + 512]
                        if dn_g >= 2:
                            K.w("pe", dve, ydve[dn_g - 2])
                        for fc in range(6):
                            K.o("pe", lambda E, dt=dtgt, b=dh, dl=dcl, f=fc,
                                t2=tt2: E.matmul(
                                    dt,
                                    dnb[b][:, :].rearrange(
                                        "p (dc fc j) -> p dc fc j", dc=8,
                                        fc=6)[:, dl, f, :],
                                    hsb[:, f, 512 * t2:512 * t2 + 512],
                                    start=(f == 0), stop=(f == 5)),
                                [(pe, 1)] if fc == 5 else ())
                        K.w("dve", pe, pe.n)
                        K.o("dve", lambda E, dt=dtgt, d=dc, t2=tt2: E.tensor_add(
                            ysb[:, d, 512 * t2:512 * t2 + 512],
                            ysb[:, d, 512 * t2:512 * t2 + 512],
                            dt), [(dve, 1)])
                        ydve[dn_g] = dve.n
                        dn_g += 1
                dn_pe[(hh, e, dh)] = pe.n
                if e + 1 < NEXP:
                    emit_dn_dma(hh, e + 1, dh, dn_pe[(hh, e, dh)])
            prev_y_add = dve.n

        if hh == 0:
            K.w("sp", pe, pe.n)
            pre1 = emit_x1h_loads(1)
        for tt2 in range(2):
            yv = ysb[:, :, 512 * tt2:512 * tt2 + 512]
            K.w("dve", dve)
            K.w("act", dve, dve.n)
            K.w("pe", dve, dve.n)
            for s_ in (dgu0, dgu1, ddn0, ddn1):
                K.w("pe", s_, s_.n)
            emit_ln(yv, ybd, sqd)
            K.w("act", dve, dve.n)
            K.w("act", dst0, dst0.n)
            tcol = 1024 * hh + 512 * tt2
            dma("act", dst0, outr[:, :, tcol:tcol + 512], yv)
        if hh == 0:
            K.w("sp", dst0, dst0.n)
            y_next = emit_y_loads(1)
            K.gate([s for s in ALLS if s not in
                    (dsp, dsx, dst0, dgu0, dgu1, ddn0, ddn1)])
        else:
            K.gate(ALLS)

    return _finish(nc, K)


def _finish(nc, K):
    with nc.Block() as block:
        @block.tensor
        def _(tensor):
            for f in K.q["pe"]:
                f(tensor)

        @block.vector
        def _(vector):
            for f in K.q["dve"]:
                f(vector)

        @block.scalar
        def _(scalar):
            for f in K.q["act"]:
                f(scalar)

        @block.gpsimd
        def _(gpsimd):
            for f in K.q["po"]:
                f(gpsimd)

        @block.sync
        def _(sync):
            for f in K.q["sp"]:
                f(sync)
    return nc



def _host_prep(W_in, b_in, W_out, b_out, gate_w, gate_up, down, similarity):
    perm = np.concatenate([np.arange(0, ROT, 2), np.arange(1, ROT, 2),
                           np.arange(ROT, D)]).astype(np.int64)
    rows = np.concatenate([perm, D + perm, 2 * D + perm])
    W_in_p = np.asarray(W_in)[rows]
    b_in_p = np.asarray(b_in)[rows]
    win_t = np.ascontiguousarray(
        W_in_p.reshape(48, 128, 16, 128).transpose(0, 3, 2, 1))
    bin_a = np.ascontiguousarray(b_in_p.reshape(48, 128).T)
    WoutT_p = np.ascontiguousarray(np.asarray(W_out).T[perm])
    wout_t = np.ascontiguousarray(
        WoutT_p.reshape(16, 128, 16, 128).transpose(2, 1, 0, 3))
    bout_a = np.ascontiguousarray(np.asarray(b_out).reshape(16, 128).T)
    gw_t = np.ascontiguousarray(
        np.asarray(gate_w).T.reshape(16, 128, 32).transpose(1, 0, 2))
    gup_t = np.ascontiguousarray(
        np.asarray(gate_up).reshape(32, 12, 128, 16, 128)
        .transpose(0, 1, 4, 3, 2))
    gdn_t = np.ascontiguousarray(
        np.asarray(down).reshape(32, 16, 128, 6, 128)
        .transpose(0, 4, 1, 3, 2))
    sim_a = np.ascontiguousarray(np.asarray(similarity), dtype=np.float32)

    inv = (10000.0 ** (-np.arange(0, ROT, 2, dtype=np.float64) / ROT))
    ang = np.arange(4096, dtype=np.float64)[None, :] * inv[:, None]
    cos_t = np.cos(ang).astype(np.float32)
    sin_t = np.sin(ang).astype(np.float32)

    cstv = np.zeros((128, 200), dtype=np.float32)
    cstv[:, 0:128] = np.eye(128, dtype=np.float32)
    cstv[:, 128] = np.arange(128, dtype=np.float32)
    cstv[:, 129] = 1.0
    cstv[:, 130:162] = np.arange(32, dtype=np.float32)[None, :]
    cstv[:, 162:166] = np.arange(4, dtype=np.float32)[None, :]
    crowv = np.ones((1, 128), dtype=np.float32)
    eh = np.zeros((32, 4096), dtype=np.float32)
    for e in range(32):
        eh[e, 128 * e:128 * (e + 1)] = 1.0
    csel = np.zeros((32, 256), dtype=np.float32)
    for r in range(16):
        csel[:, 16 * r + r] = 1.0
    return dict(win=win_t, binp=bin_a, wout=wout_t, boutp=bout_a, gw=gw_t,
                gup=gup_t, gdn=gdn_t, simm=sim_a, cost=cos_t, sint=sin_t,
                cst=cstv, crow=crowv, ehot=eh, colsel=csel)


def make_in_maps(inputs):
    common = _host_prep(
        inputs["L0_attn_W_in"], inputs["L0_attn_b_in"], inputs["L0_attn_W_out"],
        inputs["L0_attn_b_out"], inputs["L0_ffn_gate_weight"],
        inputs["L0_ffn_gate_up_proj"], inputs["L0_ffn_down_proj"],
        inputs["L0_ffn_similarity"])
    X = np.asarray(inputs["L0_attn_X"])
    in_maps = []
    for core in range(8):
        b, h = core // 2, core % 2
        xt = np.ascontiguousarray(X[b].T)
        own = xt[:, 2048 * h:2048 * (h + 1)]
        par = xt[:, 2048 * (1 - h):2048 * (2 - h)]
        m = dict(common)
        m["xb"] = np.ascontiguousarray(np.concatenate([own, par], axis=1))
        for key in ("cost", "sint"):
            t = common[key]
            t_own = t[:, 2048 * h:2048 * (h + 1)]
            t_par = t[:, 2048 * (1 - h):2048 * (2 - h)]
            m[key] = np.ascontiguousarray(np.concatenate([t_own, t_par], axis=1))
        in_maps.append(m)
    return in_maps


def kernel(**inputs):
    in_maps = make_in_maps(inputs)
    nc = build_nc()
    res = run_bass_kernel_spmd(nc, in_maps, core_ids=list(range(8)))
    X = np.asarray(inputs["L0_attn_X"])
    out = np.empty_like(X)
    for core in range(8):
        b, h = core // 2, core % 2
        oc = np.asarray(res.results[core]["out"])
        out[b, 2048 * h:2048 * (h + 1), :] = oc.T
    return out
